# version 01
# baseline (speedup 1.0000x reference)
import numpy as np
import ml_dtypes
import concourse.bass as bass
import concourse.bacc as bacc
import concourse.mybir as mybir
from concourse.bass_utils import run_bass_kernel_spmd

F32 = mybir.dt.float32
BF16 = mybir.dt.bfloat16
I16 = mybir.dt.int16
I32 = mybir.dt.int32
AF = mybir.ActivationFunctionType
ALU = mybir.AluOpType
AX = mybir.AxisListType

NCORES = 8
D = 1024
TT = 512
CAP = 1536
NEG = -1.0e30
RSQRT_MAGIC = 0x5F3759DF

ENGINES = ("t", "v", "a", "p", "s")
DMAQ = {"s": "s", "a": "a", "p": "p", "w": "p"}


def _bf(x):
    return np.ascontiguousarray(np.asarray(x, np.float32).astype(ml_dtypes.bfloat16))


def _ap(x):
    if hasattr(x, "tensor") or not type(x).__name__.endswith("TensorHandle"):
        return x
    return x[:, :]


class Prog:
    def __init__(self, nc):
        self.nc = nc
        self.ops = {e: [] for e in ENGINES}
        self.cnt = {e: 0 for e in ENGINES}
        self.dcnt = {q: 0 for q in DMAQ}
        self.hw = {e: {} for e in ENGINES}
        self.state = {}

    def _st(self, key):
        s = self.state.get(key)
        if s is None:
            s = {"w": None, "r": {}}
            self.state[key] = s
        return s

    def _wait(self, eng, src, val, force=False):
        if val <= 0:
            return
        if src == eng and not force:
            return
        hwm = self.hw[eng]
        k = src if src != eng else "_self"
        if hwm.get(k, 0) >= val:
            return
        hwm[k] = val
        self.ops[eng].append(("w", src, val))

    UNORDERED = ("v", "a", "p")

    def _val(self, src, val):
        if src.startswith("d"):
            return self.dcnt[src[1:]]
        return val

    def _dep_read(self, eng, ap, force=False):
        s = self.state.get(ap.tensor.name)
        if s and s["w"] is not None:
            f = force or (s["w"][0] == eng and eng in self.UNORDERED)
            self._wait(eng, s["w"][0], self._val(s["w"][0], s["w"][1]), force=f)

    def _dep_write(self, eng, ap):
        s = self.state.get(ap.tensor.name)
        if not s:
            return
        if s["w"] is not None:
            f = s["w"][0] == eng and eng in self.UNORDERED
            self._wait(eng, s["w"][0], self._val(s["w"][0], s["w"][1]), force=f)
        for src, val in s["r"].items():
            f = src == eng and eng in self.UNORDERED
            self._wait(eng, src, self._val(src, val), force=f)

    def op(self, eng, meth, args, kwargs=None, outs=(), ins=(), sins=(), qdma=False):
        args = tuple(_ap(a) for a in args)
        outs = tuple(_ap(a) for a in outs)
        ins = tuple(_ap(a) for a in ins)
        sins = tuple(_ap(a) for a in sins)
        psin = tuple(a for a in ins
                     if type(a.tensor).__name__.startswith("PSum"))
        ins = tuple(a for a in ins
                    if not type(a.tensor).__name__.startswith("PSum"))
        outs = outs + psin
        for a in ins:
            self._dep_read(eng, a)
        for a in sins:
            self._dep_read(eng, a, force=True)
        for a in outs:
            self._dep_write(eng, a)
        if qdma:
            assert eng == "p"
            self.dcnt["p"] += 16
            val = self.dcnt["p"]
            src = "dp"
            self.ops[eng].append(("g", meth, args, kwargs or {}))
        else:
            self.cnt[eng] += 1
            val = self.cnt[eng]
            src = eng
            self.ops[eng].append(("c", meth, args, kwargs or {}))
        def _key(a):
            return a.tensor.name
        for a in ins:
            st = self._st(_key(a))
            if st["r"].get(src, 0) < val:
                st["r"][src] = val
        for a in sins:
            st = self._st(_key(a))
            if st["r"].get(src, 0) < val:
                st["r"][src] = val
        for a in outs:
            st = self._st(_key(a))
            st["w"] = (src, val)
            st["r"] = {}
        return val

    def dma(self, q, out, in_, **kw):
        out = _ap(out)
        in_ = _ap(in_)
        eng = DMAQ[q]
        self._dep_read(eng, in_)
        self._dep_write(eng, out)
        self.dcnt[q] += 16
        val = self.dcnt[q]
        self.ops[eng].append(("d", q, out, in_, kw))
        src = "d" + q
        st = self._st(in_.tensor.name)
        if st["r"].get(src, 0) < val:
            st["r"][src] = val
        st2 = self._st(out.tensor.name)
        st2["w"] = (src, val)
        st2["r"] = {}
        return val

    def barrier(self, qs=("s", "a", "p", "w")):
        snap = dict(self.cnt)
        dsnap = {q: self.dcnt[q] for q in qs}
        for e in ENGINES:
            for e2 in ENGINES:
                if e2 != e and snap[e2] > 0:
                    self._wait(e, e2, snap[e2])
            for q in qs:
                if dsnap[q] > 0:
                    self._wait(e, "d" + q, dsnap[q])
        self.state = {}

    def replay(self, sems, eo, eng):
        mysem = sems[eng]
        for rec in self.ops[eng]:
            kind = rec[0]
            if kind == "w":
                _, src, val = rec
                eo.wait_ge(sems[src] if src != eng else mysem, val)
            elif kind == "c":
                _, meth, args, kwargs = rec
                getattr(eo, meth)(*args, **kwargs).then_inc(mysem, 1)
            elif kind == "g":
                _, meth, args, kwargs = rec
                getattr(eo, meth)(*args, **kwargs).then_inc(sems["dp"], 16)
            else:
                _, q, out, in_, kw = rec
                eo.dma_start(out=out, in_=in_, **kw).then_inc(sems["d" + q], 16)


class Builder:
    def __init__(self, T=8192, stop_after=None, dumps=()):
        self.T = T
        self.NT = T // TT
        self.NC = T // 128
        self.stop_after = stop_after
        self.dumps = set(dumps)
        self.nc = bacc.Bacc("TRN2", target_bir_lowering=False,
                            detect_race_conditions=False)
        self.P = Prog(self.nc)
        self._guards = []
        self._stage_guards = []
        self.dram = {}
        self.sbuf = {}
        self.uid = 0

    def din(self, name, shape, dtype=F32):
        t = self.nc.dram_tensor(name, list(shape), dtype, kind="ExternalInput")
        self.dram[name] = t
        return t

    def dscr(self, name, shape, dtype=F32):
        kind = "ExternalOutput" if name in self.dumps else "Internal"
        t = self.nc.dram_tensor(name, list(shape), dtype, kind=kind)
        self.dram[name] = t
        return t

    def dout(self, name, shape, dtype=F32):
        t = self.nc.dram_tensor(name, list(shape), dtype, kind="ExternalOutput")
        self.dram[name] = t
        return t

    def sb(self, name, shape, dtype=F32, stage=False):
        self.uid += 1
        name = "%s_u%d" % (name, self.uid)
        g = self.nc.sbuf_tensor(name, list(shape), dtype)
        h = g.__enter__()
        (self._stage_guards if stage else self._guards).append(g)
        self.sbuf[name] = h
        return h

    def ps(self, name, shape, dtype=F32, stage=False):
        self.uid += 1
        name = "%s_u%d" % (name, self.uid)
        g = self.nc.psum_tensor(name, list(shape), dtype)
        h = g.__enter__()
        (self._stage_guards if stage else self._guards).append(g)
        return h

    def end_stage(self):
        self.P.barrier()
        for g in reversed(self._stage_guards):
            g.__exit__(None, None, None)
        self._stage_guards = []

    def mm(self, psum, lhsT, rhs, start, stop):
        self.P.op("t", "matmul", (psum, lhsT, rhs),
                  {"start": start, "stop": stop}, outs=(psum,), ins=(lhsT, rhs))

    def trans(self, psum, in_, ident):
        self.P.op("t", "transpose", (psum, in_, ident), outs=(psum,), ins=(in_, ident))

    def tt(self, out, a, b, alu, eng="v"):
        self.P.op(eng, "tensor_tensor", (out, a, b, alu), outs=(out,), ins=(a, b))

    def ts(self, out, a, s1, alu, eng="v"):
        self.P.op(eng, "tensor_single_scalar", (out, a, float(s1), alu),
                  outs=(out,), ins=(a,))

    def ts2(self, out, a, s1, s2, op0, op1, eng="v"):
        self.P.op(eng, "tensor_scalar", (out, a, float(s1), float(s2), op0, op1),
                  outs=(out,), ins=(a,))

    def stt(self, out, in0, scal, in1, op0, op1, eng="v"):
        self.P.op(eng, "scalar_tensor_tensor", (out, in0, scal, in1, op0, op1),
                  outs=(out,), ins=(in0, in1), sins=(scal,))

    def cp(self, out, in_, eng="v"):
        self.P.op(eng, "tensor_copy", (out, in_), outs=(out,), ins=(in_,))

    def act(self, out, in_, func, bias=0.0, scale=1.0, eng="a"):
        sins = tuple(x for x in (bias, scale) if not isinstance(x, (int, float)))
        self.P.op(eng, "activation", (out, in_, func),
                  {"bias": bias, "scale": scale}, outs=(out,), ins=(in_,), sins=sins)

    def red(self, out, in_, alu, eng="v", negate=False):
        self.P.op(eng, "tensor_reduce", (out, in_, AX.X, alu),
                  {"negate": negate}, outs=(out,), ins=(in_,))

    def memset(self, ap, v, eng="v"):
        self.P.op(eng, "memset", (ap, float(v)), outs=(ap,))

    def recip(self, out, in_):
        self.P.op("v", "reciprocal", (out, in_), outs=(out,), ins=(in_,))

    def scan(self, out, d0, d1, init, op0, op1):
        sins = () if isinstance(init, (int, float)) else (init,)
        self.P.op("v", "tensor_tensor_scan", (out, d0, d1, init, op0, op1),
                  outs=(out,), ins=(d0, d1), sins=sins)

    def iota(self, out, pattern, base=0, chmul=0):
        self.P.op("p", "iota", (out,),
                  {"pattern": pattern, "base": base, "channel_multiplier": chmul,
                   "allow_small_or_imprecise_dtypes": True}, outs=(out,))

    def dma(self, q, out, in_, **kw):
        return self.P.dma(q, out, in_, **kw)

    def rsqrt_col(self, out, v, tmp, tmp2, eps=0.0, iters=3):
        self.ts(out, v, eps, ALU.add)
        self.recip(out, out)
        self.act(out, out, AF.Sqrt)

    def load_w_bf(self, name, src, K, M, q="w", rowscale=None):
        nk = K // 128
        wsb = self.sb(name, [128, nk * M], BF16, stage=True)
        for k in range(nk):
            self.dma("w", wsb[:, k * M:(k + 1) * M], src[k * 128:(k + 1) * 128, :])
        if rowscale is not None:
            for k in range(nk):
                self.tt(wsb[:, k * M:(k + 1) * M], wsb[:, k * M:(k + 1) * M],
                        rowscale(k).broadcast_to([128, M]), ALU.mult)
        return wsb

    def wch(self, wsb, M, k, m0, m1):
        return wsb[:, k * M + m0: k * M + m1]


class Kern(Builder):
    def declare(self):
        T = self.T
        self.din("x0", [T, D], F32)
        names = [
            ("L0_attn_Wq", [D, D]), ("L0_attn_Wk", [D, D]), ("L0_attn_Wv", [D, D]),
            ("L0_attn_q_conv_w", [D, 4]), ("L0_attn_k_conv_w", [D, 4]),
            ("L0_attn_v_conv_w", [D, 4]), ("L0_attn_Wb", [D, 4]),
            ("L0_attn_fir_w3", [4, 256, 3]), ("L0_attn_fir_w7", [4, 256, 7]),
            ("L0_attn_fir_w15", [4, 256, 15]), ("L0_attn_fir_w31", [4, 256, 31]),
            ("L0_attn_router_W1", [6 * D, 2 * D]), ("L0_attn_router_b1", [2 * D]),
            ("L0_attn_router_W2", [2 * D, 20]), ("L0_attn_router_b2", [20]),
            ("L0_attn_id_gate_W", [D, 4]), ("L0_attn_id_gate_b", [4]),
            ("L0_attn_log_tau", [4]), ("L0_attn_o_norm_w", [256]),
            ("L0_attn_Wo", [D, D]),
            ("L0_ffn_Wg", [D, 32]),
            ("L0_ffn_W1", [32, D, 512]), ("L0_ffn_W2", [32, 512, D]),
            ("L0_ffn_W3", [32, D, 512]),
            ("L0_ffn_Ws1", [D, D]), ("L0_ffn_Ws2", [D, D]), ("L0_ffn_Ws3", [D, D]),
            ("L1_attn_norm_w", [D]),
            ("L1_attn_WqT", [D, D]), ("L1_attn_WkT", [D, D]), ("L1_attn_WvT", [D, D]),
            ("L1_attn_WgT", [D, 24]),
            ("L1_attn_gb", [24]),
            ("L1_attn_WoutT", [D, D]), ("L1_attn_bout", [D]),
            ("L1_ffn_Wqkv", [6, D, 3 * D]),
            ("L1_ffn_q_scale", [6, 64]), ("L1_ffn_k_scale", [6, 64]),
            ("L1_ffn_Wo", [6, D, D]),
            ("L1_ffn_ln_g", [6, D]), ("L1_ffn_ln_b", [6, D]),
            ("L1_ffn_W1", [6, D, 5632]), ("L1_ffn_b1", [6, 5632]),
            ("L1_ffn_W2", [6, 2816, D]), ("L1_ffn_b2", [6, D]),
        ]
        for n, s in names:
            self.din(n, s, F32)
        consts = [
            ("c_ident_bf", [128, 128], BF16), ("c_ident_f32", [128, 128], F32),
            ("c_ones_lhs_bf", [1, 128], BF16), ("c_ones_col_bf", [128, 1], BF16),
            ("c_cum_bf", [128, 128], BF16),
            ("c_mask_sl_neg", [128, 128], F32), ("c_mask_su_neg", [128, 128], F32),
            ("c_mask_ui", [128, 128], F32),
            ("c_gla_m", [128, 128], F32), ("c_gla_b", [128, 128], F32),
            ("c_lam_gen", [128, 384], F32),
            ("c_ind2_bf", [128, 2], BF16), ("c_ind2T_bf", [2, 128], BF16),
            ("c_swap64_bf", [128, 128], BF16), ("c_swap_h_bf", [128, 128], BF16),
            ("c_ones2_bf", [2, 128], BF16), ("c_ones64_bf", [1, 64], BF16),
            ("c_sel16", [16, 8 * 128], BF16),
            ("c_lam_gen_bf", [128, 384], BF16), ("c_lam_f0_bf", [128, 384], BF16),
            ("c_lam_f1_bf", [128, 384], BF16),
            ("c_rope1_cs2", [128, self.T], F32), ("c_rope1_ss2", [128, self.T], F32),
            ("c_xp_cs", [64, self.T], F32), ("c_xp_ssq", [64, self.T], F32),
            ("c_xp_ssk", [64, self.T], F32),
            ("c_xp_ci", [64, self.T], F32),
        ]
        for n, s, dt in consts:
            self.din(n, s, dt)
        T1 = T + 1
        self.dscr("XN0F", [D, T], BF16)
        self.dscr("XNAUG", [T1, 1024], BF16)
        self.dscr("WSEL", [T1, 64], F32)
        self.dscr("BETA", [T, 8], F32)
        self.dscr("CM", [32, CAP], I16)
        self.dscr("QF", [D, T], BF16)
        self.dscr("KF", [D, T], BF16)
        self.dscr("VF", [D, T + 32], BF16)
        self.dscr("P3", [D, T], BF16)
        self.dscr("P7", [D, T], BF16)
        self.dscr("P15", [D, T], BF16)
        self.dscr("P31", [D, T], BF16)
        self.dscr("DOF", [D, T], BF16)
        self.dscr("PRF", [32, T], F32)
        self.dscr("HMIDF", [2 * D, T], BF16)
        self.dscr("S0", [T, D], BF16)
        for i in range(4):
            self.dscr("Y%d" % i, [T1, D], F32)
        self.dscr("X1", [T, D], F32)
        self.dscr("XN1F", [D, T], BF16)
        self.dscr("HN1F", [D, T], BF16)
        self.dscr("XFA", [T + 8, D], F32)
        self.dscr("XFB", [T + 8, D], F32)
        self.dscr("QF1", [D, T], BF16)
        self.dscr("KF1", [D, T], BF16)
        self.dscr("VF1", [D, T], BF16)
        self.dscr("GL1", [T, 24], F32)
        self.dscr("Y1F", [D, T], BF16)
        self.dscr("S1", [T, D], BF16)
        self.dout("out", [T, D], F32)

        self.ident_bf = self.sb("s_ident_bf", [128, 128], BF16)
        self.ident_f32 = self.sb("s_ident_f32", [128, 128], F32)
        self.ones_lhs = self.sb("s_ones_lhs", [1, 128], BF16)
        self.ones_col = self.sb("s_ones_col", [128, 1], BF16)
        self.dma("s", self.ident_bf[:, :], self.dram["c_ident_bf"].ap())
        self.dma("s", self.ident_f32[:, :], self.dram["c_ident_f32"].ap())
        self.dma("s", self.ones_lhs[:, :], self.dram["c_ones_lhs_bf"].ap())
        self.dma("s", self.ones_col[:, :], self.dram["c_ones_col_bf"].ap())
        self.zrow = self.sb("s_zrow", [128, 1152], BF16)
        self.memset(self.zrow[:, :], 0.0)
        self.dma("s", self.dram["XNAUG"].ap()[self.T:self.T + 1, :],
                 self.zrow[0:1, 0:1024])
        zf64 = self.sb("s_zf64", [1, 64], F32)
        self.memset(zf64[:, :], 0.0)
        self.dma("s", self.dram["WSEL"].ap()[self.T:self.T + 1, :], zf64[:, :])
        zf = self.sb("s_zf32", [128, 64], F32)
        self.memset(zf[:, :], 0.0)
        for c in range(8):
            self.dma("s", self.dram["VF"].ap()[128 * c:128 * (c + 1), 0:32],
                     self.zrow[0:128, 0:32])
        self.dma("s", self.dram["XFA"].ap()[0:8, :].rearrange("a b -> (a b)").rearrange("(a b) -> a b", a=128),
                 zf[:, :])
        self.dma("s", self.dram["XFB"].ap()[0:8, :].rearrange("a b -> (a b)").rearrange("(a b) -> a b", a=128),
                 zf[:, :])

    def s_ln0(self):
        x0 = self.dram["x0"].ap()
        xnaug = self.dram["XNAUG"].ap()
        xn0f = self.dram["XN0F"].ap()
        xt = [self.sb("ln_x%d" % i, [128, D], F32, stage=True) for i in range(4)]
        xnb = [self.sb("ln_xn%d" % i, [128, D], BF16, stage=True) for i in range(4)]
        cols = self.sb("ln_cols", [128, 8], F32, stage=True)
        pbf = [self.ps("ln_ps%d" % i, [128, 512], BF16, stage=True) for i in range(2)]
        ev = [self.sb("ln_ev%d" % i, [128, 512], BF16, stage=True) for i in range(2)]
        for it in range(self.NT):
            t0 = it * TT
            for j in range(4):
                r0 = t0 + 128 * j
                self.dma("s", xt[j][:, :], x0[r0:r0 + 128, :])
                nm = cols[:, 0:1]
                v = cols[:, 1:2]
                rs = cols[:, 2:3]
                tp1 = cols[:, 3:4]
                tp2 = cols[:, 4:5]
                self.red(nm, xt[j][:, :], ALU.add, negate=True)
                self.ts(nm, nm, 1.0 / D, ALU.mult)
                self.act(xt[j][:, :], xt[j][:, :], AF.Identity, bias=nm)
                self.act(xnb[j][:, :], xt[j][:, :], AF.Square)
                self.red(v, xnb[j][:, :], ALU.add)
                self.ts(v, v, 1.0 / D, ALU.mult)
                self.rsqrt_col(rs, v, tp1, tp2, eps=1e-5)
                self.tt(xnb[j][:, :], xt[j][:, :], rs.broadcast_to([128, D]), ALU.mult)
                self.dma("s", xnaug[r0:r0 + 128, 0:D], xnb[j][:, :])
            for c in range(8):
                pb = pbf[c % 2]
                e = ev[c % 2]
                for j in range(4):
                    self.trans(pb[:, 128 * j:128 * (j + 1)],
                               xnb[j][:, 128 * c:128 * (c + 1)], self.ident_bf[:, :])
                self.cp(e[:, :], pb[:, :])
                self.dma("s", xn0f[128 * c:128 * (c + 1), t0:t0 + TT], e[:, :])
        self.end_stage()

    def s_scores(self):
        T = self.T
        xn0f = self.dram["XN0F"].ap()
        xnaug = self.dram["XNAUG"].ap()
        beta_d = self.dram["BETA"].ap()
        wgbp = self.sb("sc_w", [128, 8 * 40], BF16, stage=True)
        for k in range(8):
            r = slice(128 * k, 128 * (k + 1))
            self.dma("w", wgbp[:, k * 40:k * 40 + 32], self.dram["L0_ffn_Wg"].ap()[r, :])
            self.dma("w", wgbp[:, k * 40 + 32:k * 40 + 36], self.dram["L0_attn_Wb"].ap()[r, :])
            self.dma("w", wgbp[:, k * 40 + 36:k * 40 + 40], self.dram["L0_attn_id_gate_W"].ap()[r, :])
        brow = self.sb("sc_b", [1, 8], BF16, stage=True)
        self.memset(brow[:, :], 0.0)
        self.dma("w", brow[0:1, 4:8], self.dram["L0_attn_id_gate_b"].ap().rearrange("(o a) -> o a", o=1))
        slabs = [self.sb("sc_s%d" % k, [128, TT], BF16, stage=True) for k in range(8)]
        ps40 = [self.ps("sc_p%d" % i, [128, 512], F32, stage=True) for i in range(2)]
        pmi = self.ps("sc_pmi", [128, 512], F32, stage=True)
        mif = self.sb("sc_mif", [32, T], F32, stage=True)
        sc = self.sb("sc_sc", [128, 40], F32, stage=True)
        sc2 = self.sb("sc_sc2", [128, 32], F32, stage=True)
        wtf = self.sb("sc_wtf", [128, 32], F32, stage=True)
        wst = self.sb("sc_wst", [128, 64], F32, stage=True)
        self.memset(wst[:, 32:64], 0.0)
        cols = self.sb("sc_cols", [128, 8], F32, stage=True)
        for it in range(self.NT):
            t0 = it * TT
            for k in range(8):
                self.dma("s", slabs[k][:, :], xn0f[128 * k:128 * (k + 1), t0:t0 + TT])
            for j in range(4):
                p = ps40[j % 2][:, 0:40]
                for k in range(8):
                    self.mm(p, slabs[k][:, 128 * j:128 * (j + 1)],
                            wgbp[:, k * 40:(k + 1) * 40], k == 0, False)
                self.mm(ps40[j % 2][:, 32:40], self.ones_lhs[:, :], brow[:, :], False, True)
                nmx = cols[:, 0:1]
                ssum = cols[:, 1:2]
                rcp = cols[:, 2:3]
                thr = cols[:, 3:4]
                tcol = cols[:, 4:5]
                self.red(nmx, p[:, 0:32], ALU.max, negate=True)
                self.act(sc[:, 0:32], p[:, 0:32], AF.Exp, bias=nmx)
                self.red(ssum, sc[:, 0:32], ALU.add)
                self.recip(rcp, ssum)
                self.tt(sc[:, 0:32], sc[:, 0:32], rcp.broadcast_to([128, 32]), ALU.mult)
                self.act(sc[:, 32:40], p[:, 32:40], AF.Exp, scale=-1.0)
                self.ts(sc[:, 32:40], sc[:, 32:40], 1.0, ALU.add)
                self.recip(sc[:, 32:40], sc[:, 32:40])
                self.dma("s", beta_d[t0 + 128 * j:t0 + 128 * (j + 1), :], sc[:, 32:40])
                self.cp(sc2[:, :], sc[:, 0:32])
                for i in range(3):
                    mx = cols[:, 5:6]
                    self.red(mx, sc2[:, :], ALU.max)
                    eq = cols[:, 6:7]
                    self.tt(wtf[:, :], sc2[:, :], mx.broadcast_to([128, 32]), ALU.is_ge)
                    self.ts(wtf[:, :], wtf[:, :], NEG, ALU.mult)
                    self.tt(sc2[:, :], sc2[:, :], wtf[:, :], ALU.add)
                self.red(thr, sc2[:, :], ALU.max)
                ge = sc2
                self.tt(ge[:, :], sc[:, 0:32], thr.broadcast_to([128, 32]), ALU.is_ge)
                self.tt(wst[:, 0:32], sc[:, 0:32], ge[:, :], ALU.mult)
                r0 = t0 + 128 * j
                self.dma("s", self.dram["WSEL"].ap()[r0:r0 + 128, :], wst[:, :])
                self.iota(tcol, [[0, 1]], base=r0, chmul=1)
                self.tt(wtf[:, :], ge[:, :], tcol.broadcast_to([128, 32]), ALU.mult)
                self.ts(ge[:, :], ge[:, :], 1.0, ALU.subtract)
                self.tt(wtf[:, :], wtf[:, :], ge[:, :], ALU.add)
                self.trans(pmi[0:32, 128 * j:128 * (j + 1)], wtf[:, :], self.ident_f32[:, :])
            self.cp(mif[:, t0:t0 + TT], pmi[0:32, :])
        msk = self.sb("sc_msk", [32, T], F32, stage=True)
        rnk = self.sb("sc_rnk", [32, T], F32, stage=True)
        s16 = self.sb("sc_s16", [32, T], I16, stage=True)
        d16 = self.sb("sc_d16", [32, T], I16, stage=True)
        cm16 = self.sb("sc_cm", [32, CAP], I16, stage=True)
        cmf = self.sb("sc_cmf", [32, CAP], F32, stage=True)
        self.ts(msk[:, :], mif[:, :], 0.0, ALU.is_ge)
        self.scan(rnk[:, :], msk[:, :], msk[:, :], 0.0, ALU.add, ALU.bypass)
        self.tt(rnk[:, :], rnk[:, :], msk[:, :], ALU.subtract)
        self.ts(mif[:, :], rnk[:, :], float(CAP), ALU.is_lt)
        self.tt(msk[:, :], msk[:, :], mif[:, :], ALU.mult)
        self.tt(rnk[:, :], rnk[:, :], msk[:, :], ALU.mult)
        self.ts(msk[:, :], msk[:, :], 1.0, ALU.subtract)
        self.tt(rnk[:, :], rnk[:, :], msk[:, :], ALU.add)
        self.cp(s16[:, :], rnk[:, :], eng="p")
        self.iota(d16[:, :], [[1, T]], base=-T, chmul=0)
        self.P.op("p", "local_scatter", (cm16[:, :], d16[:, :], s16[:, :], 32, CAP, T),
                  outs=(cm16[:, :],), ins=(d16[:, :], s16[:, :]))
        self.cp(cmf[:, :], cm16[:, :], eng="p")
        self.ts(cmf[:, :], cmf[:, :], float(T), ALU.add)
        self.cp(cm16[:, :], cmf[:, :], eng="p")
        self.dma("s", self.dram["CM"].ap()[:, :], cm16[:, :])
        self.end_stage()


    def s_proj0(self):
        T = self.T
        xn0f = self.dram["XN0F"].ap()
        wq = self.load_w_bf("pw_q", self.dram["L0_attn_Wq"].ap(), D, D)
        wk = self.load_w_bf("pw_k", self.dram["L0_attn_Wk"].ap(), D, D)
        wv = self.load_w_bf("pw_v", self.dram["L0_attn_Wv"].ap(), D, D)
        wmap = {"q": wq, "k": wk, "v": wv}
        dst = {"q": self.dram["QF"].ap(), "k": self.dram["KF"].ap(),
               "v": self.dram["VF"].ap()}
        voff = {"q": 0, "k": 0, "v": 32}
        cw = self.sb("p0_cw", [128, 32], F32, stage=True)
        for k in range(8):
            for wi, nm in enumerate(("q", "k", "v")):
                pass
        srcs = {"q": self.dram["L0_attn_q_conv_w"].ap(),
                "k": self.dram["L0_attn_k_conv_w"].ap(),
                "v": self.dram["L0_attn_v_conv_w"].ap()}
        cwb = {}
        for nm in ("q", "k", "v"):
            cwb[nm] = self.sb("p0_cw_" + nm, [128, 32], F32, stage=True)
            for k in range(8):
                self.dma("w", cwb[nm][:, 4 * k:4 * (k + 1)],
                         srcs[nm][128 * k:128 * (k + 1), :])
        diag = self.sb("p0_diag", [128, 96 * 128], BF16, stage=True)
        di = {}
        n = 0
        for nm in ("q", "k", "v"):
            for m in range(8):
                for s in range(4):
                    d = diag[:, n * 128:(n + 1) * 128]
                    di[(nm, m, s)] = d
                    col = cwb[nm][:, 4 * m + (3 - s):4 * m + (4 - s)]
                    self.tt(d, self.ident_bf[:, :], col.broadcast_to([128, 128]), ALU.mult)
                    n += 1
        halo = self.sb("p0_halo", [128, 24 * 3], BF16, stage=True)
        self.memset(halo[:, :], 0.0)
        slabs = [self.sb("p0_s%d" % k, [128, TT], BF16, stage=True) for k in range(8)]
        conv = [self.sb("p0_c%d" % i, [128, 515], BF16, stage=True) for i in range(3)]
        outb = [self.sb("p0_o%d" % i, [128, TT], BF16, stage=True) for i in range(3)]
        pp = [self.ps("p0_p%d" % i, [128, 512], F32, stage=True) for i in range(4)]
        for it in range(self.NT):
            t0 = it * TT
            for k in range(8):
                self.dma("s", slabs[k][:, :], xn0f[128 * k:128 * (k + 1), t0:t0 + TT])
            for wi, nm in enumerate(("q", "k", "v")):
                w = wmap[nm]
                for m in range(8):
                    p = pp[(wi * 8 + m) % 2]
                    for k in range(8):
                        self.mm(p, self.wch(w, D, k, m * 128, (m + 1) * 128),
                                slabs[k][:, :], k == 0, k == 7)
                    cb = conv[(wi * 8 + m) % 3]
                    hslice = halo[:, (wi * 8 + m) * 3:(wi * 8 + m) * 3 + 3]
                    self.cp(cb[:, 0:3], hslice)
                    self.cp(cb[:, 3:515], p[:, :])
                    self.cp(hslice, cb[:, 512:515])
                    p2 = pp[2 + (wi * 8 + m) % 2]
                    for s in range(4):
                        self.mm(p2, di[(nm, m, s)], cb[:, 3 - s:515 - s], s == 0, s == 3)
                    ob = outb[(wi * 8 + m) % 3]
                    self.act(ob[:, :], p2[:, :], AF.Silu)
                    self.dma("s", dst[nm][128 * m:128 * (m + 1),
                                          voff[nm] + t0:voff[nm] + t0 + TT], ob[:, :])
        self.end_stage()

    def s_fir(self):
        T = self.T
        vf = self.dram["VF"].ap()
        taps = [(3, "L0_attn_fir_w3", "P3"), (7, "L0_attn_fir_w7", "P7"),
                (15, "L0_attn_fir_w15", "P15"), (31, "L0_attn_fir_w31", "P31")]
        fw = self.sb("f_w", [128, 56], F32, stage=True)
        diag = self.sb("f_diag", [128, 56 * 128], BF16, stage=True)
        vs = [self.sb("f_v%d" % i, [128, 543], BF16, stage=True) for i in range(2)]
        ob = [self.sb("f_o%d" % i, [128, TT], BF16, stage=True) for i in range(4)]
        pp = [self.ps("f_p%d" % i, [128, 512], F32, stage=True) for i in range(4)]
        for m in range(8):
            h, dv0 = m // 2, (m % 2) * 128
            off = 0
            dmap = {}
            for K, wname, _ in taps:
                self.dma("w", fw[:, off:off + K],
                         self.dram[wname].ap()[h, dv0:dv0 + 128, :])
                for s in range(K):
                    d = diag[:, (off + s) * 128:(off + s + 1) * 128]
                    dmap[(K, s)] = d
                    col = fw[:, off + (K - 1 - s):off + (K - s)]
                    self.tt(d, self.ident_bf[:, :], col.broadcast_to([128, 128]), ALU.mult)
                off += K
            for it in range(self.NT):
                t0 = it * TT
                v = vs[it % 2]
                self.dma("s", v[:, :], vf[128 * m:128 * (m + 1), t0 + 1:t0 + 544])
                for pi, (K, _, oname) in enumerate(taps):
                    p = pp[pi]
                    for s in range(K):
                        self.mm(p, dmap[(K, s)], v[:, 31 - s:543 - s], s == 0, s == K - 1)
                    self.cp(ob[pi][:, :], p[:, :])
                    self.dma("s", self.dram[oname].ap()[128 * m:128 * (m + 1), t0:t0 + TT],
                             ob[pi][:, :])
        self.end_stage()


    def s_delta(self):
        T = self.T
        QF, KF, VF = (self.dram[n].ap() for n in ("QF", "KF", "VF"))
        DOF = self.dram["DOF"].ap()
        beta_d = self.dram["BETA"].ap()
        S_sb = [self.sb("d_S%d" % h, [128, 512], F32, stage=True) for h in range(4)]
        S_bf = [self.sb("d_Sb%d" % h, [128, 512], BF16, stage=True) for h in range(4)]
        for h in range(4):
            self.memset(S_sb[h][:, :], 0.0)
            self.memset(S_bf[h][:, :], 0.0)
        qs = [self.sb("d_q%d" % k, [128, 128], BF16, stage=True) for k in range(8)]
        ks = [self.sb("d_k%d" % k, [128, 128], BF16, stage=True) for k in range(8)]
        vs = [self.sb("d_v%d" % k, [128, 128], BF16, stage=True) for k in range(8)]
        bt = self.sb("d_bt", [128, 4], F32, stage=True)
        row4 = self.sb("d_row4", [1, 512], BF16, stage=True)
        rowf = self.sb("d_rowf", [1, 128], F32, stage=True)
        rowb = self.sb("d_rowb", [1, 128], BF16, stage=True)
        sqb = [self.sb("d_sq%d" % j, [128, 128], BF16, stage=True) for j in range(2)]
        qn = [self.sb("d_qn%d" % j, [128, 128], BF16, stage=True) for j in range(2)]
        kn = [self.sb("d_kn%d" % j, [128, 128], BF16, stage=True) for j in range(2)]
        kb = [self.sb("d_kb%d" % j, [128, 128], BF16, stage=True) for j in range(2)]
        vb = [self.sb("d_vb%d" % j, [128, 128], BF16, stage=True) for j in range(2)]
        npw = [self.sb("d_np%d" % i, [128, 128], BF16, stage=True) for i in range(7)]
        npwT = [self.sb("d_npT%d" % i, [128, 128], BF16, stage=True) for i in range(7)]
        cur = [self.sb("d_cur%d" % i, [128, 512], BF16, stage=True) for i in range(2)]
        uadj = self.sb("d_uadj", [128, 256], BF16, stage=True)
        attT = self.sb("d_attT", [128, 128], BF16, stage=True)
        wfm = [self.sb("d_wfm%d" % r, [128, 128], BF16, stage=True) for r in range(2)]
        kntm = [self.sb("d_kntm%d" % r, [128, 128], BF16, stage=True) for r in range(2)]
        obf = [self.sb("d_o%d" % r, [128, 128], BF16, stage=True) for r in range(2)]
        pA = self.ps("d_pA", [128, 512], F32, stage=True)
        pB = self.ps("d_pB", [128, 512], F32, stage=True)
        pApp = self.ps("d_pApp", [128, 512], F32, stage=True)
        pU = self.ps("d_pU", [128, 512], F32, stage=True)
        pO = self.ps("d_pO", [128, 512], F32, stage=True)
        pR = self.ps("d_pR", [128, 512], F32, stage=True)
        pT1 = self.ps("d_pT1", [128, 1024], BF16, stage=True)
        pT2 = self.ps("d_pT2", [128, 1024], BF16, stage=True)
        msl = self.sb("d_msl", [128, 128], F32, stage=True)
        msu = self.sb("d_msu", [128, 128], F32, stage=True)
        mui = self.sb("d_mui", [128, 128], F32, stage=True)
        self.dma("s", msl, self.dram["c_mask_sl_neg"].ap())
        self.dma("s", msu, self.dram["c_mask_su_neg"].ap())
        self.dma("s", mui, self.dram["c_mask_ui"].ap())
        for c in range(self.NC):
            tc = c * 128
            for k in range(8):
                self.dma("s", qs[k], QF[128 * k:128 * (k + 1), tc:tc + 128])
                self.dma("s", ks[k], KF[128 * k:128 * (k + 1), tc:tc + 128])
                self.dma("s", vs[k], VF[128 * k:128 * (k + 1), 32 + tc:32 + tc + 128])
            self.dma("s", bt, beta_d[tc:tc + 128, 0:4])
            for h in range(4):
                self.trans(pR[0:1, 128 * h:128 * (h + 1)], bt[:, h:h + 1],
                           self.ident_f32[:, :])
            self.cp(row4[:, :], pR[0:1, 0:512])
            for h in range(4):
                q2 = qs[2 * h:2 * h + 2]
                k2 = ks[2 * h:2 * h + 2]
                v2 = vs[2 * h:2 * h + 2]
                for (src2, dst2) in ((q2, qn), (k2, kn)):
                    for j in range(2):
                        self.act(sqb[j], src2[j], AF.Square)
                    for j in range(2):
                        self.mm(pR[0:1, 0:128], self.ones_col, sqb[j], j == 0, j == 1)
                    self.ts(rowf, pR[0:1, 0:128], 1e-6, ALU.add)
                    self.recip(rowf, rowf)
                    self.act(rowf, rowf, AF.Sqrt)
                    self.cp(rowb, rowf)
                    self.mm(pB[:, 0:128], self.ones_lhs, rowb, True, True)
                    for j in range(2):
                        self.tt(dst2[j], src2[j], pB[:, 0:128], ALU.mult)
                self.mm(pB[:, 128:256], self.ones_lhs, row4[:, 128 * h:128 * (h + 1)], True, True)
                for j in range(2):
                    self.tt(kb[j], kn[j], pB[:, 128:256], ALU.mult)
                    self.tt(vb[j], v2[j], pB[:, 128:256], ALU.mult)
                for j in range(2):
                    self.mm(pA[:, 0:128], kb[j], kn[j], j == 0, j == 1)
                for j in range(2):
                    self.mm(pA[:, 128:256], kn[j], kb[j], j == 0, j == 1)
                self.tt(npw[0], pA[:, 0:128], msl, ALU.mult)
                self.tt(npwT[0], pA[:, 128:256], msu, ALU.mult)
                for i in range(5):
                    self.mm(pB[:, 256:384], npwT[i], npw[i], True, True)
                    self.mm(pB[:, 384:512], npw[i], npwT[i], True, True)
                    self.cp(npw[i + 1], pB[:, 256:384])
                    self.cp(npwT[i + 1], pB[:, 384:512])
                self.mm(pB[:, 256:384], npw[5], npwT[5], True, True)
                self.cp(npwT[6], pB[:, 256:384])
                for j in range(2):
                    self.trans(pT1[:, 128 * j:128 * (j + 1)], vb[j], self.ident_bf[:, :])
                    self.trans(pT1[:, 256 + 128 * j:256 + 128 * (j + 1)], kb[j],
                               self.ident_bf[:, :])
                self.cp(cur[0][:, :], pT1[:, 0:512])
                cparity = 0
                for fi in (6, 5, 4, 3, 2, 1, 0):
                    self.mm(pApp, self.ident_bf[:, :], cur[cparity], True, False)
                    self.mm(pApp, npwT[fi], cur[cparity], False, True)
                    self.cp(cur[1 - cparity][:, :], pApp)
                    cparity = 1 - cparity
                u_tm = cur[cparity][:, 0:256]
                w_tm = cur[cparity][:, 256:512]
                for r in range(2):
                    self.trans(pT2[:, 128 * r:128 * (r + 1)],
                               w_tm[:, 128 * r:128 * (r + 1)], self.ident_bf[:, :])
                    self.cp(wfm[r], pT2[:, 128 * r:128 * (r + 1)])
                for r in range(2):
                    self.mm(pU[:, 0:256], wfm[r], S_bf[h][:, 256 * r:256 * (r + 1)],
                            r == 0, r == 1)
                self.tt(uadj, u_tm, pU[:, 0:256], ALU.subtract)
                for j in range(2):
                    self.mm(pA[:, 256:384], kn[j], qn[j], j == 0, j == 1)
                self.tt(attT, pA[:, 256:384], mui, ALU.mult)
                for mr in range(2):
                    po = pO[:, 128 * mr:128 * (mr + 1)]
                    for r in range(2):
                        self.mm(po, S_bf[h][:, 256 * r + 128 * mr:256 * r + 128 * mr + 128],
                                qn[r], r == 0, False)
                    self.mm(po, uadj[:, 128 * mr:128 * mr + 128], attT, False, True)
                for mr in range(2):
                    self.cp(obf[mr], pO[:, 128 * mr:128 * (mr + 1)])
                    self.dma("s", DOF[256 * h + 128 * mr:256 * h + 128 * (mr + 1),
                                     tc:tc + 128], obf[mr])
                for r in range(2):
                    self.trans(pT2[:, 256 + 128 * r:256 + 128 * (r + 1)], kn[r],
                               self.ident_bf[:, :])
                    self.cp(kntm[r], pT2[:, 256 + 128 * r:256 + 128 * (r + 1)])
                for r in range(2):
                    self.mm(pU[:, 256:512], kntm[r], uadj, True, True)
                    self.tt(S_sb[h][:, 256 * r:256 * (r + 1)],
                            S_sb[h][:, 256 * r:256 * (r + 1)], pU[:, 256:512], ALU.add)
                    self.cp(S_bf[h][:, 256 * r:256 * (r + 1)],
                            S_sb[h][:, 256 * r:256 * (r + 1)])
        self.end_stage()

    def s_router(self):
        T = self.T
        srcs = [self.dram["XN0F"].ap(), self.dram["P3"].ap(), self.dram["P7"].ap(),
                self.dram["P15"].ap(), self.dram["P31"].ap(), self.dram["DOF"].ap()]
        HM = self.dram["HMIDF"].ap()
        W1 = self.dram["L0_attn_router_W1"].ap()
        b1 = self.sb("r_b1", [128, 16], F32, stage=True)
        for m in range(16):
            self.dma("w", b1[:, m:m + 1],
                     self.dram["L0_attn_router_b1"].ap()[128 * m:128 * (m + 1)]
                     .rearrange("(a b) -> a b", b=1))
        w1 = self.sb("r_w1", [128, 48 * 1024], BF16, stage=True)
        slabs = [self.sb("r_s%d" % k, [128, TT], BF16, stage=True) for k in range(48)]
        ev = [self.sb("r_e%d" % i, [128, TT], BF16, stage=True) for i in range(3)]
        pp = [self.ps("r_p%d" % i, [128, 512], F32, stage=True) for i in range(3)]
        for mh in range(2):
            for k in range(48):
                self.dma("w", w1[:, k * 1024:(k + 1) * 1024],
                         W1[128 * k:128 * (k + 1), 1024 * mh:1024 * (mh + 1)])
            for it in range(self.NT):
                t0 = it * TT
                for g in range(6):
                    for k in range(8):
                        self.dma("s", slabs[8 * g + k],
                                 srcs[g][128 * k:128 * (k + 1), t0:t0 + TT])
                for m8 in range(8):
                    m = 8 * mh + m8
                    p = pp[m8 % 3]
                    for k in range(48):
                        self.mm(p, w1[:, k * 1024 + m8 * 128:k * 1024 + (m8 + 1) * 128],
                                slabs[k], k == 0, k == 47)
                    e = ev[m8 % 3]
                    self.act(e, p, AF.Gelu, bias=b1[:, m:m + 1])
                    self.dma("s", HM[128 * m:128 * (m + 1), t0:t0 + TT], e)
        self.end_stage()

    def s_probs(self):
        T = self.T
        HM = self.dram["HMIDF"].ap()
        PRF = self.dram["PRF"].ap()
        beta_d = self.dram["BETA"].ap()
        w2 = self.load_w_bf("pb_w2", self.dram["L0_attn_router_W2"].ap(), 2 * D, 20)
        brow = self.sb("pb_b", [1, 20], BF16, stage=True)
        self.dma("w", brow, self.dram["L0_attn_router_b2"].ap()
                 .rearrange("(o a) -> o a", o=1))
        trow = self.sb("pb_t", [1, 4], F32, stage=True)
        trowb = self.sb("pb_tb", [1, 4], BF16, stage=True)
        rtau = self.sb("pb_rt", [128, 4], F32, stage=True)
        self.dma("w", trow, self.dram["L0_attn_log_tau"].ap()
                 .rearrange("(o a) -> o a", o=1))
        self.act(trow, trow, AF.Exp)
        self.ts(trow, trow, 0.5, ALU.max)
        self.recip(trow, trow)
        self.cp(trowb, trow)
        slabs = [self.sb("pb_s%d" % k, [128, TT], BF16, stage=True) for k in range(16)]
        pv = self.sb("pb_pv", [128, 4], F32, stage=True)
        sq = self.sb("pb_sq", [128, 4], F32, stage=True)
        pm = self.sb("pb_pm", [128, 28], F32, stage=True)
        sc5 = self.sb("pb_sc5", [128, 5], F32, stage=True)
        cols = self.sb("pb_c", [128, 4], F32, stage=True)
        pp = self.ps("pb_p", [128, 512], F32, stage=True)
        pt = self.ps("pb_pt", [128, 512], F32, stage=True)
        evb = self.sb("pb_ev", [28, 512], F32, stage=True)
        self.mm(pt[:, 0:4], self.ones_lhs, trowb, True, True)
        self.cp(rtau, pt[:, 0:4])
        for it in range(self.NT):
            t0 = it * TT
            for k in range(16):
                self.dma("s", slabs[k], HM[128 * k:128 * (k + 1), t0:t0 + TT])
            for j in range(4):
                p20 = pp[:, 0:20]
                for k in range(16):
                    self.mm(p20, slabs[k][:, 128 * j:128 * (j + 1)], self.wch(w2, 20, k, 0, 20),
                            k == 0, False)
                self.mm(p20, self.ones_lhs, brow, False, True)
                for h in range(4):
                    self.tt(sc5, p20[:, 5 * h:5 * h + 5],
                            rtau[:, h:h + 1].broadcast_to([128, 5]), ALU.mult)
                    nmx = cols[:, 0:1]
                    self.red(nmx, sc5, ALU.max, negate=True)
                    self.act(sc5, sc5, AF.Exp, bias=nmx)
                    ssum = cols[:, 1:2]
                    self.red(ssum, sc5, ALU.add)
                    self.recip(ssum, ssum)
                    self.tt(pm[:, 5 * h:5 * h + 5], sc5,
                            ssum.broadcast_to([128, 5]), ALU.mult)
                r0 = t0 + 128 * j
                self.dma("s", pv, beta_d[r0:r0 + 128, 4:8])
                self.act(sq, pv, AF.Square)
                self.ts(sq, sq, 0.09, ALU.mult)
                self.ts(pv, pv, 0.9, ALU.mult)
                self.tt(pm[:, 20:24], pv, sq, ALU.add)
                self.ts2(pm[:, 24:28], pm[:, 20:24], -1.0, 1.0, ALU.mult, ALU.add)
                self.trans(pt[0:28, 128 * j:128 * (j + 1)], pm[:, :],
                           self.ident_f32[:, :])
            self.cp(evb[:, :], pt[0:28, 0:512])
            self.dma("s", PRF[0:28, t0:t0 + TT], evb[:, :])
        self.end_stage()

    def s_ctx_wo(self):
        T = self.T
        PRF = self.dram["PRF"].ap()
        S0 = self.dram["S0"].ap()
        paths = [self.dram[n].ap() for n in ("P3", "P7", "P15", "P31", "DOF")]
        VF = self.dram["VF"].ap()
        onw = self.sb("cw_onw", [128, 2], F32, stage=True)
        for r in range(2):
            self.dma("w", onw[:, r:r + 1], self.dram["L0_attn_o_norm_w"].ap()
                     [128 * r:128 * (r + 1)].rearrange("(a b) -> a b", b=1))
        wo = self.load_w_bf("cw_wo", self.dram["L0_attn_Wo"].ap(), D, D,
                            rowscale=lambda k: onw[:, k % 2:k % 2 + 1])
        rows = self.sb("cw_rows", [1, 8 * 512], F32, stage=True)
        rowsb = self.sb("cw_rb", [1, 8 * 512], BF16, stage=True)
        sl = [self.sb("cw_sl%d" % i, [128, TT], BF16, stage=True) for i in range(6)]
        oacc = [self.sb("cw_o%d" % i, [128, TT], F32, stage=True) for i in range(2)]
        tmp = self.sb("cw_tmp", [128, TT], F32, stage=True)
        sqb = self.sb("cw_sq", [128, TT], BF16, stage=True)
        onb = [self.sb("cw_on%d" % k, [128, TT], BF16, stage=True) for k in range(8)]
        rsr = self.sb("cw_rsr", [1, 512], F32, stage=True)
        rsb = self.sb("cw_rsb", [1, 512], BF16, stage=True)
        sfm = [self.sb("cw_sf%d" % i, [128, TT], BF16, stage=True) for i in range(8)]
        stm = [self.sb("cw_st%d" % i, [128, 1024], BF16, stage=True) for i in range(2)]
        pb = self.ps("cw_pb", [128, 512], F32, stage=True)
        pw = [self.ps("cw_pw%d" % i, [128, 512], F32, stage=True) for i in range(2)]
        ptr = [self.ps("cw_pt%d" % i, [128, 1024], BF16, stage=True) for i in range(2)]
        for it in range(self.NT):
            t0 = it * TT
            for h in range(4):
                for i in range(5):
                    self.dma("s", rows[:, i * 512:(i + 1) * 512],
                             PRF[5 * h + i:5 * h + i + 1, t0:t0 + TT])
                self.dma("s", rows[:, 5 * 512:6 * 512], PRF[20 + h:21 + h, t0:t0 + TT])
                self.dma("s", rows[:, 6 * 512:7 * 512], PRF[24 + h:25 + h, t0:t0 + TT])
                for i in range(5):
                    self.tt(rows[:, i * 512:(i + 1) * 512], rows[:, i * 512:(i + 1) * 512],
                            rows[:, 6 * 512:7 * 512], ALU.mult)
                self.cp(rowsb[:, 0:6 * 512], rows[:, 0:6 * 512])
                for mc in range(2):
                    m = 2 * h + mc
                    for i in range(5):
                        self.dma("s", sl[i], paths[i][128 * m:128 * (m + 1), t0:t0 + TT])
                    self.dma("s", sl[5], VF[128 * m:128 * (m + 1), 32 + t0:32 + t0 + TT])
                    o = oacc[mc]
                    for i in range(6):
                        self.mm(pb, self.ones_lhs, rowsb[:, i * 512:(i + 1) * 512],
                                True, True)
                        if i == 0:
                            self.tt(o, sl[0], pb, ALU.mult)
                        else:
                            self.tt(tmp, sl[i], pb, ALU.mult)
                            self.tt(o, o, tmp, ALU.add)
                for mc in range(2):
                    self.act(sqb, oacc[mc], AF.Square)
                    self.mm(pw[0][0:1, 0:512], self.ones_col, sqb, mc == 0, mc == 1)
                self.ts(rsr, pw[0][0:1, 0:512], 1.0 / 256.0, ALU.mult)
                self.ts(rsr, rsr, 1e-5, ALU.add)
                self.recip(rsr, rsr)
                self.act(rsr, rsr, AF.Sqrt)
                self.cp(rsb, rsr)
                self.mm(pb, self.ones_lhs, rsb, True, True)
                for mc in range(2):
                    self.tt(onb[2 * h + mc], oacc[mc], pb, ALU.mult)
            for m2 in range(8):
                p = pw[m2 % 2]
                for k in range(8):
                    self.mm(p, self.wch(wo, D, k, m2 * 128, (m2 + 1) * 128),
                            onb[k], k == 0, k == 7)
                self.cp(sfm[m2], p)
            for j in range(4):
                for m2 in range(8):
                    self.trans(ptr[j % 2][:, 128 * m2:128 * (m2 + 1)],
                               sfm[m2][:, 128 * j:128 * (j + 1)], self.ident_bf[:, :])
                st = stm[j % 2]
                self.cp(st, ptr[j % 2][:, :])
                self.dma("s", S0[t0 + 128 * j:t0 + 128 * (j + 1), :], st)
        self.end_stage()

    def s_shared(self):
        T = self.T
        XN = self.dram["XN0F"].ap()
        ws1 = self.load_w_bf("sh_w1", self.dram["L0_ffn_Ws1"].ap(), D, D)
        ws3 = self.load_w_bf("sh_w3", self.dram["L0_ffn_Ws3"].ap(), D, D)
        ws2 = self.load_w_bf("sh_w2", self.dram["L0_ffn_Ws2"].ap(), D, D)
        slabs = [self.sb("sh_s%d" % k, [128, TT], BF16, stage=True) for k in range(8)]
        h1 = [self.sb("sh_h1%d" % k, [128, TT], BF16, stage=True) for k in range(8)]
        hh = [self.sb("sh_hh%d" % k, [128, TT], BF16, stage=True) for k in range(8)]
        zbf = [self.sb("sh_z%d" % i, [128, TT], BF16, stage=True) for i in range(8)]
        ytl = [self.sb("sh_y%d" % i, [128, 1024], F32, stage=True) for i in range(2)]
        zt = self.sb("sh_zt", [128, 1024], F32, stage=True)
        pp = [self.ps("sh_p%d" % i, [128, 512], F32, stage=True) for i in range(3)]
        ptr = [self.ps("sh_pt%d" % i, [128, 1024], BF16, stage=True) for i in range(2)]
        self.memset(zt[:, :], 0.0)
        Y = [self.dram["Y%d" % i].ap() for i in range(4)]
        for c in range(self.NC):
            for yi in range(1, 4):
                self.dma("a", Y[yi][128 * c:128 * (c + 1), :], zt)
        for it in range(self.NT):
            t0 = it * TT
            for k in range(8):
                self.dma("s", slabs[k], XN[128 * k:128 * (k + 1), t0:t0 + TT])
            for m in range(8):
                p = pp[m % 2]
                for k in range(8):
                    self.mm(p, self.wch(ws1, D, k, m * 128, (m + 1) * 128), slabs[k],
                            k == 0, k == 7)
                self.act(h1[m], p, AF.Silu)
            for m in range(8):
                p = pp[m % 2]
                for k in range(8):
                    self.mm(p, self.wch(ws3, D, k, m * 128, (m + 1) * 128), slabs[k],
                            k == 0, k == 7)
                self.cp(hh[m], p)
                self.tt(hh[m], h1[m], hh[m], ALU.mult)
            for m2 in range(8):
                p = pp[2]
                for k in range(8):
                    self.mm(p, self.wch(ws2, D, k, m2 * 128, (m2 + 1) * 128), hh[k],
                            k == 0, k == 7)
                self.cp(zbf[m2], p)
            for j in range(4):
                for m2 in range(8):
                    self.trans(ptr[j % 2][:, 128 * m2:128 * (m2 + 1)],
                               zbf[m2][:, 128 * j:128 * (j + 1)], self.ident_bf[:, :])
                yt = ytl[j % 2]
                self.cp(yt, ptr[j % 2][:, :])
                self.dma("s", Y[0][t0 + 128 * j:t0 + 128 * (j + 1), :], yt)
        self.end_stage()


    def s_moe(self):
        T = self.T
        NS = CAP // 512
        XNAUG = self.dram["XNAUG"].ap()
        WSEL = self.dram["WSEL"].ap()
        CM = self.dram["CM"].ap()
        Y = [self.dram["Y%d" % i].ap() for i in range(4)]
        W1 = self.dram["L0_ffn_W1"].ap()
        W2 = self.dram["L0_ffn_W2"].ap()
        W3 = self.dram["L0_ffn_W3"].ap()
        idxs = self.sb("m_idx", [128, CAP // 16], I16, stage=True)
        gx = self.sb("m_gx", [128, 8 * CAP], BF16, stage=True)
        wv = self.sb("m_wv", [128, (CAP // 128) * 64], F32, stage=True)
        w1e = [self.sb("m_w1%d" % i, [128, 8 * 512], BF16, stage=True) for i in range(2)]
        w3e = [self.sb("m_w3%d" % i, [128, 8 * 512], BF16, stage=True) for i in range(2)]
        w2e = [self.sb("m_w2%d" % i, [128, 4 * 1024], BF16, stage=True) for i in range(2)]
        h1 = [self.sb("m_h1%d" % m, [128, CAP], BF16, stage=True) for m in range(4)]
        hh = [self.sb("m_hh%d" % m, [128, CAP], BF16, stage=True) for m in range(4)]
        ofm = [self.sb("m_of%d" % m, [128, CAP], BF16, stage=True) for m in range(8)]
        sct = self.sb("m_sc", [128, 4 * 1024], F32, stage=True)
        pp = [self.ps("m_p%d" % i, [128, 512], F32, stage=True) for i in range(3)]
        ptr = [self.ps("m_pt%d" % i, [128, 1024], BF16, stage=True) for i in range(2)]
        for e in range(32):
            wpar = e % 2
            for k in range(8):
                self.dma("w", w1e[wpar][:, 512 * k:512 * (k + 1)],
                         W1[e, 128 * k:128 * (k + 1), :])
                self.dma("w", w3e[wpar][:, 512 * k:512 * (k + 1)],
                         W3[e, 128 * k:128 * (k + 1), :])
            for k in range(4):
                self.dma("w", w2e[wpar][:, 1024 * k:1024 * (k + 1)],
                         W2[e, 128 * k:128 * (k + 1), :])
            cmrow = CM[e:e + 1, :].rearrange("o (p s) -> (o p) s", p=16)
            for g in range(8):
                self.dma("s", idxs[16 * g:16 * (g + 1), :], cmrow)
            self.P.op("p", "dma_gather",
                      (gx[:, :].rearrange("p (c n) -> p c n", c=8), XNAUG,
                       idxs[:, :], CAP, CAP, 1024),
                      {"transpose": True, "single_packet": False},
                      outs=(gx[:, :],), ins=(idxs[:, :],), qdma=True)
            self.P.op("p", "dma_gather",
                      (wv[:, :].rearrange("p (n c) -> p n c", c=64), WSEL,
                       idxs[:, :], CAP, CAP, 64),
                      {"transpose": False, "single_packet": False},
                      outs=(wv[:, :],), ins=(idxs[:, :],), qdma=True)
            for m in range(4):
                for nt in range(NS):
                    p = pp[nt % 2]
                    for k in range(8):
                        self.mm(p, w1e[wpar][:, 512 * k + 128 * m:512 * k + 128 * (m + 1)],
                                gx[:, CAP * k + 512 * nt:CAP * k + 512 * (nt + 1)],
                                k == 0, k == 7)
                    self.act(h1[m][:, 512 * nt:512 * (nt + 1)], p, AF.Silu)
            for m in range(4):
                for nt in range(NS):
                    p = pp[nt % 2]
                    for k in range(8):
                        self.mm(p, w3e[wpar][:, 512 * k + 128 * m:512 * k + 128 * (m + 1)],
                                gx[:, CAP * k + 512 * nt:CAP * k + 512 * (nt + 1)],
                                k == 0, k == 7)
                    self.cp(hh[m][:, 512 * nt:512 * (nt + 1)], p)
                self.tt(hh[m][:, :], h1[m][:, :], hh[m][:, :], ALU.mult)
            for m2 in range(8):
                for nt in range(NS):
                    p = pp[nt % 2]
                    for k in range(4):
                        self.mm(p, w2e[wpar][:, 1024 * k + 128 * m2:1024 * k + 128 * (m2 + 1)],
                                hh[k][:, 512 * nt:512 * (nt + 1)], k == 0, k == 3)
                    self.cp(ofm[m2][:, 512 * nt:512 * (nt + 1)], p)
            for jg in range(NS):
                for j4 in range(4):
                    j = 4 * jg + j4
                    for m2 in range(8):
                        self.trans(ptr[j4 % 2][:, 128 * m2:128 * (m2 + 1)],
                                   ofm[m2][:, 128 * j:128 * (j + 1)],
                                   self.ident_bf[:, :])
                    self.tt(sct[:, 1024 * j4:1024 * (j4 + 1)], ptr[j4 % 2][:, :],
                            wv[:, 64 * (4 * jg + j4) + e:64 * (4 * jg + j4) + e + 1]
                            .broadcast_to([128, 1024]), ALU.mult)
                self.P.op("p", "dma_scatter_add",
                          (Y[e % 4], sct[:, :].rearrange("p (n c) -> p n c", c=1024),
                           idxs[:, 32 * jg:32 * (jg + 1)], 512, 512, 1024),
                          {"single_packet": False},
                          outs=(Y[e % 4],), ins=(sct[:, :], idxs[:, :]), qdma=True)
        self.end_stage()

    def s_combine0(self):
        T = self.T
        x0 = self.dram["x0"].ap()
        S0 = self.dram["S0"].ap()
        Y = [self.dram["Y%d" % i].ap() for i in range(4)]
        X1 = self.dram["X1"].ap()
        XFA = self.dram["XFA"].ap()
        XN1F = self.dram["XN1F"].ap()
        HN1F = self.dram["HN1F"].ap()
        xt = [self.sb("cb_x%d" % i, [128, D], F32, stage=True) for i in range(2)]
        yt = [self.sb("cb_y%d" % i, [128, D], F32, stage=True) for i in range(2)]
        st = [self.sb("cb_s%d" % i, [128, D], BF16, stage=True) for i in range(2)]
        g1 = self.sb("cb_g", [128, D], F32, stage=True)
        t1 = self.sb("cb_t", [128, D], F32, stage=True)
        sqb = self.sb("cb_sq", [128, D], BF16, stage=True)
        xnb = [self.sb("cb_xn%d" % j, [128, D], BF16, stage=True) for j in range(4)]
        hnb = [self.sb("cb_hn%d" % j, [128, D], BF16, stage=True) for j in range(4)]
        cols = self.sb("cb_c", [128, 8], F32, stage=True)
        pbf = [self.ps("cb_p%d" % i, [128, 1024], BF16, stage=True) for i in range(2)]
        ev = [self.sb("cb_e%d" % i, [128, 512], BF16, stage=True) for i in range(2)]
        for it in range(self.NT):
            t0 = it * TT
            for j in range(4):
                r0 = t0 + 128 * j
                xb = xt[j % 2]
                yb = yt[j % 2]
                sb_ = st[j % 2]
                self.dma("s", xb, x0[r0:r0 + 128, :])
                self.dma("s", sb_, S0[r0:r0 + 128, :])
                self.dma("s", yb, Y[0][r0:r0 + 128, :])
                self.dma("s", g1, Y[1][r0:r0 + 128, :])
                self.tt(g1, g1, yb, ALU.add)
                self.dma("s", yb, Y[2][r0:r0 + 128, :])
                self.tt(g1, g1, yb, ALU.add)
                self.dma("s", yb, Y[3][r0:r0 + 128, :])
                self.tt(g1, g1, yb, ALU.add)
                self.act(t1, g1, AF.Exp, scale=-1.0)
                self.ts(t1, t1, 1.0, ALU.add)
                self.recip(t1, t1)
                self.tt(g1, g1, t1, ALU.mult)
                self.tt(t1, sb_, g1, ALU.mult)
                self.tt(xb, xb, t1, ALU.add)
                self.dma("s", X1[r0:r0 + 128, :], xb)
                nm = cols[:, 0:1]
                v = cols[:, 1:2]
                rs = cols[:, 2:3]
                self.red(nm, xb, ALU.add, negate=True)
                self.ts(nm, nm, 1.0 / D, ALU.mult)
                self.act(xb, xb, AF.Identity, bias=nm)
                self.act(sqb, xb, AF.Square)
                self.red(v, sqb, ALU.add)
                self.ts(v, v, 1.0 / D, ALU.mult)
                self.ts(v, v, 1e-5, ALU.add)
                self.act(v, v, AF.Ln)
                self.act(rs, v, AF.Exp, scale=-0.5)
                self.tt(xb, xb, rs.broadcast_to([128, D]), ALU.mult)
                self.dma("s", XFA[8 + r0:8 + r0 + 128, :], xb)
                self.cp(xnb[j], xb)
                v2 = cols[:, 3:4]
                rs2 = cols[:, 4:5]
                self.act(sqb, xb, AF.Square)
                self.red(v2, sqb, ALU.add)
                self.ts(v2, v2, 1.0 / D, ALU.mult)
                self.ts(v2, v2, 1e-6, ALU.add)
                self.act(v2, v2, AF.Ln)
                self.act(rs2, v2, AF.Exp, scale=-0.5)
                self.tt(t1, xb, rs2.broadcast_to([128, D]), ALU.mult)
                self.cp(hnb[j], t1)
            for c in range(8):
                for j in range(4):
                    self.trans(pbf[0][:, 128 * j:128 * (j + 1)],
                               xnb[j][:, 128 * c:128 * (c + 1)], self.ident_bf[:, :])
                    self.trans(pbf[1][:, 128 * j:128 * (j + 1)],
                               hnb[j][:, 128 * c:128 * (c + 1)], self.ident_bf[:, :])
                self.cp(ev[0], pbf[0][:, 0:512])
                self.cp(ev[1], pbf[1][:, 0:512])
                self.dma("s", XN1F[128 * c:128 * (c + 1), t0:t0 + TT], ev[0])
                self.dma("s", HN1F[128 * c:128 * (c + 1), t0:t0 + TT], ev[1])
        self.end_stage()


    def s_l1proj(self):
        T = self.T
        HN = self.dram["HN1F"].ap()
        GL = self.dram["GL1"].ap()
        nw = self.sb("lp_nw", [128, 8], F32, stage=True)
        for k in range(8):
            self.dma("w", nw[:, k:k + 1], self.dram["L1_attn_norm_w"].ap()
                     [128 * k:128 * (k + 1)].rearrange("(a b) -> a b", b=1))
        rsc = lambda k: nw[:, k:k + 1]
        wq = self.load_w_bf("lp_wq", self.dram["L1_attn_WqT"].ap(), D, D, rowscale=rsc)
        wk = self.load_w_bf("lp_wk", self.dram["L1_attn_WkT"].ap(), D, D, rowscale=rsc)
        wv = self.load_w_bf("lp_wv", self.dram["L1_attn_WvT"].ap(), D, D, rowscale=rsc)
        wg = self.load_w_bf("lp_wg", self.dram["L1_attn_WgT"].ap(), D, 24, rowscale=rsc)
        gbrow = self.sb("lp_gb", [1, 24], BF16, stage=True)
        self.dma("w", gbrow, self.dram["L1_attn_gb"].ap().rearrange("(o a) -> o a", o=1))
        swap = self.sb("lp_swap", [128, 128], BF16, stage=True)
        self.dma("s", swap, self.dram["c_swap_h_bf"].ap())
        self.cum_bf = self.sb("lp_cum", [128, 128], BF16, stage=True)
        self.dma("s", self.cum_bf, self.dram["c_cum_bf"].ap())
        cs2 = self.sb("lp_cs", [128, TT], F32, stage=True)
        ss2 = self.sb("lp_ss", [128, TT], F32, stage=True)
        dsts = {"q": self.dram["QF1"].ap(), "k": self.dram["KF1"].ap(),
                "v": self.dram["VF1"].ap()}
        slabs = [self.sb("lp_s%d" % k, [128, TT], BF16, stage=True) for k in range(8)]
        pre = [self.sb("lp_pre%d" % i, [128, TT], BF16, stage=True) for i in range(2)]
        rot = [self.sb("lp_rot%d" % i, [128, TT], F32, stage=True) for i in range(2)]
        ob = [self.sb("lp_o%d" % i, [128, TT], BF16, stage=True) for i in range(2)]
        e24 = self.sb("lp_e24", [128, 24], F32, stage=True)
        l16 = self.sb("lp_l16", [128, 16], BF16, stage=True)
        lf = self.sb("lp_lf", [128, 8], F32, stage=True)
        glo = self.sb("lp_glo", [128, 24], F32, stage=True)
        pp = [self.ps("lp_p%d" % i, [128, 512], F32, stage=True) for i in range(3)]
        psw = self.ps("lp_psw", [128, 512], F32, stage=True)
        for it in range(self.NT):
            t0 = it * TT
            for k in range(8):
                self.dma("s", slabs[k], HN[128 * k:128 * (k + 1), t0:t0 + TT])
            self.dma("s", cs2, self.dram["c_rope1_cs2"].ap()[:, t0:t0 + TT])
            self.dma("s", ss2, self.dram["c_rope1_ss2"].ap()[:, t0:t0 + TT])
            for nm, w in (("q", wq), ("k", wk), ("v", wv)):
                for m in range(8):
                    p = pp[m % 2]
                    for k in range(8):
                        self.mm(p, self.wch(w, D, k, m * 128, (m + 1) * 128),
                                slabs[k], k == 0, k == 7)
                    if nm == "v":
                        self.cp(ob[m % 2], p)
                        self.dma("s", dsts[nm][128 * m:128 * (m + 1), t0:t0 + TT],
                                 ob[m % 2])
                    else:
                        pr = pre[m % 2]
                        self.cp(pr, p)
                        self.mm(psw, swap, pr, True, True)
                        r = rot[m % 2]
                        self.tt(r, pr, cs2, ALU.mult)
                        self.tt(rot[1 - m % 2], psw, ss2, ALU.mult)
                        self.tt(ob[m % 2], r, rot[1 - m % 2], ALU.add)
                        self.dma("s", dsts[nm][128 * m:128 * (m + 1), t0:t0 + TT],
                                 ob[m % 2])
            for j in range(4):
                p24 = pp[2][:, 0:24]
                for k in range(8):
                    self.mm(p24, slabs[k][:, 128 * j:128 * (j + 1)],
                            self.wch(wg, 24, k, 0, 24), k == 0, False)
                self.mm(p24, self.ones_lhs, gbrow, False, True)
                self.act(e24, p24, AF.Exp, scale=-1.0)
                self.ts(e24, e24, 1.0, ALU.add)
                self.recip(glo[:, :], e24)
                self.act(lf, e24[:, 0:8], AF.Ln)
                self.cp(l16[:, 0:8], lf)
                self.tt(lf, lf, l16[:, 0:8], ALU.subtract)
                self.cp(l16[:, 8:16], lf)
                pl = pp[2][:, 32:48]
                self.mm(pl, self.cum_bf[:, :], l16, True, True)
                self.cp(lf, pl[:, 0:8])
                self.tt(lf, lf, pl[:, 8:16], ALU.add)
                self.ts(lf, lf, -1.0, ALU.mult)
                r0 = t0 + 128 * j
                self.dma("s", GL[r0:r0 + 128, 0:8], lf)
                self.dma("s", GL[r0:r0 + 128, 8:24], glo[:, 8:24])
        self.end_stage()

    def s_gla(self):
        T = self.T
        QF, KF, VF = (self.dram[n].ap() for n in ("QF1", "KF1", "VF1"))
        GL = self.dram["GL1"].ap()
        Y1 = self.dram["Y1F"].ap()
        M_sb = [self.sb("g_M%d" % h, [128, 128], F32, stage=True) for h in range(8)]
        M_bf = [self.sb("g_Mb%d" % h, [128, 128], BF16, stage=True) for h in range(8)]
        for h in range(8):
            self.memset(M_sb[h][:, :], 0.0)
            self.memset(M_bf[h][:, :], 0.0)
        ones2 = self.sb("g_ones2", [2, 128], BF16, stage=True)
        self.dma("s", ones2, self.dram["c_ones2_bf"].ap())
        gm = self.sb("g_gm", [128, 128], F32, stage=True)
        gb = self.sb("g_gb", [128, 128], F32, stage=True)
        self.dma("s", gm, self.dram["c_gla_m"].ap())
        self.dma("s", gb, self.dram["c_gla_b"].ap())
        qsl = [self.sb("g_q%d" % h, [128, TT], BF16, stage=True) for h in range(8)]
        ksl = [self.sb("g_k%d" % h, [128, TT], BF16, stage=True) for h in range(8)]
        vsl = [self.sb("g_v%d" % h, [128, TT], BF16, stage=True) for h in range(8)]
        glt = [self.sb("g_gl%d" % j, [128, 24], F32, stage=True) for j in range(4)]
        lb2 = self.sb("g_lb2", [128, 2], BF16, stage=True)
        lrow = self.sb("g_lrow", [2, 128], BF16, stage=True)
        ogb = self.sb("g_ogb", [128, 1], BF16, stage=True)
        ogrow = self.sb("g_ogr", [1, 128], BF16, stage=True)
        ogsb = self.sb("g_ogsb", [128, 128], F32, stage=True)
        dm = self.sb("g_dm", [128, 128], F32, stage=True)
        wd = self.sb("g_wd", [128, 128], F32, stage=True)
        elbc = self.sb("g_el", [128, 128], F32, stage=True)
        qd = self.sb("g_qd", [128, 128], BF16, stage=True)
        att = self.sb("g_att", [128, 128], BF16, stage=True)
        vtm = self.sb("g_vtm", [128, 128], BF16, stage=True)
        kd = self.sb("g_kd", [128, 128], BF16, stage=True)
        ybf = self.sb("g_y", [128, 128], BF16, stage=True)
        gscol = self.sb("g_gs", [128, 1], F32, stage=True)
        dcol = self.sb("g_dc", [128, 1], F32, stage=True)
        elc = self.sb("g_elc", [128, 1], F32, stage=True)
        pS = self.ps("g_pS", [128, 512], F32, stage=True)
        pY = self.ps("g_pY", [128, 512], F32, stage=True)
        pM = self.ps("g_pM", [128, 512], F32, stage=True)
        pB = self.ps("g_pB", [128, 512], F32, stage=True)
        pT = self.ps("g_pT", [128, 1024], BF16, stage=True)
        SC = 1.0 / (128.0 ** 0.5)
        for it in range(self.NT):
            t0 = it * TT
            for h in range(8):
                self.dma("s", qsl[h], QF[128 * h:128 * (h + 1), t0:t0 + TT])
                self.dma("s", ksl[h], KF[128 * h:128 * (h + 1), t0:t0 + TT])
                self.dma("s", vsl[h], VF[128 * h:128 * (h + 1), t0:t0 + TT])
            for j in range(4):
                self.dma("s", glt[j], GL[t0 + 128 * j:t0 + 128 * (j + 1), :])
            for j in range(4):
                sl = slice(128 * j, 128 * (j + 1))
                for h in range(8):
                    L = glt[j][:, h:h + 1]
                    q = qsl[h][:, sl]
                    k = ksl[h][:, sl]
                    v = vsl[h][:, sl]
                    self.cp(lb2[:, 0:1], L)
                    self.tt(lb2[:, 1:2], L, lb2[:, 0:1], ALU.subtract)
                    self.mm(pB[0:2, 0:128], lb2, self.ident_bf[:, :], True, True)
                    self.cp(lrow, pB[0:2, 0:128])
                    self.mm(pB[:, 128:256], ones2, lrow, True, True)
                    self.cp(ogb, glt[j][:, 16 + h:17 + h])
                    self.mm(pB[0:1, 384:512], ogb, self.ident_bf[:, :], True, True)
                    self.cp(ogrow, pB[0:1, 384:512])
                    self.mm(pB[:, 256:384], self.ones_lhs, ogrow, True, True)
                    self.cp(ogsb, pB[:, 256:384])
                    self.tt(dm, pB[:, 128:256], L.broadcast_to([128, 128]), ALU.subtract)
                    self.tt(dm, dm, gm, ALU.mult)
                    self.tt(dm, dm, gb, ALU.add)
                    self.act(wd, dm, AF.Exp)
                    self.act(elbc, pB[:, 128:256], AF.Exp)
                    self.ts(gscol, glt[j][:, 8 + h:9 + h], SC, ALU.mult)
                    self.mm(pS[:, 0:128], k, q, True, True)
                    self.tt(wd, pS[:, 0:128], wd, ALU.mult)
                    self.tt(att, wd, gscol.broadcast_to([128, 128]), ALU.mult)
                    self.tt(qd, q, elbc, ALU.mult)
                    self.trans(pT[:, 0:128], v, self.ident_bf[:, :])
                    self.cp(vtm, pT[:, 0:128])
                    self.mm(pY[:, 0:128], M_bf[h], qd, True, False)
                    self.mm(pY[:, 0:128], vtm, att, False, True)
                    self.tt(ybf, pY[:, 0:128], ogsb, ALU.mult)
                    self.dma("s", Y1[128 * h:128 * (h + 1), t0 + 128 * j:
                                     t0 + 128 * (j + 1)], ybf)
                    self.mm(pB[0:128, 128:129], ones2, lrow[:, 127:128], True, True)
                    self.tt(dcol, pB[0:128, 128:129], L, ALU.subtract)
                    self.act(dcol, dcol, AF.Exp)
                    self.tt(dcol, dcol, gscol, ALU.mult)
                    self.act(elc, pB[0:128, 128:129], AF.Exp)
                    self.trans(pT[:, 128:256], k, self.ident_bf[:, :])
                    self.tt(kd, pT[:, 128:256], dcol.broadcast_to([128, 128]), ALU.mult)
                    self.mm(pM[:, 0:128], kd, vtm, True, True)
                    self.stt(M_sb[h], M_sb[h], elc, pM[:, 0:128], ALU.mult, ALU.add)
                    self.cp(M_bf[h], M_sb[h])
        self.end_stage()

    def s_wout(self):
        T = self.T
        Y1 = self.dram["Y1F"].ap()
        S1 = self.dram["S1"].ap()
        wo = self.load_w_bf("wo_w", self.dram["L1_attn_WoutT"].ap(), D, D)
        bcol = self.sb("wo_b", [128, 8], F32, stage=True)
        for m in range(8):
            self.dma("w", bcol[:, m:m + 1], self.dram["L1_attn_bout"].ap()
                     [128 * m:128 * (m + 1)].rearrange("(a b) -> a b", b=1))
        slabs = [self.sb("wo_s%d" % k, [128, TT], BF16, stage=True) for k in range(8)]
        sfm = [self.sb("wo_f%d" % m, [128, TT], BF16, stage=True) for m in range(8)]
        stm = [self.sb("wo_t%d" % i, [128, 1024], BF16, stage=True) for i in range(2)]
        pp = [self.ps("wo_p%d" % i, [128, 512], F32, stage=True) for i in range(2)]
        ptr = [self.ps("wo_pt%d" % i, [128, 1024], BF16, stage=True) for i in range(2)]
        for it in range(self.NT):
            t0 = it * TT
            for k in range(8):
                self.dma("s", slabs[k], Y1[128 * k:128 * (k + 1), t0:t0 + TT])
            for m in range(8):
                p = pp[m % 2]
                for k in range(8):
                    self.mm(p, self.wch(wo, D, k, m * 128, (m + 1) * 128), slabs[k],
                            k == 0, k == 7)
                self.act(sfm[m], p, AF.Identity, bias=bcol[:, m:m + 1])
            for j in range(4):
                for m in range(8):
                    self.trans(ptr[j % 2][:, 128 * m:128 * (m + 1)],
                               sfm[m][:, 128 * j:128 * (j + 1)], self.ident_bf[:, :])
                st = stm[j % 2]
                self.cp(st, ptr[j % 2][:, :])
                self.dma("s", S1[t0 + 128 * j:t0 + 128 * (j + 1), :], st)
        self.end_stage()


    def s_l1ffn_a(self, li):
        T = self.T
        src_x = self.dram["XFA"].ap()
        dst_x = self.dram["XFB"].ap()
        wqkv = self.load_w_bf("fa_wqkv", self.dram["L1_ffn_Wqkv"].ap()[li], D, 3 * D)
        wo = self.load_w_bf("fa_wo", self.dram["L1_ffn_Wo"].ap()[li], D, D)
        qsn = self.sb("fa_qsn", [128, 4], F32, stage=True)
        for (ci, nm, swp) in ((0, "L1_ffn_q_scale", 0), (1, "L1_ffn_q_scale", 1),
                              (2, "L1_ffn_k_scale", 0), (3, "L1_ffn_k_scale", 1)):
            wsrc = self.dram[nm].ap()[li]
            for half in range(2):
                for p32 in range(2):
                    s0 = 32 * ((p32 + swp) % 2)
                    self.dma("w", qsn[64 * half + 32 * p32:64 * half + 32 * (p32 + 1),
                                      ci:ci + 1],
                             wsrc[s0:s0 + 32].rearrange("(a b) -> a b", b=1))
        self.ts(qsn[:, 0:1], qsn[:, 0:1], 8.0, ALU.mult)
        self.ts(qsn[:, 1:2], qsn[:, 1:2], 8.0, ALU.mult)
        swap = self.sb("fa_swap", [128, 128], BF16, stage=True)
        self.dma("s", swap, self.dram["c_swap64_bf"].ap())
        ind2 = self.sb("fa_ind2", [128, 2], BF16, stage=True)
        self.dma("s", ind2, self.dram["c_ind2_bf"].ap())
        ind2T = self.sb("fa_ind2T", [2, 128], BF16, stage=True)
        self.dma("s", ind2T, self.dram["c_ind2T_bf"].ap())
        ones64 = self.sb("fa_o64", [1, 64], BF16, stage=True)
        self.dma("s", ones64, self.dram["c_ones64_bf"].ap())
        lamg = self.sb("fa_lg", [128, 384], BF16, stage=True)
        lamf0 = self.sb("fa_lf0", [128, 384], BF16, stage=True)
        lamf1 = self.sb("fa_lf1", [128, 384], BF16, stage=True)
        self.dma("s", lamg, self.dram["c_lam_gen_bf"].ap())
        self.dma("s", lamf0, self.dram["c_lam_f0_bf"].ap())
        self.dma("s", lamf1, self.dram["c_lam_f1_bf"].ap())
        tabs = [self.sb("fa_tab%d" % i, [128, TT], F32, stage=True) for i in range(4)]
        AB = [self.sb("fa_ab%d" % i, [128, TT], BF16, stage=True) for i in range(4)]
        xsb = [self.sb("fa_xs%d" % j, [128, D], BF16, stage=True) for j in range(4)]
        xsfm = [self.sb("fa_xf%d" % k, [128, TT], BF16, stage=True) for k in range(8)]
        qraw = [self.sb("fa_qr%d" % c, [128, TT], BF16, stage=True) for c in range(2)]
        sqb = self.sb("fa_sq", [128, TT], BF16, stage=True)
        nrm = self.sb("fa_nrm", [2, 8 * TT], F32, stage=True)
        nrmb = self.sb("fa_nrmb", [2, 8 * TT], BF16, stage=True)
        qn = self.sb("fa_qn", [128, TT], BF16, stage=True)
        r1 = self.sb("fa_r1", [128, TT], F32, stage=True)
        r2 = self.sb("fa_r2", [128, TT], F32, stage=True)
        qrot = [self.sb("fa_qq%d" % c, [128, TT], BF16, stage=True) for c in range(8)]
        kroll = [self.sb("fa_kr%d" % c, [128, 768], BF16, stage=True) for c in range(8)]
        vroll = [self.sb("fa_vr%d" % c, [128, 768], BF16, stage=True) for c in range(8)]
        vtm = [self.sb("fa_vt%d" % h, [128, 384], BF16, stage=True) for h in range(16)]
        pbf = self.sb("fa_pbf", [128, 384], BF16, stage=True)
        ptsb = self.sb("fa_pts", [128, 384], BF16, stage=True)
        rrow = self.sb("fa_rr", [1, 256], F32, stage=True)
        rrb = self.sb("fa_rrb", [1, 256], BF16, stage=True)
        bcs = self.sb("fa_bcs", [128, 128], F32, stage=True)
        afm = [self.sb("fa_af%d" % c, [128, TT], BF16, stage=True) for c in range(8)]
        wofm = [self.sb("fa_wf%d" % m, [128, TT], BF16, stage=True) for m in range(8)]
        xut = [self.sb("fa_xu0", [128, D], F32, stage=True)]
        xot = [self.sb("fa_xo0", [128, D], F32, stage=True)]
        cols = self.sb("fa_c", [128, 4], F32, stage=True)
        pQ = [self.ps("fa_pq%d" % i, [128, 512], F32, stage=True) for i in range(2)]
        pN = self.ps("fa_pn", [128, 512], F32, stage=True)
        pBC = self.ps("fa_pbc", [128, 512], F32, stage=True)
        pS = self.ps("fa_ps", [128, 512], F32, stage=True)
        pOR = self.ps("fa_por", [128, 512], F32, stage=True)
        pT1 = self.ps("fa_pt1", [128, 1024], BF16, stage=True)
        pT2 = self.ps("fa_pt2", [128, 1024], BF16, stage=True)
        for c in range(8):
            self.memset(kroll[c][:, :], 0.0)
            self.memset(vroll[c][:, :], 0.0)
        for h in range(16):
            self.memset(vtm[h][:, :], 0.0)
        for it in range(self.NT):
            t0 = it * TT
            if it > 0:
                for c in range(8):
                    self.cp(kroll[c][:, 0:256], kroll[c][:, 512:768])
                    self.cp(vroll[c][:, 0:256], vroll[c][:, 512:768])
                for h in range(16):
                    self.cp(vtm[h][:, 0:128], vtm[h][:, 256:384])
            self.dma("s", tabs[0][0:64, :], self.dram["c_xp_cs"].ap()[0:64, t0:t0 + TT])
            self.dma("s", tabs[0][64:128, :], self.dram["c_xp_cs"].ap()[0:64, t0:t0 + TT])
            self.dma("s", tabs[1][0:64, :], self.dram["c_xp_ssq"].ap()[0:64, t0:t0 + TT])
            self.dma("s", tabs[1][64:128, :], self.dram["c_xp_ssq"].ap()[0:64, t0:t0 + TT])
            self.dma("s", tabs[2][0:64, :], self.dram["c_xp_ci"].ap()[0:64, t0:t0 + TT])
            self.dma("s", tabs[2][64:128, :], self.dram["c_xp_ci"].ap()[0:64, t0:t0 + TT])
            self.dma("s", tabs[3][0:64, :], self.dram["c_xp_ssk"].ap()[0:64, t0:t0 + TT])
            self.dma("s", tabs[3][64:128, :], self.dram["c_xp_ssk"].ap()[0:64, t0:t0 + TT])
            self.tt(AB[0], tabs[0], qsn[:, 0:1].broadcast_to([128, TT]), ALU.mult)
            self.tt(AB[1], tabs[1], qsn[:, 1:2].broadcast_to([128, TT]), ALU.mult)
            self.tt(AB[2], tabs[2], qsn[:, 2:3].broadcast_to([128, TT]), ALU.mult)
            self.tt(AB[3], tabs[3], qsn[:, 3:4].broadcast_to([128, TT]), ALU.mult)
            for j in range(4):
                r0 = t0 + 128 * j
                self.dma("w", xsb[j][:, 0:512], src_x[8 + r0:8 + r0 + 128, 0:512])
                self.dma("w", xsb[j][:, 512:1024], src_x[7 + r0:7 + r0 + 128, 512:1024])
            for c in range(8):
                for j in range(4):
                    self.trans(pT1[:, 128 * j:128 * (j + 1)],
                               xsb[j][:, 128 * c:128 * (c + 1)], self.ident_bf[:, :])
                self.cp(xsfm[c], pT1[:, 0:512])
            for grp, base in (("q", 0), ("k", 8), ("v", 16)):
                for c in range(8):
                    m = base + c
                    p = pQ[c % 2]
                    for k in range(8):
                        self.mm(p, self.wch(wqkv, 3 * D, k, m * 128, (m + 1) * 128),
                                xsfm[k], k == 0, k == 7)
                    if grp == "v":
                        self.cp(vroll[c][:, 256:768], p)
                    else:
                        qr_ = qraw[c % 2]
                        self.cp(qr_, p)
                        self.act(sqb, p, AF.Square)
                        nd = nrm
                        ndb = nrmb
                        self.mm(pN[0:2, 0:512], ind2, sqb, True, True)
                        self.cp(nd[:, TT * c:TT * (c + 1)], pN[0:2, 0:512])
                        seg = nd[:, TT * c:TT * (c + 1)]
                        self.act(seg, seg, AF.Ln)
                        self.act(seg, seg, AF.Exp, scale=0.5)
                        self.ts(seg, seg, 1e-12, ALU.max)
                        self.recip(seg, seg)
                        self.cp(ndb[:, TT * c:TT * (c + 1)], seg)
                        self.mm(pBC, ind2T, ndb[:, TT * c:TT * (c + 1)], True, True)
                        self.tt(qn, qr_, pBC, ALU.mult)
                        self.mm(pS, swap, qn, True, True)
                        ta = AB[0] if grp == "q" else AB[2]
                        tb = AB[1] if grp == "q" else AB[3]
                        self.tt(r1, qn, ta, ALU.mult)
                        self.tt(r2, pS, tb, ALU.mult)
                        if grp == "q":
                            self.tt(qrot[c], r1, r2, ALU.add)
                        else:
                            self.tt(kroll[c][:, 256:768], r1, r2, ALU.add)
            for c in range(8):
                for half in range(2):
                    h = 2 * c + half
                    for j in range(4):
                        self.trans(pT2[:, 64 * j:64 * (j + 1)],
                                   vroll[c][64 * half:64 * (half + 1),
                                            256 + 128 * j:256 + 128 * (j + 1)],
                                   self.ident_bf[64 * half:64 * (half + 1),
                                                  64 * half:64 * (half + 1)])
                    self.cp(vtm[h][:, 128:384], pT2[:, 0:256])
            for j in range(4):
                if it == 0 and j == 0:
                    lam = lamf0
                elif it == 0 and j == 1:
                    lam = lamf1
                else:
                    lam = lamg
                for c in range(8):
                    for half in range(2):
                        h = 2 * c + half
                        hb = 64 * half
                        grpS = pS[:, 0:384]
                        self.mm(grpS, self.ident_bf[:, :], lam, True, False)
                        self.mm(grpS, qrot[c][hb:hb + 64, 128 * j:128 * (j + 1)],
                                kroll[c][hb:hb + 64, 128 * j:128 * j + 384],
                                False, True)
                        nmx = cols[:, 0:1]
                        self.red(nmx, grpS, ALU.max, negate=True)
                        self.act(pbf, grpS, AF.Exp, bias=nmx)
                        for b in range(3):
                            self.trans(pT1[:, 128 * b:128 * (b + 1)],
                                       pbf[:, 128 * b:128 * (b + 1)],
                                       self.ident_bf[:, :])
                        self.cp(ptsb, pT1[:, 0:384])
                        for b in range(3):
                            self.mm(pOR[0:1, 128 * half:128 * (half + 1)],
                                    self.ones_col, ptsb[:, 128 * b:128 * (b + 1)],
                                    b == 0, b == 2)
                        for b in range(3):
                            self.mm(pOR[hb:hb + 64, 256:384],
                                    vtm[h][:, 64 * (j + b):64 * (j + b + 1)],
                                    ptsb[:, 128 * b:128 * (b + 1)],
                                    b == 0, b == 2)
                    self.recip(rrow, pOR[0:1, 0:256])
                    self.cp(rrb, rrow)
                    self.mm(pBC[0:64, 0:128], ones64, rrb[:, 0:128], True, True)
                    self.mm(pBC[64:128, 0:128], ones64, rrb[:, 128:256], True, True)
                    self.cp(bcs, pBC[:, 0:128])
                    self.tt(afm[c][:, 128 * j:128 * (j + 1)], pOR[:, 256:384], bcs,
                            ALU.mult)
            for m in range(8):
                p = pQ[m % 2]
                for k in range(8):
                    self.mm(p, self.wch(wo, D, k, m * 128, (m + 1) * 128), afm[k],
                            k == 0, k == 7)
                self.cp(wofm[m], p)
            for j in range(4):
                r0 = t0 + 128 * j
                for m in range(8):
                    self.trans(pT2[:, 128 * m:128 * (m + 1)],
                               wofm[m][:, 128 * j:128 * (j + 1)], self.ident_bf[:, :])
                xu = xut[0]
                xo = xot[0]
                self.dma("s", xu, src_x[8 + r0:8 + r0 + 128, :])
                self.tt(xo, pT2[:, 0:1024], xu, ALU.add)
                self.dma("s", dst_x[8 + r0:8 + r0 + 128, :], xo)
        self.end_stage()

    def s_l1ffn_b(self, li):
        T = self.T
        src_x = self.dram["XFB"].ap()
        dst_x = self.dram["XFA"].ap()
        HP = 2816
        W1 = self.dram["L1_ffn_W1"].ap()[li]
        w1a = self.sb("fb_w1a", [128, 8 * HP], BF16, stage=True)
        w1g = self.sb("fb_w1g", [128, 8 * HP], BF16, stage=True)
        for k in range(8):
            self.dma("w", w1a[:, HP * k:HP * (k + 1)],
                     W1[128 * k:128 * (k + 1), 0:HP])
            self.dma("w", w1g[:, HP * k:HP * (k + 1)],
                     W1[128 * k:128 * (k + 1), HP:2 * HP])
        w2 = self.load_w_bf("fb_w2", self.dram["L1_ffn_W2"].ap()[li], HP, D)
        lngc = self.sb("fb_lng", [128, 8], F32, stage=True)
        lnbc = self.sb("fb_lnb", [128, 8], BF16, stage=True)
        for k in range(8):
            self.dma("w", lngc[:, k:k + 1], self.dram["L1_ffn_ln_g"].ap()
                     [li, 128 * k:128 * (k + 1)].rearrange("(a b) -> a b", b=1))
            self.dma("w", lnbc[:, k:k + 1], self.dram["L1_ffn_ln_b"].ap()
                     [li, 128 * k:128 * (k + 1)].rearrange("(a b) -> a b", b=1))
        b1pa = self.sb("fb_b1a", [128, 22], F32, stage=True)
        b1pg = self.sb("fb_b1g", [128, 22], F32, stage=True)
        b2c = self.sb("fb_b2", [128, 8], F32, stage=True)
        for m in range(22):
            self.dma("w", b1pa[:, m:m + 1], self.dram["L1_ffn_b1"].ap()
                     [li, 128 * m:128 * (m + 1)].rearrange("(a b) -> a b", b=1))
            self.dma("w", b1pg[:, m:m + 1], self.dram["L1_ffn_b1"].ap()
                     [li, HP + 128 * m:HP + 128 * (m + 1)].rearrange("(a b) -> a b", b=1))
        for m in range(8):
            self.dma("w", b2c[:, m:m + 1], self.dram["L1_ffn_b2"].ap()
                     [li, 128 * m:128 * (m + 1)].rearrange("(a b) -> a b", b=1))
        pE = self.ps("fb_pe", [128, 512], F32, stage=True)
        for (wx, bx) in ((w1a, b1pa), (w1g, b1pg)):
            for m in range(22):
                for k in range(8):
                    self.mm(pE[:, m:m + 1],
                            wx[:, HP * k + 128 * m:HP * k + 128 * (m + 1)],
                            lnbc[:, k:k + 1], k == 0, k == 7)
            self.tt(bx[:, :], bx[:, :], pE[:, 0:22], ALU.add)
        for k in range(8):
            self.tt(w1a[:, HP * k:HP * (k + 1)], w1a[:, HP * k:HP * (k + 1)],
                    lngc[:, k:k + 1].broadcast_to([128, HP]), ALU.mult)
            self.tt(w1g[:, HP * k:HP * (k + 1)], w1g[:, HP * k:HP * (k + 1)],
                    lngc[:, k:k + 1].broadcast_to([128, HP]), ALU.mult)
        xt = [self.sb("fb_x%d" % i, [128, D], F32, stage=True) for i in range(2)]
        xh = [self.sb("fb_xh%d" % j, [128, D], BF16, stage=True) for j in range(4)]
        xhfm = [self.sb("fb_xf%d" % k, [128, TT], BF16, stage=True) for k in range(8)]
        sqb = self.sb("fb_sq", [128, D], BF16, stage=True)
        gfb = self.sb("fb_gf", [128, TT], BF16, stage=True)
        h2 = [self.sb("fb_h2%d" % mi, [128, TT], BF16, stage=True) for mi in range(22)]
        ofm = [self.sb("fb_of%d" % m, [128, TT], BF16, stage=True) for m in range(8)]
        xut = [self.sb("fb_xu0", [128, D], F32, stage=True)]
        xot = [self.sb("fb_xo0", [128, D], F32, stage=True)]
        cols = self.sb("fb_c", [128, 8], F32, stage=True)
        pQ = [self.ps("fb_pq%d" % i, [128, 512], F32, stage=True) for i in range(2)]
        pT1 = self.ps("fb_pt1", [128, 1024], BF16, stage=True)
        pT2 = self.ps("fb_pt2", [128, 1024], BF16, stage=True)
        for it in range(self.NT):
            t0 = it * TT
            for j in range(4):
                r0 = t0 + 128 * j
                x = xt[j % 2]
                self.dma("s", x[:, 0:512], src_x[8 + r0:8 + r0 + 128, 0:512])
                self.dma("s", x[:, 512:1024], src_x[7 + r0:7 + r0 + 128, 512:1024])
                nm = cols[:, 0:1]
                v = cols[:, 1:2]
                rs = cols[:, 2:3]
                self.red(nm, x[:, :], ALU.add, negate=True)
                self.ts(nm, nm, 1.0 / D, ALU.mult)
                self.act(x[:, :], x[:, :], AF.Identity, bias=nm)
                self.act(sqb, x[:, :], AF.Square)
                self.red(v, sqb, ALU.add)
                self.ts(v, v, 1.0 / D, ALU.mult)
                self.ts(v, v, 1e-5, ALU.add)
                self.act(v, v, AF.Ln)
                self.act(rs, v, AF.Exp, scale=-0.5)
                self.tt(xh[j], x[:, :], rs.broadcast_to([128, D]), ALU.mult)
            for c in range(8):
                for j in range(4):
                    self.trans(pT1[:, 128 * j:128 * (j + 1)],
                               xh[j][:, 128 * c:128 * (c + 1)], self.ident_bf[:, :])
                self.cp(xhfm[c], pT1[:, 0:512])
            for mi in range(22):
                pa = pQ[0]
                pg = pQ[1]
                for k in range(8):
                    self.mm(pa, w1a[:, HP * k + 128 * mi:HP * k + 128 * (mi + 1)],
                            xhfm[k], k == 0, k == 7)
                for k in range(8):
                    self.mm(pg, w1g[:, HP * k + 128 * mi:HP * k + 128 * (mi + 1)],
                            xhfm[k], k == 0, k == 7)
                self.act(gfb, pg, AF.Gelu, bias=b1pg[:, mi:mi + 1])
                self.act(h2[mi], pa, AF.Identity, bias=b1pa[:, mi:mi + 1])
                self.tt(h2[mi], h2[mi], gfb, ALU.mult)
            for m in range(8):
                p = pQ[m % 2]
                for k in range(22):
                    self.mm(p, self.wch(w2, D, k, m * 128, (m + 1) * 128), h2[k],
                            k == 0, k == 21)
                self.act(ofm[m], p, AF.Identity, bias=b2c[:, m:m + 1])
            for j in range(4):
                r0 = t0 + 128 * j
                for m in range(8):
                    self.trans(pT2[:, 128 * m:128 * (m + 1)],
                               ofm[m][:, 128 * j:128 * (j + 1)], self.ident_bf[:, :])
                xu = xut[0]
                xo = xot[0]
                self.dma("s", xu, src_x[8 + r0:8 + r0 + 128, :])
                self.tt(xo, pT2[:, 0:1024], xu, ALU.add)
                self.dma("s", dst_x[8 + r0:8 + r0 + 128, :], xo)
        self.end_stage()

    def s_l1ffn(self):
        for li in range(6):
            self.s_l1ffn_a(li)
            self.s_l1ffn_b(li)

    def s_final(self):
        T = self.T
        X1 = self.dram["X1"].ap()
        S1 = self.dram["S1"].ap()
        XF = self.dram["XFA"].ap()
        OUT = self.dram["out"].ap()
        xt = [self.sb("fn_x%d" % i, [128, D], F32, stage=True) for i in range(2)]
        st = [self.sb("fn_s%d" % i, [128, D], BF16, stage=True) for i in range(2)]
        gt = [self.sb("fn_g%d" % i, [128, D], F32, stage=True) for i in range(2)]
        t1 = self.sb("fn_t", [128, D], F32, stage=True)
        ot = [self.sb("fn_o%d" % i, [128, D], F32, stage=True) for i in range(2)]
        for c in range(self.NC):
            r0 = c * 128
            x = xt[c % 2]
            s = st[c % 2]
            g = gt[c % 2]
            o = ot[c % 2]
            self.dma("s", x, X1[r0:r0 + 128, :])
            self.dma("s", s, S1[r0:r0 + 128, :])
            self.dma("s", g, XF[8 + r0:8 + r0 + 128, :])
            self.act(t1, g, AF.Exp, scale=-1.0)
            self.ts(t1, t1, 1.0, ALU.add)
            self.recip(t1, t1)
            self.tt(g, g, t1, ALU.mult)
            self.tt(t1, s, g, ALU.mult)
            self.tt(o, x, t1, ALU.add)
            self.dma("s", OUT[r0:r0 + 128, :], o)
        self.end_stage()


    def build(self):
        self.declare()
        for name, fn in self.STAGES:
            fn(self)
            if self.stop_after == name:
                break
        self.P.barrier()
        semnames = list(ENGINES) + ["d" + q for q in DMAQ]
        sems = {}
        for n in semnames:
            g = self.nc.semaphore("sem_" + n)
            sems[n] = g.__enter__()
            self._guards.append(g)
        nc = self.nc
        P = self.P
        with nc.Block() as block:
            block.tensor(lambda eo: P.replay(sems, eo, "t"))
            block.vector(lambda eo: P.replay(sems, eo, "v"))
            block.scalar(lambda eo: P.replay(sems, eo, "a"))
            block.gpsimd(lambda eo: P.replay(sems, eo, "p"))
            block.sync(lambda eo: P.replay(sems, eo, "s"))
        nc.finalize()
        return nc



Kern.STAGES = [("ln0", Kern.s_ln0), ("scores", Kern.s_scores), ("proj0", Kern.s_proj0), ("fir", Kern.s_fir), ("delta", Kern.s_delta), ("router", Kern.s_router), ("probs", Kern.s_probs), ("ctxwo", Kern.s_ctx_wo), ("shared", Kern.s_shared), ("moe", Kern.s_moe), ("combine0", Kern.s_combine0), ("l1proj", Kern.s_l1proj), ("gla", Kern.s_gla), ("wout", Kern.s_wout), ("l1f0a", lambda s: s.s_l1ffn_a(0)), ("l1f0b", lambda s: s.s_l1ffn_b(0)),
    ("l1f1a", lambda s: s.s_l1ffn_a(1)), ("l1f1b", lambda s: s.s_l1ffn_b(1)),
    ("l1f2a", lambda s: s.s_l1ffn_a(2)), ("l1f2b", lambda s: s.s_l1ffn_b(2)),
    ("l1f3a", lambda s: s.s_l1ffn_a(3)), ("l1f3b", lambda s: s.s_l1ffn_b(3)),
    ("l1f4a", lambda s: s.s_l1ffn_a(4)), ("l1f4b", lambda s: s.s_l1ffn_b(4)),
    ("l1f5a", lambda s: s.s_l1ffn_a(5)), ("l1f5b", lambda s: s.s_l1ffn_b(5)),
    ("final", Kern.s_final)]


def host_prep(inputs, T=8192):
    inp = {k: np.ascontiguousarray(np.asarray(v, np.float32))
           for k, v in inputs.items()}
    x = inp.pop("L0_attn_hidden_states")

    for nm in ("Wq", "Wk", "Wv", "Wout"):
        inp["L1_attn_" + nm + "T"] = np.ascontiguousarray(inp.pop("L1_attn_" + nm).T)
    wg = np.concatenate([inp.pop("L1_attn_Wbeta").T, inp.pop("L1_attn_Wig").T,
                         inp.pop("L1_attn_Wog").T], axis=1)
    inp["L1_attn_WgT"] = np.ascontiguousarray(wg)
    inp["L1_attn_gb"] = np.concatenate([inp.pop("L1_attn_bbeta"),
                                        inp.pop("L1_attn_big"),
                                        inp.pop("L1_attn_bog")])

    HI, HP = 2730, 2816
    W1 = inp.pop("L1_ffn_W1"); b1 = inp.pop("L1_ffn_b1"); W2 = inp.pop("L1_ffn_W2")
    W1p = np.zeros((6, 1024, 2 * HP), np.float32)
    W1p[:, :, :HI] = W1[:, :, :HI]
    W1p[:, :, HP:HP + HI] = W1[:, :, HI:]
    b1p = np.zeros((6, 2 * HP), np.float32)
    b1p[:, :HI] = b1[:, :HI]
    b1p[:, HP:HP + HI] = b1[:, HI:]
    W2p = np.zeros((6, HP, 1024), np.float32)
    W2p[:, :HI] = W2
    inp["L1_ffn_W1"], inp["L1_ffn_b1"], inp["L1_ffn_W2"] = W1p, b1p, W2p

    c = {}
    I = np.eye(128, dtype=np.float32)
    c["c_ident_bf"] = _bf(I)
    c["c_ident_f32"] = I.copy()
    c["c_ones_lhs_bf"] = _bf(np.ones((1, 128)))
    c["c_ones_col_bf"] = _bf(np.ones((128, 1)))
    c["c_cum_bf"] = _bf(np.triu(np.ones((128, 128))))
    c["c_mask_sl_neg"] = -np.tril(np.ones((128, 128), np.float32), -1)
    c["c_mask_su_neg"] = -np.triu(np.ones((128, 128), np.float32), 1)
    c["c_mask_ui"] = np.triu(np.ones((128, 128), np.float32))
    gm = np.triu(np.ones((128, 128), np.float32))
    c["c_gla_m"] = gm
    c["c_gla_b"] = ((1.0 - gm) * NEG).astype(np.float32)
    qi = np.arange(128)[:, None]
    kj = np.arange(384)[None, :] - 256
    m = (qi < kj) | (qi > kj + 256)
    c["c_lam_gen"] = np.where(m, NEG, 0.0).astype(np.float32)
    c["c_lam_gen_bf"] = _bf(c["c_lam_gen"])
    bj = np.arange(384)[None, :]
    c["c_lam_f0_bf"] = _bf(np.where(m | (bj < 256), NEG, 0.0))
    c["c_lam_f1_bf"] = _bf(np.where(m | (bj < 128), NEG, 0.0))
    ind2 = np.zeros((128, 2), np.float32)
    ind2[:64, 0] = 1.0
    ind2[64:, 1] = 1.0
    c["c_ind2_bf"] = _bf(ind2)
    c["c_ind2T_bf"] = _bf(np.ascontiguousarray(ind2.T))
    sw = np.zeros((128, 128), np.float32)
    for h in range(2):
        b0 = 64 * h
        for r in range(32):
            sw[b0 + 32 + r, b0 + r] = 1.0
            sw[b0 + r, b0 + 32 + r] = 1.0
    c["c_swap64_bf"] = _bf(sw)
    swh = np.zeros((128, 128), np.float32)
    for m in range(128):
        swh[(m + 64) % 128, m] = 1.0
    c["c_swap_h_bf"] = _bf(swh)
    c["c_ones2_bf"] = _bf(np.ones((2, 128)))
    c["c_ones64_bf"] = _bf(np.ones((1, 64)))
    sel = np.zeros((16, 8 * 128), np.float32)
    for cc in range(8):
        sel[2 * cc, cc * 128:cc * 128 + 64] = 1.0
        sel[2 * cc + 1, cc * 128 + 64:cc * 128 + 128] = 1.0
    c["c_sel16"] = _bf(sel)
    half = 64
    invf = 1.0 / 10000.0 ** (np.arange(half, dtype=np.float32) / half)
    fr = np.arange(T, dtype=np.float32)[:, None] * invf[None, :]
    cosr = np.cos(fr).T
    sinr = np.sin(fr).T
    c["c_rope1_cs2"] = np.ascontiguousarray(np.concatenate([cosr, cosr], 0))
    c["c_rope1_ss2"] = np.ascontiguousarray(np.concatenate([-sinr, sinr], 0))
    h2 = np.arange(0, 64, 2, dtype=np.float32)
    invf2 = 1.0 / 10000.0 ** (h2 / 64.0)
    t = np.arange(T, dtype=np.float32)
    fr2 = t[:, None] * invf2[None, :]
    fr2 = np.concatenate([fr2, fr2], -1)
    base = (h2 + 0.4 * 64) / (1.4 * 64)
    power = (t - T // 2) / 128.0
    sc = base[None, :] ** power[:, None]
    sc = np.concatenate([sc, sc], -1)
    sgn = np.ones(64, np.float32)
    sgn[:32] = -1.0
    c["c_xp_cs"] = np.ascontiguousarray((np.cos(fr2) * sc).T)
    c["c_xp_ssq"] = np.ascontiguousarray((np.sin(fr2) * sc).T * sgn[:, None])
    c["c_xp_ssk"] = np.ascontiguousarray((np.sin(fr2) / sc).T * sgn[:, None])
    c["c_xp_ci"] = np.ascontiguousarray((np.cos(fr2) / sc).T)
    for k in ("c_ident_bf", "c_ones_lhs_bf", "c_ones_col_bf", "c_cum_bf",
              "c_ind2_bf", "c_ind2T_bf", "c_swap64_bf"):
        pass

    maps = []
    for core in range(NCORES):
        mcore = {"x0": np.ascontiguousarray(x[core, :T])}
        mcore.update(inp)
        mcore.update(c)
        maps.append(mcore)
    return maps


def kernel(**inputs):
    T = 8192
    maps = host_prep(inputs, T)
    b = Kern(T=T)
    nc = b.build()
    names = {n for n, t in b.dram.items()}
    in_names = [n for n, t in b.dram.items()]
    core_maps = []
    for m in maps:
        core_maps.append({k: v for k, v in m.items() if k in names})
    res = run_bass_kernel_spmd(nc, core_maps, core_ids=list(range(NCORES)))
    out = np.stack([np.asarray(r["out"], np.float32) for r in res.results], axis=0)
    return out


# version 02 (final)
# speedup vs baseline: 1.1232x; 1.1232x over previous
import numpy as np
import ml_dtypes
import bass_rust
import concourse.bass as bass
import concourse.bacc as bacc
import concourse.mybir as mybir
from concourse.bass_utils import run_bass_kernel_spmd
from concourse.hw_specs import get_activation_tables


class TBacc(bacc.Bacc):

    def compile(self):
        orig = bass_rust.insert_act_table_loads
        tabs = None
        try:
            cat = get_activation_tables(self.m.arch)
            names = list(cat)
            if len(names) > 6 and names[6] == "natural_log_exp_and_others":
                tabs = {}
                for i, n in enumerate(names):
                    if i in (0, 1, 4, 5):
                        tabs[n] = set()
                    elif i == 2:
                        tabs[n] = {f for f in cat[n]
                                   if f == AF.Erf or f == AF.Sigmoid}
                    elif i == 3:
                        tabs[n] = {f for f in cat[n] if f == AF.Sqrt}
                    else:
                        tabs[n] = cat[n]
        except Exception:
            tabs = None
        if tabs is not None:
            def patched(b, tables=None):
                return orig(b, tabs)
            bass_rust.insert_act_table_loads = patched
        try:
            super().compile()
        finally:
            bass_rust.insert_act_table_loads = orig

F32 = mybir.dt.float32
BF16 = mybir.dt.bfloat16
I16 = mybir.dt.int16
I32 = mybir.dt.int32
AF = mybir.ActivationFunctionType
ALU = mybir.AluOpType
AX = mybir.AxisListType

NCORES = 8
D = 1024
TT = 512
CAP = 1536
NEG = -1.0e30
RSQRT_MAGIC = 0x5F3759DF

ENGINES = ("t", "v", "a", "p", "s")
DMAQ = {"s": "s", "a": "a", "p": "p", "w": "p"}


def _bf(x):
    return np.ascontiguousarray(np.asarray(x, np.float32).astype(ml_dtypes.bfloat16))


def _ap(x):
    if hasattr(x, "tensor") or not type(x).__name__.endswith("TensorHandle"):
        return x
    return x[:, :]


class Prog:
    def __init__(self, nc):
        self.nc = nc
        self.ops = {e: [] for e in ENGINES}
        self.cnt = {e: 0 for e in ENGINES}
        self.dcnt = {q: 0 for q in DMAQ}
        self.hw = {e: {} for e in ENGINES}
        self.state = {}

    def _st(self, key):
        s = self.state.get(key)
        if s is None:
            s = {"w": None, "r": {}}
            self.state[key] = s
        return s

    def _wait(self, eng, src, val, force=False):
        if val <= 0:
            return
        if src == eng and not force:
            return
        hwm = self.hw[eng]
        k = src if src != eng else "_self"
        if hwm.get(k, 0) >= val:
            return
        hwm[k] = val
        self.ops[eng].append(("w", src, val))

    UNORDERED = ("v", "p")

    def _val(self, src, val):
        if src.startswith("d"):
            return self.dcnt[src[1:]]
        return val

    def _dep_read(self, eng, ap, force=False):
        s = self.state.get(ap.tensor.name)
        if s and s["w"] is not None:
            f = force or (s["w"][0] == eng and eng in self.UNORDERED)
            self._wait(eng, s["w"][0], self._val(s["w"][0], s["w"][1]), force=f)

    def _dep_write(self, eng, ap):
        s = self.state.get(ap.tensor.name)
        if not s:
            return
        if s["w"] is not None:
            f = s["w"][0] == eng and eng in self.UNORDERED
            self._wait(eng, s["w"][0], self._val(s["w"][0], s["w"][1]), force=f)
        for src, val in s["r"].items():
            f = src == eng and eng in self.UNORDERED
            self._wait(eng, src, self._val(src, val), force=f)

    def op(self, eng, meth, args, kwargs=None, outs=(), ins=(), sins=(), qdma=False):
        args = tuple(_ap(a) for a in args)
        outs = tuple(_ap(a) for a in outs)
        ins = tuple(_ap(a) for a in ins)
        sins = tuple(_ap(a) for a in sins)
        psin = tuple(a for a in ins
                     if type(a.tensor).__name__.startswith("PSum"))
        ins = tuple(a for a in ins
                    if not type(a.tensor).__name__.startswith("PSum"))
        outs = outs + psin
        for a in ins:
            self._dep_read(eng, a)
        for a in sins:
            self._dep_read(eng, a, force=True)
        for a in outs:
            self._dep_write(eng, a)
        if qdma:
            assert eng == "p"
            self.dcnt["p"] += 16
            val = self.dcnt["p"]
            src = "dp"
            self.ops[eng].append(("g", meth, args, kwargs or {}))
        else:
            self.cnt[eng] += 1
            val = self.cnt[eng]
            src = eng
            self.ops[eng].append(("c", meth, args, kwargs or {}))
        def _key(a):
            return a.tensor.name
        for a in ins:
            st = self._st(_key(a))
            if st["r"].get(src, 0) < val:
                st["r"][src] = val
        for a in sins:
            st = self._st(_key(a))
            if st["r"].get(src, 0) < val:
                st["r"][src] = val
        for a in outs:
            st = self._st(_key(a))
            st["w"] = (src, val)
            st["r"] = {}
        return val

    def dma(self, q, out, in_, **kw):
        out = _ap(out)
        in_ = _ap(in_)
        eng = DMAQ[q]
        self._dep_read(eng, in_)
        self._dep_write(eng, out)
        self.dcnt[q] += 16
        val = self.dcnt[q]
        self.ops[eng].append(("d", q, out, in_, kw))
        src = "d" + q
        st = self._st(in_.tensor.name)
        if st["r"].get(src, 0) < val:
            st["r"][src] = val
        st2 = self._st(out.tensor.name)
        st2["w"] = (src, val)
        st2["r"] = {}
        return val

    def barrier(self, qs=("s", "a", "p", "w")):
        snap = dict(self.cnt)
        dsnap = {q: self.dcnt[q] for q in qs}
        for e in ENGINES:
            for e2 in ENGINES:
                if e2 != e and snap[e2] > 0:
                    self._wait(e, e2, snap[e2])
            for q in qs:
                if dsnap[q] > 0:
                    self._wait(e, "d" + q, dsnap[q])
        self.state = {}

    def replay(self, sems, eo, eng):
        mysem = sems[eng]
        for rec in self.ops[eng]:
            kind = rec[0]
            if kind == "w":
                _, src, val = rec
                eo.wait_ge(sems[src] if src != eng else mysem, val)
            elif kind == "c":
                _, meth, args, kwargs = rec
                getattr(eo, meth)(*args, **kwargs).then_inc(mysem, 1)
            elif kind == "g":
                _, meth, args, kwargs = rec
                getattr(eo, meth)(*args, **kwargs).then_inc(sems["dp"], 16)
            else:
                _, q, out, in_, kw = rec
                eo.dma_start(out=out, in_=in_, **kw).then_inc(sems["d" + q], 16)


class Builder:
    def __init__(self, T=8192, stop_after=None, dumps=()):
        self.T = T
        self.NT = T // TT
        self.NC = T // 128
        self.stop_after = stop_after
        self.dumps = set(dumps)
        self.nc = TBacc("TRN2", target_bir_lowering=False,
                        detect_race_conditions=False)
        self.P = Prog(self.nc)
        self._guards = []
        self._stage_guards = []
        self.dram = {}
        self.sbuf = {}
        self.uid = 0

    def din(self, name, shape, dtype=F32):
        t = self.nc.dram_tensor(name, list(shape), dtype, kind="ExternalInput")
        self.dram[name] = t
        return t

    def dscr(self, name, shape, dtype=F32):
        kind = "ExternalOutput" if name in self.dumps else "Internal"
        t = self.nc.dram_tensor(name, list(shape), dtype, kind=kind)
        self.dram[name] = t
        return t

    def dout(self, name, shape, dtype=F32):
        t = self.nc.dram_tensor(name, list(shape), dtype, kind="ExternalOutput")
        self.dram[name] = t
        return t

    def sb(self, name, shape, dtype=F32, stage=False):
        self.uid += 1
        name = "%s_u%d" % (name, self.uid)
        g = self.nc.sbuf_tensor(name, list(shape), dtype)
        h = g.__enter__()
        (self._stage_guards if stage else self._guards).append(g)
        self.sbuf[name] = h
        return h

    def ps(self, name, shape, dtype=F32, stage=False):
        self.uid += 1
        name = "%s_u%d" % (name, self.uid)
        g = self.nc.psum_tensor(name, list(shape), dtype)
        h = g.__enter__()
        (self._stage_guards if stage else self._guards).append(g)
        return h

    def end_stage(self):
        self.P.barrier()
        for g in reversed(self._stage_guards):
            g.__exit__(None, None, None)
        self._stage_guards = []

    def mm(self, psum, lhsT, rhs, start, stop):
        self.P.op("t", "matmul", (psum, lhsT, rhs),
                  {"start": start, "stop": stop}, outs=(psum,), ins=(lhsT, rhs))

    def trans(self, psum, in_, ident):
        self.P.op("t", "transpose", (psum, in_, ident), outs=(psum,), ins=(in_, ident))

    def tt(self, out, a, b, alu, eng="v"):
        self.P.op(eng, "tensor_tensor", (out, a, b, alu), outs=(out,), ins=(a, b))

    def ts(self, out, a, s1, alu, eng="v"):
        self.P.op(eng, "tensor_single_scalar", (out, a, float(s1), alu),
                  outs=(out,), ins=(a,))

    def ts2(self, out, a, s1, s2, op0, op1, eng="v"):
        self.P.op(eng, "tensor_scalar", (out, a, float(s1), float(s2), op0, op1),
                  outs=(out,), ins=(a,))

    def stt(self, out, in0, scal, in1, op0, op1, eng="v"):
        self.P.op(eng, "scalar_tensor_tensor", (out, in0, scal, in1, op0, op1),
                  outs=(out,), ins=(in0, in1), sins=(scal,))

    def cp(self, out, in_, eng="v"):
        self.P.op(eng, "tensor_copy", (out, in_), outs=(out,), ins=(in_,))

    def act(self, out, in_, func, bias=0.0, scale=1.0, eng="a"):
        sins = tuple(x for x in (bias, scale) if not isinstance(x, (int, float)))
        self.P.op(eng, "activation", (out, in_, func),
                  {"bias": bias, "scale": scale}, outs=(out,), ins=(in_,), sins=sins)

    def red(self, out, in_, alu, eng="v", negate=False):
        self.P.op(eng, "tensor_reduce", (out, in_, AX.X, alu),
                  {"negate": negate}, outs=(out,), ins=(in_,))

    def memset(self, ap, v, eng="v"):
        self.P.op(eng, "memset", (ap, float(v)), outs=(ap,))

    def recip(self, out, in_):
        self.P.op("v", "reciprocal", (out, in_), outs=(out,), ins=(in_,))

    def scan(self, out, d0, d1, init, op0, op1):
        sins = () if isinstance(init, (int, float)) else (init,)
        self.P.op("v", "tensor_tensor_scan", (out, d0, d1, init, op0, op1),
                  outs=(out,), ins=(d0, d1), sins=sins)

    def iota(self, out, pattern, base=0, chmul=0):
        self.P.op("p", "iota", (out,),
                  {"pattern": pattern, "base": base, "channel_multiplier": chmul,
                   "allow_small_or_imprecise_dtypes": True}, outs=(out,))

    def dma(self, q, out, in_, **kw):
        return self.P.dma(q, out, in_, **kw)

    def rsqrt_col(self, out, v, tmp, tmp2, eps=0.0, iters=3):
        self.ts(out, v, eps, ALU.add)
        self.act(out, out, AF.Ln)
        self.act(out, out, AF.Exp, scale=-0.5)

    def load_w_bf(self, name, src, K, M, q="w", rowscale=None):
        nk = K // 128
        wsb = self.sb(name, [128, nk * M], BF16, stage=True)
        for k in range(nk):
            self.dma("w", wsb[:, k * M:(k + 1) * M], src[k * 128:(k + 1) * 128, :])
        if rowscale is not None:
            for k in range(nk):
                self.tt(wsb[:, k * M:(k + 1) * M], wsb[:, k * M:(k + 1) * M],
                        rowscale(k).broadcast_to([128, M]), ALU.mult)
        return wsb

    def wch(self, wsb, M, k, m0, m1):
        return wsb[:, k * M + m0: k * M + m1]


class Kern(Builder):
    def declare(self):
        T = self.T
        self.din("x0", [T, D], F32)
        names = [
            ("L0_attn_Wq", [D, D]), ("L0_attn_Wk", [D, D]), ("L0_attn_Wv", [D, D]),
            ("L0_attn_q_conv_w", [D, 4]), ("L0_attn_k_conv_w", [D, 4]),
            ("L0_attn_v_conv_w", [D, 4]), ("L0_attn_Wb", [D, 4]),
            ("L0_attn_fir_w3", [4, 256, 3]), ("L0_attn_fir_w7", [4, 256, 7]),
            ("L0_attn_fir_w15", [4, 256, 15]), ("L0_attn_fir_w31", [4, 256, 31]),
            ("L0_attn_router_W1", [6 * D, 2 * D]), ("L0_attn_router_b1", [2 * D]),
            ("L0_attn_router_W2", [2 * D, 20]), ("L0_attn_router_b2", [20]),
            ("L0_attn_id_gate_W", [D, 4]), ("L0_attn_id_gate_b", [4]),
            ("L0_attn_log_tau", [4]), ("L0_attn_o_norm_w", [256]),
            ("L0_attn_Wo", [D, D]),
            ("L0_ffn_Wg", [D, 32]),
            ("L0_ffn_W1", [32, D, 512]), ("L0_ffn_W2", [32, 512, D]),
            ("L0_ffn_W3", [32, D, 512]),
            ("L0_ffn_Ws1", [D, D]), ("L0_ffn_Ws2", [D, D]), ("L0_ffn_Ws3", [D, D]),
            ("L1_attn_norm_w", [D]),
            ("L1_attn_WqT", [D, D]), ("L1_attn_WkT", [D, D]), ("L1_attn_WvT", [D, D]),
            ("L1_attn_WgT", [D, 24]),
            ("L1_attn_gb", [24]),
            ("L1_attn_WoutT", [D, D]), ("L1_attn_bout", [D]),
            ("L1_ffn_Wqkv", [6, D, 3 * D]),
            ("L1_ffn_q_scale", [6, 64]), ("L1_ffn_k_scale", [6, 64]),
            ("L1_ffn_Wo", [6, D, D]),
            ("L1_ffn_ln_g", [6, D]), ("L1_ffn_ln_b", [6, D]),
            ("L1_ffn_W1", [6, D, 5632]), ("L1_ffn_b1", [6, 5632]),
            ("L1_ffn_W2", [6, 2816, D]), ("L1_ffn_b2", [6, D]),
        ]
        for n, s in names:
            self.din(n, s, F32)
        consts = [
            ("c_ident_bf", [128, 128], BF16), ("c_ident_f32", [128, 128], F32),
            ("c_ones_lhs_bf", [1, 128], BF16), ("c_ones_col_bf", [128, 1], BF16),
            ("c_cum_bf", [128, 128], BF16),
            ("c_mask_sl_neg", [128, 128], F32), ("c_mask_su_neg", [128, 128], F32),
            ("c_mask_ui", [128, 128], F32),
            ("c_gla_m", [128, 128], F32), ("c_gla_b", [128, 128], F32),
            ("c_lam_gen", [128, 384], F32),
            ("c_ind2_bf", [128, 2], BF16), ("c_ind2T_bf", [2, 128], BF16),
            ("c_swap64_bf", [128, 128], BF16), ("c_swap_h_bf", [128, 128], BF16),
            ("c_ones2_bf", [2, 128], BF16), ("c_ones64_bf", [1, 64], BF16),
            ("c_sel16", [16, 8 * 128], BF16),
            ("c_lam_gen_bf", [128, 384], BF16), ("c_lam_f0_bf", [128, 384], BF16),
            ("c_lam_f1_bf", [128, 384], BF16),
            ("c_rope1_cs2", [128, self.T], F32), ("c_rope1_ss2", [128, self.T], F32),
            ("c_xp_cs", [64, self.T], F32), ("c_xp_ssq", [64, self.T], F32),
            ("c_xp_ssk", [64, self.T], F32),
            ("c_xp_ci", [64, self.T], F32),
        ]
        for n, s, dt in consts:
            self.din(n, s, dt)
        T1 = T + 1
        self.dscr("XN0F", [D, T], BF16)
        self.dscr("XNAUG", [T1, 1024], BF16)
        self.dscr("WSEL", [T1, 64], F32)
        self.dscr("BETA", [T, 8], F32)
        self.dscr("CM", [32, CAP], I16)
        self.dscr("QF", [D, T], BF16)
        self.dscr("KF", [D, T], BF16)
        self.dscr("VF", [D, T + 32], BF16)
        self.dscr("P3", [D, T], BF16)
        self.dscr("P7", [D, T], BF16)
        self.dscr("P15", [D, T], BF16)
        self.dscr("P31", [D, T], BF16)
        self.dscr("DOF", [D, T], BF16)
        self.dscr("PRF", [32, T], F32)
        self.dscr("HMIDF", [2 * D, T], BF16)
        self.dscr("S0", [T, D], BF16)
        for i in range(4):
            self.dscr("Y%d" % i, [T1, D], F32)
        self.dscr("X1", [T, D], F32)
        self.dscr("XN1F", [D, T], BF16)
        self.dscr("HN1F", [D, T], BF16)
        self.dscr("XFA", [T + 8, D], F32)
        self.dscr("XFB", [T + 8, D], F32)
        self.dscr("QF1", [D, T], BF16)
        self.dscr("KF1", [D, T], BF16)
        self.dscr("VF1", [D, T], BF16)
        self.dscr("GL1", [T, 24], F32)
        self.dscr("Y1F", [D, T], BF16)
        self.dscr("S1", [T, D], BF16)
        self.dout("out", [T, D], F32)

        self.ident_bf = self.sb("s_ident_bf", [128, 128], BF16)
        self.ident_f32 = self.sb("s_ident_f32", [128, 128], F32)
        self.ones_lhs = self.sb("s_ones_lhs", [1, 128], BF16)
        self.ones_col = self.sb("s_ones_col", [128, 1], BF16)
        self.dma("s", self.ident_bf[:, :], self.dram["c_ident_bf"].ap())
        self.dma("s", self.ident_f32[:, :], self.dram["c_ident_f32"].ap())
        self.dma("s", self.ones_lhs[:, :], self.dram["c_ones_lhs_bf"].ap())
        self.dma("s", self.ones_col[:, :], self.dram["c_ones_col_bf"].ap())
        self.onef = self.sb("s_onef", [128, 1], F32)
        self.memset(self.onef[:, :], 1.0)
        self.e24col = self.sb("s_e24col", [128, 1], F32)
        self.memset(self.e24col[:, :], 1e-24)
        self.zrow = self.sb("s_zrow", [128, 1152], BF16)
        self.memset(self.zrow[:, :], 0.0)
        self.dma("s", self.dram["XNAUG"].ap()[self.T:self.T + 1, :],
                 self.zrow[0:1, 0:1024])
        zf64 = self.sb("s_zf64", [1, 64], F32)
        self.memset(zf64[:, :], 0.0)
        self.dma("s", self.dram["WSEL"].ap()[self.T:self.T + 1, :], zf64[:, :])
        zf = self.sb("s_zf32", [128, 64], F32)
        self.memset(zf[:, :], 0.0)
        for c in range(8):
            self.dma("s", self.dram["VF"].ap()[128 * c:128 * (c + 1), 0:32],
                     self.zrow[0:128, 0:32])
        self.dma("s", self.dram["XFA"].ap()[0:8, :].rearrange("a b -> (a b)").rearrange("(a b) -> a b", a=128),
                 zf[:, :])
        self.dma("s", self.dram["XFB"].ap()[0:8, :].rearrange("a b -> (a b)").rearrange("(a b) -> a b", a=128),
                 zf[:, :])

    def s_ln0(self):
        x0 = self.dram["x0"].ap()
        xnaug = self.dram["XNAUG"].ap()
        xn0f = self.dram["XN0F"].ap()
        xt = [self.sb("ln_x%d" % i, [128, D], F32, stage=True) for i in range(4)]
        xnb = [self.sb("ln_xn%d" % i, [128, D], BF16, stage=True) for i in range(4)]
        cols = self.sb("ln_cols", [128, 8], F32, stage=True)
        pbf = [self.ps("ln_ps%d" % i, [128, 512], BF16, stage=True) for i in range(2)]
        ev = [self.sb("ln_ev%d" % i, [128, 512], BF16, stage=True) for i in range(2)]
        for it in range(self.NT):
            t0 = it * TT
            for j in range(4):
                r0 = t0 + 128 * j
                self.dma("s", xt[j][:, :], x0[r0:r0 + 128, :])
                nm = cols[:, 0:1]
                v = cols[:, 1:2]
                rs = cols[:, 2:3]
                tp1 = cols[:, 3:4]
                tp2 = cols[:, 4:5]
                self.red(nm, xt[j][:, :], ALU.add, negate=True)
                self.ts(nm, nm, 1.0 / D, ALU.mult)
                self.act(xt[j][:, :], xt[j][:, :], AF.Identity, bias=nm)
                self.act(xnb[j][:, :], xt[j][:, :], AF.Square)
                self.red(v, xnb[j][:, :], ALU.add)
                self.ts(v, v, 1.0 / D, ALU.mult)
                self.rsqrt_col(rs, v, tp1, tp2, eps=1e-5)
                self.tt(xnb[j][:, :], xt[j][:, :], rs.broadcast_to([128, D]), ALU.mult)
                self.dma("s", xnaug[r0:r0 + 128, 0:D], xnb[j][:, :])
            for c in range(8):
                pb = pbf[c % 2]
                e = ev[c % 2]
                for j in range(4):
                    self.trans(pb[:, 128 * j:128 * (j + 1)],
                               xnb[j][:, 128 * c:128 * (c + 1)], self.ident_bf[:, :])
                self.cp(e[:, :], pb[:, :])
                self.dma("s", xn0f[128 * c:128 * (c + 1), t0:t0 + TT], e[:, :])
        self.end_stage()

    def s_scores(self):
        T = self.T
        xn0f = self.dram["XN0F"].ap()
        xnaug = self.dram["XNAUG"].ap()
        beta_d = self.dram["BETA"].ap()
        wgbp = self.sb("sc_w", [128, 8 * 40], BF16, stage=True)
        for k in range(8):
            r = slice(128 * k, 128 * (k + 1))
            self.dma("w", wgbp[:, k * 40:k * 40 + 32], self.dram["L0_ffn_Wg"].ap()[r, :])
            self.dma("w", wgbp[:, k * 40 + 32:k * 40 + 36], self.dram["L0_attn_Wb"].ap()[r, :])
            self.dma("w", wgbp[:, k * 40 + 36:k * 40 + 40], self.dram["L0_attn_id_gate_W"].ap()[r, :])
        brow = self.sb("sc_b", [1, 8], BF16, stage=True)
        self.memset(brow[:, :], 0.0)
        self.dma("w", brow[0:1, 4:8], self.dram["L0_attn_id_gate_b"].ap().rearrange("(o a) -> o a", o=1))
        slabs = [self.sb("sc_s%d" % k, [128, TT], BF16, stage=True) for k in range(8)]
        ps40 = [self.ps("sc_p%d" % i, [128, 512], F32, stage=True) for i in range(2)]
        pmi = self.ps("sc_pmi", [128, 512], F32, stage=True)
        mif = self.sb("sc_mif", [32, T], F32, stage=True)
        sc = self.sb("sc_sc", [128, 40], F32, stage=True)
        sc2 = self.sb("sc_sc2", [128, 32], F32, stage=True)
        wtf = self.sb("sc_wtf", [128, 32], F32, stage=True)
        wst = self.sb("sc_wst", [128, 64], F32, stage=True)
        self.memset(wst[:, 32:64], 0.0)
        cols = self.sb("sc_cols", [128, 8], F32, stage=True)
        for it in range(self.NT):
            t0 = it * TT
            for k in range(8):
                self.dma("s", slabs[k][:, :], xn0f[128 * k:128 * (k + 1), t0:t0 + TT])
            for j in range(4):
                p = ps40[j % 2][:, 0:40]
                for k in range(8):
                    self.mm(p, slabs[k][:, 128 * j:128 * (j + 1)],
                            wgbp[:, k * 40:(k + 1) * 40], k == 0, False)
                self.mm(ps40[j % 2][:, 32:40], self.ones_lhs[:, :], brow[:, :], False, True)
                nmx = cols[:, 0:1]
                ssum = cols[:, 1:2]
                rcp = cols[:, 2:3]
                thr = cols[:, 3:4]
                tcol = cols[:, 4:5]
                self.red(nmx, p[:, 0:32], ALU.max, negate=True)
                self.act(sc[:, 0:32], p[:, 0:32], AF.Exp, bias=nmx)
                self.red(ssum, sc[:, 0:32], ALU.add)
                self.recip(rcp, ssum)
                self.tt(sc[:, 0:32], sc[:, 0:32], rcp.broadcast_to([128, 32]), ALU.mult)
                self.act(sc[:, 32:40], p[:, 32:40], AF.Exp, scale=-1.0)
                self.ts(sc[:, 32:40], sc[:, 32:40], 1.0, ALU.add)
                self.recip(sc[:, 32:40], sc[:, 32:40])
                self.dma("s", beta_d[t0 + 128 * j:t0 + 128 * (j + 1), :], sc[:, 32:40])
                self.cp(sc2[:, :], sc[:, 0:32])
                for i in range(3):
                    mx = cols[:, 5:6]
                    self.red(mx, sc2[:, :], ALU.max)
                    eq = cols[:, 6:7]
                    self.tt(wtf[:, :], sc2[:, :], mx.broadcast_to([128, 32]), ALU.is_ge)
                    self.ts(wtf[:, :], wtf[:, :], NEG, ALU.mult)
                    self.tt(sc2[:, :], sc2[:, :], wtf[:, :], ALU.add)
                self.red(thr, sc2[:, :], ALU.max)
                ge = sc2
                self.tt(ge[:, :], sc[:, 0:32], thr.broadcast_to([128, 32]), ALU.is_ge)
                self.tt(wst[:, 0:32], sc[:, 0:32], ge[:, :], ALU.mult)
                r0 = t0 + 128 * j
                self.dma("s", self.dram["WSEL"].ap()[r0:r0 + 128, :], wst[:, :])
                self.iota(tcol, [[0, 1]], base=r0, chmul=1)
                self.tt(wtf[:, :], ge[:, :], tcol.broadcast_to([128, 32]), ALU.mult)
                self.ts(ge[:, :], ge[:, :], 1.0, ALU.subtract)
                self.tt(wtf[:, :], wtf[:, :], ge[:, :], ALU.add)
                self.trans(pmi[0:32, 128 * j:128 * (j + 1)], wtf[:, :], self.ident_f32[:, :])
            self.cp(mif[:, t0:t0 + TT], pmi[0:32, :])
        msk = self.sb("sc_msk", [32, T], F32, stage=True)
        rnk = self.sb("sc_rnk", [32, T], F32, stage=True)
        s16 = self.sb("sc_s16", [32, T], I16, stage=True)
        d16 = self.sb("sc_d16", [32, T], I16, stage=True)
        cm16 = self.sb("sc_cm", [32, CAP], I16, stage=True)
        cmf = self.sb("sc_cmf", [32, CAP], F32, stage=True)
        self.ts(msk[:, :], mif[:, :], 0.0, ALU.is_ge)
        self.scan(rnk[:, :], msk[:, :], msk[:, :], 0.0, ALU.add, ALU.bypass)
        self.tt(rnk[:, :], rnk[:, :], msk[:, :], ALU.subtract)
        self.ts(mif[:, :], rnk[:, :], float(CAP), ALU.is_lt)
        self.tt(msk[:, :], msk[:, :], mif[:, :], ALU.mult)
        self.tt(rnk[:, :], rnk[:, :], msk[:, :], ALU.mult)
        self.ts(msk[:, :], msk[:, :], 1.0, ALU.subtract)
        self.tt(rnk[:, :], rnk[:, :], msk[:, :], ALU.add)
        self.cp(s16[:, :], rnk[:, :], eng="p")
        self.iota(d16[:, :], [[1, T]], base=-T, chmul=0)
        self.P.op("p", "local_scatter", (cm16[:, :], d16[:, :], s16[:, :], 32, CAP, T),
                  outs=(cm16[:, :],), ins=(d16[:, :], s16[:, :]))
        self.cp(cmf[:, :], cm16[:, :], eng="p")
        self.ts(cmf[:, :], cmf[:, :], float(T), ALU.add)
        self.cp(cm16[:, :], cmf[:, :], eng="p")
        self.dma("s", self.dram["CM"].ap()[:, :], cm16[:, :])
        self.end_stage()


    def s_proj0(self):
        T = self.T
        xn0f = self.dram["XN0F"].ap()
        wq = self.load_w_bf("pw_q", self.dram["L0_attn_Wq"].ap(), D, D)
        wk = self.load_w_bf("pw_k", self.dram["L0_attn_Wk"].ap(), D, D)
        wv = self.load_w_bf("pw_v", self.dram["L0_attn_Wv"].ap(), D, D)
        wmap = {"q": wq, "k": wk, "v": wv}
        dst = {"q": self.dram["QF"].ap(), "k": self.dram["KF"].ap(),
               "v": self.dram["VF"].ap()}
        voff = {"q": 0, "k": 0, "v": 32}
        cw = self.sb("p0_cw", [128, 32], F32, stage=True)
        for k in range(8):
            for wi, nm in enumerate(("q", "k", "v")):
                pass
        srcs = {"q": self.dram["L0_attn_q_conv_w"].ap(),
                "k": self.dram["L0_attn_k_conv_w"].ap(),
                "v": self.dram["L0_attn_v_conv_w"].ap()}
        cwb = {}
        for nm in ("q", "k", "v"):
            cwb[nm] = self.sb("p0_cw_" + nm, [128, 32], F32, stage=True)
            for k in range(8):
                self.dma("w", cwb[nm][:, 4 * k:4 * (k + 1)],
                         srcs[nm][128 * k:128 * (k + 1), :])
        diag = self.sb("p0_diag", [128, 96 * 128], BF16, stage=True)
        di = {}
        n = 0
        for nm in ("q", "k", "v"):
            for m in range(8):
                for s in range(4):
                    d = diag[:, n * 128:(n + 1) * 128]
                    di[(nm, m, s)] = d
                    col = cwb[nm][:, 4 * m + (3 - s):4 * m + (4 - s)]
                    self.tt(d, self.ident_bf[:, :], col.broadcast_to([128, 128]), ALU.mult)
                    n += 1
        halo = self.sb("p0_halo", [128, 24 * 3], BF16, stage=True)
        self.memset(halo[:, :], 0.0)
        slabs = [self.sb("p0_s%d" % k, [128, TT], BF16, stage=True) for k in range(8)]
        conv = [self.sb("p0_c%d" % i, [128, 515], BF16, stage=True) for i in range(3)]
        outb = [self.sb("p0_o%d" % i, [128, TT], BF16, stage=True) for i in range(3)]
        pp = [self.ps("p0_p%d" % i, [128, 512], F32, stage=True) for i in range(4)]
        for it in range(self.NT):
            t0 = it * TT
            for k in range(8):
                self.dma("s", slabs[k][:, :], xn0f[128 * k:128 * (k + 1), t0:t0 + TT])
            for wi, nm in enumerate(("q", "k", "v")):
                w = wmap[nm]
                for m in range(8):
                    p = pp[(wi * 8 + m) % 2]
                    for k in range(8):
                        self.mm(p, self.wch(w, D, k, m * 128, (m + 1) * 128),
                                slabs[k][:, :], k == 0, k == 7)
                    cb = conv[(wi * 8 + m) % 3]
                    hslice = halo[:, (wi * 8 + m) * 3:(wi * 8 + m) * 3 + 3]
                    self.cp(cb[:, 0:3], hslice)
                    self.cp(cb[:, 3:515], p[:, :])
                    self.cp(hslice, cb[:, 512:515])
                    p2 = pp[2 + (wi * 8 + m) % 2]
                    for s in range(4):
                        self.mm(p2, di[(nm, m, s)], cb[:, 3 - s:515 - s], s == 0, s == 3)
                    ob = outb[(wi * 8 + m) % 3]
                    self.act(ob[:, :], p2[:, :], AF.Silu)
                    self.dma("s", dst[nm][128 * m:128 * (m + 1),
                                          voff[nm] + t0:voff[nm] + t0 + TT], ob[:, :])
        self.end_stage()

    def s_fir(self):
        T = self.T
        vf = self.dram["VF"].ap()
        taps = [(3, "L0_attn_fir_w3", "P3"), (7, "L0_attn_fir_w7", "P7"),
                (15, "L0_attn_fir_w15", "P15"), (31, "L0_attn_fir_w31", "P31")]
        fw = self.sb("f_w", [128, 56], F32, stage=True)
        diag = self.sb("f_diag", [128, 56 * 128], BF16, stage=True)
        vs = [self.sb("f_v%d" % i, [128, 543], BF16, stage=True) for i in range(2)]
        ob = [self.sb("f_o%d" % i, [128, TT], BF16, stage=True) for i in range(4)]
        pp = [self.ps("f_p%d" % i, [128, 512], F32, stage=True) for i in range(4)]
        for m in range(8):
            h, dv0 = m // 2, (m % 2) * 128
            off = 0
            dmap = {}
            for K, wname, _ in taps:
                self.dma("w", fw[:, off:off + K],
                         self.dram[wname].ap()[h, dv0:dv0 + 128, :])
                for s in range(K):
                    d = diag[:, (off + s) * 128:(off + s + 1) * 128]
                    dmap[(K, s)] = d
                    col = fw[:, off + (K - 1 - s):off + (K - s)]
                    self.tt(d, self.ident_bf[:, :], col.broadcast_to([128, 128]), ALU.mult)
                off += K
            for it in range(self.NT):
                t0 = it * TT
                v = vs[it % 2]
                self.dma("s", v[:, :], vf[128 * m:128 * (m + 1), t0 + 1:t0 + 544])
                for pi, (K, _, oname) in enumerate(taps):
                    p = pp[pi]
                    for s in range(K):
                        self.mm(p, dmap[(K, s)], v[:, 31 - s:543 - s], s == 0, s == K - 1)
                    self.cp(ob[pi][:, :], p[:, :])
                    self.dma("s", self.dram[oname].ap()[128 * m:128 * (m + 1), t0:t0 + TT],
                             ob[pi][:, :])
        self.end_stage()


    def s_delta(self):
        T = self.T
        QF, KF, VF = (self.dram[n].ap() for n in ("QF", "KF", "VF"))
        DOF = self.dram["DOF"].ap()
        beta_d = self.dram["BETA"].ap()
        S_sb = [self.sb("d_S%d" % h, [128, 512], F32, stage=True) for h in range(4)]
        S_bf = [self.sb("d_Sb%d" % h, [128, 512], BF16, stage=True) for h in range(4)]
        for h in range(4):
            self.memset(S_sb[h][:, :], 0.0)
            self.memset(S_bf[h][:, :], 0.0)
        qs = [self.sb("d_q%d" % k, [128, 128], BF16, stage=True) for k in range(8)]
        ks = [self.sb("d_k%d" % k, [128, 128], BF16, stage=True) for k in range(8)]
        vs = [self.sb("d_v%d" % k, [128, 128], BF16, stage=True) for k in range(8)]
        bt = self.sb("d_bt", [128, 4], F32, stage=True)
        row4 = self.sb("d_row4", [1, 512], BF16, stage=True)
        rowf = self.sb("d_rowf", [1, 128], F32, stage=True)
        rowb = self.sb("d_rowb", [1, 128], BF16, stage=True)
        sqb = [self.sb("d_sq%d" % j, [128, 128], BF16, stage=True) for j in range(2)]
        qn = [self.sb("d_qn%d" % j, [128, 128], BF16, stage=True) for j in range(2)]
        kn = [self.sb("d_kn%d" % j, [128, 128], BF16, stage=True) for j in range(2)]
        kb = [self.sb("d_kb%d" % j, [128, 128], BF16, stage=True) for j in range(2)]
        vb = [self.sb("d_vb%d" % j, [128, 128], BF16, stage=True) for j in range(2)]
        npw = [self.sb("d_np%d" % i, [128, 128], BF16, stage=True) for i in range(7)]
        npwT = [self.sb("d_npT%d" % i, [128, 128], BF16, stage=True) for i in range(7)]
        cur = [self.sb("d_cur%d" % i, [128, 512], BF16, stage=True) for i in range(2)]
        uadj = self.sb("d_uadj", [128, 256], BF16, stage=True)
        attT = self.sb("d_attT", [128, 128], BF16, stage=True)
        wfm = [self.sb("d_wfm%d" % r, [128, 128], BF16, stage=True) for r in range(2)]
        kntm = [self.sb("d_kntm%d" % r, [128, 128], BF16, stage=True) for r in range(2)]
        obf = [self.sb("d_o%d" % r, [128, 128], BF16, stage=True) for r in range(2)]
        pA = self.ps("d_pA", [128, 512], F32, stage=True)
        pB = self.ps("d_pB", [128, 512], F32, stage=True)
        pApp = self.ps("d_pApp", [128, 512], F32, stage=True)
        pU = self.ps("d_pU", [128, 512], F32, stage=True)
        pO = self.ps("d_pO", [128, 512], F32, stage=True)
        pR = self.ps("d_pR", [128, 512], F32, stage=True)
        pT1 = self.ps("d_pT1", [128, 1024], BF16, stage=True)
        pT2 = self.ps("d_pT2", [128, 1024], BF16, stage=True)
        msl = self.sb("d_msl", [128, 128], F32, stage=True)
        msu = self.sb("d_msu", [128, 128], F32, stage=True)
        mui = self.sb("d_mui", [128, 128], F32, stage=True)
        self.dma("s", msl, self.dram["c_mask_sl_neg"].ap())
        self.dma("s", msu, self.dram["c_mask_su_neg"].ap())
        self.dma("s", mui, self.dram["c_mask_ui"].ap())
        for c in range(self.NC):
            tc = c * 128
            for k in range(8):
                self.dma("s", qs[k], QF[128 * k:128 * (k + 1), tc:tc + 128])
                self.dma("s", ks[k], KF[128 * k:128 * (k + 1), tc:tc + 128])
                self.dma("s", vs[k], VF[128 * k:128 * (k + 1), 32 + tc:32 + tc + 128])
            self.dma("s", bt, beta_d[tc:tc + 128, 0:4])
            for h in range(4):
                self.trans(pR[0:1, 128 * h:128 * (h + 1)], bt[:, h:h + 1],
                           self.ident_f32[:, :])
            self.cp(row4[:, :], pR[0:1, 0:512])
            for h in range(4):
                q2 = qs[2 * h:2 * h + 2]
                k2 = ks[2 * h:2 * h + 2]
                v2 = vs[2 * h:2 * h + 2]
                for (src2, dst2) in ((q2, qn), (k2, kn)):
                    for j in range(2):
                        self.act(sqb[j], src2[j], AF.Square)
                    for j in range(2):
                        self.mm(pR[0:1, 0:128], self.ones_col, sqb[j], j == 0, j == 1)
                    self.ts(rowf, pR[0:1, 0:128], 1e-6, ALU.add)
                    self.act(rowf, rowf, AF.Ln)
                    self.act(rowf, rowf, AF.Exp, scale=-0.5)
                    self.cp(rowb, rowf)
                    self.mm(pB[:, 0:128], self.ones_lhs, rowb, True, True)
                    for j in range(2):
                        self.tt(dst2[j], src2[j], pB[:, 0:128], ALU.mult)
                self.mm(pB[:, 128:256], self.ones_lhs, row4[:, 128 * h:128 * (h + 1)], True, True)
                for j in range(2):
                    self.tt(kb[j], kn[j], pB[:, 128:256], ALU.mult)
                    self.tt(vb[j], v2[j], pB[:, 128:256], ALU.mult)
                for j in range(2):
                    self.mm(pA[:, 0:128], kb[j], kn[j], j == 0, j == 1)
                for j in range(2):
                    self.mm(pA[:, 128:256], kn[j], kb[j], j == 0, j == 1)
                self.tt(npw[0], pA[:, 0:128], msl, ALU.mult)
                self.tt(npwT[0], pA[:, 128:256], msu, ALU.mult)
                for i in range(5):
                    self.mm(pB[:, 256:384], npwT[i], npw[i], True, True)
                    self.mm(pB[:, 384:512], npw[i], npwT[i], True, True)
                    self.cp(npw[i + 1], pB[:, 256:384])
                    self.cp(npwT[i + 1], pB[:, 384:512])
                self.mm(pB[:, 256:384], npw[5], npwT[5], True, True)
                self.cp(npwT[6], pB[:, 256:384])
                for j in range(2):
                    self.trans(pT1[:, 128 * j:128 * (j + 1)], vb[j], self.ident_bf[:, :])
                    self.trans(pT1[:, 256 + 128 * j:256 + 128 * (j + 1)], kb[j],
                               self.ident_bf[:, :])
                self.cp(cur[0][:, :], pT1[:, 0:512])
                cparity = 0
                for fi in (6, 5, 4, 3, 2, 1, 0):
                    self.mm(pApp, self.ident_bf[:, :], cur[cparity], True, False)
                    self.mm(pApp, npwT[fi], cur[cparity], False, True)
                    self.cp(cur[1 - cparity][:, :], pApp)
                    cparity = 1 - cparity
                u_tm = cur[cparity][:, 0:256]
                w_tm = cur[cparity][:, 256:512]
                for r in range(2):
                    self.trans(pT2[:, 128 * r:128 * (r + 1)],
                               w_tm[:, 128 * r:128 * (r + 1)], self.ident_bf[:, :])
                    self.cp(wfm[r], pT2[:, 128 * r:128 * (r + 1)])
                for r in range(2):
                    self.mm(pU[:, 0:256], wfm[r], S_bf[h][:, 256 * r:256 * (r + 1)],
                            r == 0, r == 1)
                self.tt(uadj, u_tm, pU[:, 0:256], ALU.subtract)
                for j in range(2):
                    self.mm(pA[:, 256:384], kn[j], qn[j], j == 0, j == 1)
                self.tt(attT, pA[:, 256:384], mui, ALU.mult)
                for mr in range(2):
                    po = pO[:, 128 * mr:128 * (mr + 1)]
                    for r in range(2):
                        self.mm(po, S_bf[h][:, 256 * r + 128 * mr:256 * r + 128 * mr + 128],
                                qn[r], r == 0, False)
                    self.mm(po, uadj[:, 128 * mr:128 * mr + 128], attT, False, True)
                for mr in range(2):
                    self.cp(obf[mr], pO[:, 128 * mr:128 * (mr + 1)])
                    self.dma("s", DOF[256 * h + 128 * mr:256 * h + 128 * (mr + 1),
                                     tc:tc + 128], obf[mr])
                for r in range(2):
                    self.trans(pT2[:, 256 + 128 * r:256 + 128 * (r + 1)], kn[r],
                               self.ident_bf[:, :])
                    self.cp(kntm[r], pT2[:, 256 + 128 * r:256 + 128 * (r + 1)])
                for r in range(2):
                    self.mm(pU[:, 256:512], kntm[r], uadj, True, True)
                    self.tt(S_sb[h][:, 256 * r:256 * (r + 1)],
                            S_sb[h][:, 256 * r:256 * (r + 1)], pU[:, 256:512], ALU.add)
                    self.cp(S_bf[h][:, 256 * r:256 * (r + 1)],
                            S_sb[h][:, 256 * r:256 * (r + 1)])
        self.end_stage()

    def s_router(self):
        T = self.T
        srcs = [self.dram["XN0F"].ap(), self.dram["P3"].ap(), self.dram["P7"].ap(),
                self.dram["P15"].ap(), self.dram["P31"].ap(), self.dram["DOF"].ap()]
        HM = self.dram["HMIDF"].ap()
        W1 = self.dram["L0_attn_router_W1"].ap()
        b1 = self.sb("r_b1", [128, 16], F32, stage=True)
        for m in range(16):
            self.dma("w", b1[:, m:m + 1],
                     self.dram["L0_attn_router_b1"].ap()[128 * m:128 * (m + 1)]
                     .rearrange("(a b) -> a b", b=1))
        w1 = self.sb("r_w1", [128, 48 * 1024], BF16, stage=True)
        slabs = [self.sb("r_s%d" % k, [128, TT], BF16, stage=True) for k in range(48)]
        ev = [self.sb("r_e%d" % i, [128, TT], BF16, stage=True) for i in range(3)]
        pp = [self.ps("r_p%d" % i, [128, 512], F32, stage=True) for i in range(3)]
        for mh in range(2):
            for k in range(48):
                self.dma("w", w1[:, k * 1024:(k + 1) * 1024],
                         W1[128 * k:128 * (k + 1), 1024 * mh:1024 * (mh + 1)])
            for it in range(self.NT):
                t0 = it * TT
                for g in range(6):
                    for k in range(8):
                        self.dma("s", slabs[8 * g + k],
                                 srcs[g][128 * k:128 * (k + 1), t0:t0 + TT])
                for m8 in range(8):
                    m = 8 * mh + m8
                    p = pp[m8 % 3]
                    for k in range(48):
                        self.mm(p, w1[:, k * 1024 + m8 * 128:k * 1024 + (m8 + 1) * 128],
                                slabs[k], k == 0, k == 47)
                    e = ev[m8 % 3]
                    self.act(e, p, AF.Gelu, bias=b1[:, m:m + 1])
                    self.dma("s", HM[128 * m:128 * (m + 1), t0:t0 + TT], e)
        self.end_stage()

    def s_probs(self):
        T = self.T
        HM = self.dram["HMIDF"].ap()
        PRF = self.dram["PRF"].ap()
        beta_d = self.dram["BETA"].ap()
        w2 = self.load_w_bf("pb_w2", self.dram["L0_attn_router_W2"].ap(), 2 * D, 20)
        brow = self.sb("pb_b", [1, 20], BF16, stage=True)
        self.dma("w", brow, self.dram["L0_attn_router_b2"].ap()
                 .rearrange("(o a) -> o a", o=1))
        trow = self.sb("pb_t", [1, 4], F32, stage=True)
        trowb = self.sb("pb_tb", [1, 4], BF16, stage=True)
        rtau = self.sb("pb_rt", [128, 4], F32, stage=True)
        self.dma("w", trow, self.dram["L0_attn_log_tau"].ap()
                 .rearrange("(o a) -> o a", o=1))
        self.act(trow, trow, AF.Exp)
        self.ts(trow, trow, 0.5, ALU.max)
        self.recip(trow, trow)
        self.cp(trowb, trow)
        slabs = [self.sb("pb_s%d" % k, [128, TT], BF16, stage=True) for k in range(16)]
        pv = self.sb("pb_pv", [128, 4], F32, stage=True)
        sq = self.sb("pb_sq", [128, 4], F32, stage=True)
        pm = self.sb("pb_pm", [128, 28], F32, stage=True)
        sc5 = self.sb("pb_sc5", [128, 5], F32, stage=True)
        cols = self.sb("pb_c", [128, 4], F32, stage=True)
        pp = self.ps("pb_p", [128, 512], F32, stage=True)
        pt = self.ps("pb_pt", [128, 512], F32, stage=True)
        evb = self.sb("pb_ev", [28, 512], F32, stage=True)
        self.mm(pt[:, 0:4], self.ones_lhs, trowb, True, True)
        self.cp(rtau, pt[:, 0:4])
        for it in range(self.NT):
            t0 = it * TT
            for k in range(16):
                self.dma("s", slabs[k], HM[128 * k:128 * (k + 1), t0:t0 + TT])
            for j in range(4):
                p20 = pp[:, 0:20]
                for k in range(16):
                    self.mm(p20, slabs[k][:, 128 * j:128 * (j + 1)], self.wch(w2, 20, k, 0, 20),
                            k == 0, False)
                self.mm(p20, self.ones_lhs, brow, False, True)
                for h in range(4):
                    self.tt(sc5, p20[:, 5 * h:5 * h + 5],
                            rtau[:, h:h + 1].broadcast_to([128, 5]), ALU.mult)
                    nmx = cols[:, 0:1]
                    self.red(nmx, sc5, ALU.max, negate=True)
                    self.act(sc5, sc5, AF.Exp, bias=nmx)
                    ssum = cols[:, 1:2]
                    self.red(ssum, sc5, ALU.add)
                    self.recip(ssum, ssum)
                    self.tt(pm[:, 5 * h:5 * h + 5], sc5,
                            ssum.broadcast_to([128, 5]), ALU.mult)
                r0 = t0 + 128 * j
                self.dma("s", pv, beta_d[r0:r0 + 128, 4:8])
                self.act(sq, pv, AF.Square)
                self.ts(sq, sq, 0.09, ALU.mult)
                self.ts(pv, pv, 0.9, ALU.mult)
                self.tt(pm[:, 20:24], pv, sq, ALU.add)
                self.ts2(pm[:, 24:28], pm[:, 20:24], -1.0, 1.0, ALU.mult, ALU.add)
                self.trans(pt[0:28, 128 * j:128 * (j + 1)], pm[:, :],
                           self.ident_f32[:, :])
            self.cp(evb[:, :], pt[0:28, 0:512])
            self.dma("s", PRF[0:28, t0:t0 + TT], evb[:, :])
        self.end_stage()

    def s_ctx_wo(self):
        T = self.T
        PRF = self.dram["PRF"].ap()
        S0 = self.dram["S0"].ap()
        paths = [self.dram[n].ap() for n in ("P3", "P7", "P15", "P31", "DOF")]
        VF = self.dram["VF"].ap()
        onw = self.sb("cw_onw", [128, 2], F32, stage=True)
        for r in range(2):
            self.dma("w", onw[:, r:r + 1], self.dram["L0_attn_o_norm_w"].ap()
                     [128 * r:128 * (r + 1)].rearrange("(a b) -> a b", b=1))
        wo = self.load_w_bf("cw_wo", self.dram["L0_attn_Wo"].ap(), D, D,
                            rowscale=lambda k: onw[:, k % 2:k % 2 + 1])
        rows = self.sb("cw_rows", [1, 8 * 512], F32, stage=True)
        rowsb = self.sb("cw_rb", [1, 8 * 512], BF16, stage=True)
        sl = [self.sb("cw_sl%d" % i, [128, TT], BF16, stage=True) for i in range(6)]
        oacc = [self.sb("cw_o%d" % i, [128, TT], F32, stage=True) for i in range(2)]
        tmp = self.sb("cw_tmp", [128, TT], F32, stage=True)
        sqb = self.sb("cw_sq", [128, TT], BF16, stage=True)
        onb = [self.sb("cw_on%d" % k, [128, TT], BF16, stage=True) for k in range(8)]
        rsr = self.sb("cw_rsr", [1, 512], F32, stage=True)
        rsb = self.sb("cw_rsb", [1, 512], BF16, stage=True)
        sfm = [self.sb("cw_sf%d" % i, [128, TT], BF16, stage=True) for i in range(8)]
        stm = [self.sb("cw_st%d" % i, [128, 1024], BF16, stage=True) for i in range(2)]
        pb = self.ps("cw_pb", [128, 512], F32, stage=True)
        pw = [self.ps("cw_pw%d" % i, [128, 512], F32, stage=True) for i in range(2)]
        ptr = [self.ps("cw_pt%d" % i, [128, 1024], BF16, stage=True) for i in range(2)]
        for it in range(self.NT):
            t0 = it * TT
            for h in range(4):
                for i in range(5):
                    self.dma("s", rows[:, i * 512:(i + 1) * 512],
                             PRF[5 * h + i:5 * h + i + 1, t0:t0 + TT])
                self.dma("s", rows[:, 5 * 512:6 * 512], PRF[20 + h:21 + h, t0:t0 + TT])
                self.dma("s", rows[:, 6 * 512:7 * 512], PRF[24 + h:25 + h, t0:t0 + TT])
                for i in range(5):
                    self.tt(rows[:, i * 512:(i + 1) * 512], rows[:, i * 512:(i + 1) * 512],
                            rows[:, 6 * 512:7 * 512], ALU.mult)
                self.cp(rowsb[:, 0:6 * 512], rows[:, 0:6 * 512])
                for mc in range(2):
                    m = 2 * h + mc
                    for i in range(5):
                        self.dma("s", sl[i], paths[i][128 * m:128 * (m + 1), t0:t0 + TT])
                    self.dma("s", sl[5], VF[128 * m:128 * (m + 1), 32 + t0:32 + t0 + TT])
                    o = oacc[mc]
                    for i in range(6):
                        self.mm(pb, self.ones_lhs, rowsb[:, i * 512:(i + 1) * 512],
                                True, True)
                        if i == 0:
                            self.tt(o, sl[0], pb, ALU.mult)
                        else:
                            self.tt(tmp, sl[i], pb, ALU.mult)
                            self.tt(o, o, tmp, ALU.add)
                for mc in range(2):
                    self.act(sqb, oacc[mc], AF.Square)
                    self.mm(pw[0][0:1, 0:512], self.ones_col, sqb, mc == 0, mc == 1)
                self.ts(rsr, pw[0][0:1, 0:512], 1.0 / 256.0, ALU.mult)
                self.ts(rsr, rsr, 1e-5, ALU.add)
                self.act(rsr, rsr, AF.Ln)
                self.act(rsr, rsr, AF.Exp, scale=-0.5)
                self.cp(rsb, rsr)
                self.mm(pb, self.ones_lhs, rsb, True, True)
                for mc in range(2):
                    self.tt(onb[2 * h + mc], oacc[mc], pb, ALU.mult)
            for m2 in range(8):
                p = pw[m2 % 2]
                for k in range(8):
                    self.mm(p, self.wch(wo, D, k, m2 * 128, (m2 + 1) * 128),
                            onb[k], k == 0, k == 7)
                self.cp(sfm[m2], p)
            for j in range(4):
                for m2 in range(8):
                    self.trans(ptr[j % 2][:, 128 * m2:128 * (m2 + 1)],
                               sfm[m2][:, 128 * j:128 * (j + 1)], self.ident_bf[:, :])
                st = stm[j % 2]
                self.cp(st, ptr[j % 2][:, :])
                self.dma("s", S0[t0 + 128 * j:t0 + 128 * (j + 1), :], st)
        self.end_stage()

    def s_shared(self):
        T = self.T
        XN = self.dram["XN0F"].ap()
        ws1 = self.load_w_bf("sh_w1", self.dram["L0_ffn_Ws1"].ap(), D, D)
        ws3 = self.load_w_bf("sh_w3", self.dram["L0_ffn_Ws3"].ap(), D, D)
        ws2 = self.load_w_bf("sh_w2", self.dram["L0_ffn_Ws2"].ap(), D, D)
        slabs = [self.sb("sh_s%d" % k, [128, TT], BF16, stage=True) for k in range(8)]
        h1 = [self.sb("sh_h1%d" % k, [128, TT], BF16, stage=True) for k in range(8)]
        hh = [self.sb("sh_hh%d" % k, [128, TT], BF16, stage=True) for k in range(8)]
        zbf = [self.sb("sh_z%d" % i, [128, TT], BF16, stage=True) for i in range(8)]
        ytl = [self.sb("sh_y%d" % i, [128, 1024], F32, stage=True) for i in range(2)]
        zt = self.sb("sh_zt", [128, 1024], F32, stage=True)
        pp = [self.ps("sh_p%d" % i, [128, 512], F32, stage=True) for i in range(3)]
        ptr = [self.ps("sh_pt%d" % i, [128, 1024], BF16, stage=True) for i in range(2)]
        self.memset(zt[:, :], 0.0)
        Y = [self.dram["Y%d" % i].ap() for i in range(4)]
        for c in range(self.NC):
            for yi in range(1, 4):
                self.dma("a", Y[yi][128 * c:128 * (c + 1), :], zt)
        for it in range(self.NT):
            t0 = it * TT
            for k in range(8):
                self.dma("s", slabs[k], XN[128 * k:128 * (k + 1), t0:t0 + TT])
            for m in range(8):
                p = pp[m % 2]
                for k in range(8):
                    self.mm(p, self.wch(ws1, D, k, m * 128, (m + 1) * 128), slabs[k],
                            k == 0, k == 7)
                self.act(h1[m], p, AF.Silu)
            for m in range(8):
                p = pp[m % 2]
                for k in range(8):
                    self.mm(p, self.wch(ws3, D, k, m * 128, (m + 1) * 128), slabs[k],
                            k == 0, k == 7)
                self.cp(hh[m], p)
                self.tt(hh[m], h1[m], hh[m], ALU.mult)
            for m2 in range(8):
                p = pp[2]
                for k in range(8):
                    self.mm(p, self.wch(ws2, D, k, m2 * 128, (m2 + 1) * 128), hh[k],
                            k == 0, k == 7)
                self.cp(zbf[m2], p)
            for j in range(4):
                for m2 in range(8):
                    self.trans(ptr[j % 2][:, 128 * m2:128 * (m2 + 1)],
                               zbf[m2][:, 128 * j:128 * (j + 1)], self.ident_bf[:, :])
                yt = ytl[j % 2]
                self.cp(yt, ptr[j % 2][:, :])
                self.dma("s", Y[0][t0 + 128 * j:t0 + 128 * (j + 1), :], yt)
        self.end_stage()


    def s_moe(self):
        T = self.T
        NS = CAP // 512
        XNAUG = self.dram["XNAUG"].ap()
        WSEL = self.dram["WSEL"].ap()
        CM = self.dram["CM"].ap()
        Y = [self.dram["Y%d" % i].ap() for i in range(4)]
        W1 = self.dram["L0_ffn_W1"].ap()
        W2 = self.dram["L0_ffn_W2"].ap()
        W3 = self.dram["L0_ffn_W3"].ap()
        idxs = self.sb("m_idx", [128, CAP // 16], I16, stage=True)
        gx = self.sb("m_gx", [128, 8 * CAP], BF16, stage=True)
        wv = self.sb("m_wv", [128, (CAP // 128) * 64], F32, stage=True)
        w1e = [self.sb("m_w1%d" % i, [128, 8 * 512], BF16, stage=True) for i in range(2)]
        w3e = [self.sb("m_w3%d" % i, [128, 8 * 512], BF16, stage=True) for i in range(2)]
        w2e = [self.sb("m_w2%d" % i, [128, 4 * 1024], BF16, stage=True) for i in range(2)]
        h1 = [self.sb("m_h1%d" % m, [128, CAP], BF16, stage=True) for m in range(4)]
        hh = [self.sb("m_hh%d" % m, [128, CAP], BF16, stage=True) for m in range(4)]
        ofm = [self.sb("m_of%d" % m, [128, CAP], BF16, stage=True) for m in range(8)]
        sct = self.sb("m_sc", [128, 4 * 1024], F32, stage=True)
        pp = [self.ps("m_p%d" % i, [128, 512], F32, stage=True) for i in range(3)]
        ptr = [self.ps("m_pt%d" % i, [128, 1024], BF16, stage=True) for i in range(2)]
        for e in range(32):
            wpar = e % 2
            for k in range(8):
                self.dma("w", w1e[wpar][:, 512 * k:512 * (k + 1)],
                         W1[e, 128 * k:128 * (k + 1), :])
                self.dma("w", w3e[wpar][:, 512 * k:512 * (k + 1)],
                         W3[e, 128 * k:128 * (k + 1), :])
            for k in range(4):
                self.dma("w", w2e[wpar][:, 1024 * k:1024 * (k + 1)],
                         W2[e, 128 * k:128 * (k + 1), :])
            cmrow = CM[e:e + 1, :].rearrange("o (p s) -> (o p) s", p=16)
            for g in range(8):
                self.dma("s", idxs[16 * g:16 * (g + 1), :], cmrow)
            self.P.op("p", "dma_gather",
                      (gx[:, :].rearrange("p (c n) -> p c n", c=8), XNAUG,
                       idxs[:, :], CAP, CAP, 1024),
                      {"transpose": True, "single_packet": False},
                      outs=(gx[:, :],), ins=(idxs[:, :],), qdma=True)
            self.P.op("p", "dma_gather",
                      (wv[:, :].rearrange("p (n c) -> p n c", c=64), WSEL,
                       idxs[:, :], CAP, CAP, 64),
                      {"transpose": False, "single_packet": False},
                      outs=(wv[:, :],), ins=(idxs[:, :],), qdma=True)
            for m in range(4):
                for nt in range(NS):
                    p = pp[nt % 2]
                    for k in range(8):
                        self.mm(p, w1e[wpar][:, 512 * k + 128 * m:512 * k + 128 * (m + 1)],
                                gx[:, CAP * k + 512 * nt:CAP * k + 512 * (nt + 1)],
                                k == 0, k == 7)
                    self.act(h1[m][:, 512 * nt:512 * (nt + 1)], p, AF.Silu)
            for m in range(4):
                for nt in range(NS):
                    p = pp[nt % 2]
                    for k in range(8):
                        self.mm(p, w3e[wpar][:, 512 * k + 128 * m:512 * k + 128 * (m + 1)],
                                gx[:, CAP * k + 512 * nt:CAP * k + 512 * (nt + 1)],
                                k == 0, k == 7)
                    self.cp(hh[m][:, 512 * nt:512 * (nt + 1)], p)
                self.tt(hh[m][:, :], h1[m][:, :], hh[m][:, :], ALU.mult)
            for m2 in range(8):
                for nt in range(NS):
                    p = pp[nt % 2]
                    for k in range(4):
                        self.mm(p, w2e[wpar][:, 1024 * k + 128 * m2:1024 * k + 128 * (m2 + 1)],
                                hh[k][:, 512 * nt:512 * (nt + 1)], k == 0, k == 3)
                    self.cp(ofm[m2][:, 512 * nt:512 * (nt + 1)], p)
            for jg in range(NS):
                for j4 in range(4):
                    j = 4 * jg + j4
                    for m2 in range(8):
                        self.trans(ptr[j4 % 2][:, 128 * m2:128 * (m2 + 1)],
                                   ofm[m2][:, 128 * j:128 * (j + 1)],
                                   self.ident_bf[:, :])
                    self.tt(sct[:, 1024 * j4:1024 * (j4 + 1)], ptr[j4 % 2][:, :],
                            wv[:, 64 * (4 * jg + j4) + e:64 * (4 * jg + j4) + e + 1]
                            .broadcast_to([128, 1024]), ALU.mult)
                self.P.op("p", "dma_scatter_add",
                          (Y[e % 4], sct[:, :].rearrange("p (n c) -> p n c", c=1024),
                           idxs[:, 32 * jg:32 * (jg + 1)], 512, 512, 1024),
                          {"single_packet": False},
                          outs=(Y[e % 4],), ins=(sct[:, :], idxs[:, :]), qdma=True)
        self.end_stage()

    def s_combine0(self):
        T = self.T
        x0 = self.dram["x0"].ap()
        S0 = self.dram["S0"].ap()
        Y = [self.dram["Y%d" % i].ap() for i in range(4)]
        X1 = self.dram["X1"].ap()
        XFA = self.dram["XFA"].ap()
        XN1F = self.dram["XN1F"].ap()
        HN1F = self.dram["HN1F"].ap()
        xt = [self.sb("cb_x%d" % i, [128, D], F32, stage=True) for i in range(2)]
        yt = [self.sb("cb_y%d" % i, [128, D], F32, stage=True) for i in range(2)]
        st = [self.sb("cb_s%d" % i, [128, D], BF16, stage=True) for i in range(2)]
        g1 = self.sb("cb_g", [128, D], F32, stage=True)
        t1 = self.sb("cb_t", [128, D], F32, stage=True)
        sqb = self.sb("cb_sq", [128, D], BF16, stage=True)
        xnb = [self.sb("cb_xn%d" % j, [128, D], BF16, stage=True) for j in range(4)]
        hnb = [self.sb("cb_hn%d" % j, [128, D], BF16, stage=True) for j in range(4)]
        cols = self.sb("cb_c", [128, 8], F32, stage=True)
        pbf = [self.ps("cb_p%d" % i, [128, 1024], BF16, stage=True) for i in range(2)]
        ev = [self.sb("cb_e%d" % i, [128, 512], BF16, stage=True) for i in range(2)]
        for it in range(self.NT):
            t0 = it * TT
            for j in range(4):
                r0 = t0 + 128 * j
                xb = xt[j % 2]
                yb = yt[j % 2]
                sb_ = st[j % 2]
                self.dma("s", xb, x0[r0:r0 + 128, :])
                self.dma("s", sb_, S0[r0:r0 + 128, :])
                self.dma("s", yb, Y[0][r0:r0 + 128, :])
                self.dma("s", g1, Y[1][r0:r0 + 128, :])
                self.tt(g1, g1, yb, ALU.add)
                self.dma("s", yb, Y[2][r0:r0 + 128, :])
                self.tt(g1, g1, yb, ALU.add)
                self.dma("s", yb, Y[3][r0:r0 + 128, :])
                self.tt(g1, g1, yb, ALU.add)
                self.act(t1, g1, AF.Exp, scale=-1.0)
                self.act(t1, t1, AF.Ln, bias=self.onef[:, 0:1])
                self.act(t1, t1, AF.Exp, scale=-1.0)
                self.tt(g1, g1, t1, ALU.mult)
                self.tt(t1, sb_, g1, ALU.mult)
                self.tt(xb, xb, t1, ALU.add)
                self.dma("s", X1[r0:r0 + 128, :], xb)
                nm = cols[:, 0:1]
                v = cols[:, 1:2]
                rs = cols[:, 2:3]
                self.red(nm, xb, ALU.add, negate=True)
                self.ts(nm, nm, 1.0 / D, ALU.mult)
                self.act(xb, xb, AF.Identity, bias=nm)
                self.act(sqb, xb, AF.Square)
                self.red(v, sqb, ALU.add)
                self.ts(v, v, 1.0 / D, ALU.mult)
                self.ts(v, v, 1e-5, ALU.add)
                self.act(v, v, AF.Ln)
                self.act(rs, v, AF.Exp, scale=-0.5)
                self.tt(xb, xb, rs.broadcast_to([128, D]), ALU.mult)
                self.dma("s", XFA[8 + r0:8 + r0 + 128, :], xb)
                self.cp(xnb[j], xb)
                v2 = cols[:, 3:4]
                rs2 = cols[:, 4:5]
                self.act(sqb, xb, AF.Square)
                self.red(v2, sqb, ALU.add)
                self.ts(v2, v2, 1.0 / D, ALU.mult)
                self.ts(v2, v2, 1e-6, ALU.add)
                self.act(v2, v2, AF.Ln)
                self.act(rs2, v2, AF.Exp, scale=-0.5)
                self.tt(t1, xb, rs2.broadcast_to([128, D]), ALU.mult)
                self.cp(hnb[j], t1)
            for c in range(8):
                for j in range(4):
                    self.trans(pbf[0][:, 128 * j:128 * (j + 1)],
                               xnb[j][:, 128 * c:128 * (c + 1)], self.ident_bf[:, :])
                    self.trans(pbf[1][:, 128 * j:128 * (j + 1)],
                               hnb[j][:, 128 * c:128 * (c + 1)], self.ident_bf[:, :])
                self.cp(ev[0], pbf[0][:, 0:512])
                self.cp(ev[1], pbf[1][:, 0:512])
                self.dma("s", XN1F[128 * c:128 * (c + 1), t0:t0 + TT], ev[0])
                self.dma("s", HN1F[128 * c:128 * (c + 1), t0:t0 + TT], ev[1])
        self.end_stage()


    def s_l1proj(self):
        T = self.T
        HN = self.dram["HN1F"].ap()
        GL = self.dram["GL1"].ap()
        nw = self.sb("lp_nw", [128, 8], F32, stage=True)
        for k in range(8):
            self.dma("w", nw[:, k:k + 1], self.dram["L1_attn_norm_w"].ap()
                     [128 * k:128 * (k + 1)].rearrange("(a b) -> a b", b=1))
        rsc = lambda k: nw[:, k:k + 1]
        wq = self.load_w_bf("lp_wq", self.dram["L1_attn_WqT"].ap(), D, D, rowscale=rsc)
        wk = self.load_w_bf("lp_wk", self.dram["L1_attn_WkT"].ap(), D, D, rowscale=rsc)
        wv = self.load_w_bf("lp_wv", self.dram["L1_attn_WvT"].ap(), D, D, rowscale=rsc)
        wg = self.load_w_bf("lp_wg", self.dram["L1_attn_WgT"].ap(), D, 24, rowscale=rsc)
        gbrow = self.sb("lp_gb", [1, 24], BF16, stage=True)
        self.dma("w", gbrow, self.dram["L1_attn_gb"].ap().rearrange("(o a) -> o a", o=1))
        swap = self.sb("lp_swap", [128, 128], BF16, stage=True)
        self.dma("s", swap, self.dram["c_swap_h_bf"].ap())
        self.cum_bf = self.sb("lp_cum", [128, 128], BF16, stage=True)
        self.dma("s", self.cum_bf, self.dram["c_cum_bf"].ap())
        cs2 = self.sb("lp_cs", [128, TT], F32, stage=True)
        ss2 = self.sb("lp_ss", [128, TT], F32, stage=True)
        dsts = {"q": self.dram["QF1"].ap(), "k": self.dram["KF1"].ap(),
                "v": self.dram["VF1"].ap()}
        slabs = [self.sb("lp_s%d" % k, [128, TT], BF16, stage=True) for k in range(8)]
        pre = [self.sb("lp_pre%d" % i, [128, TT], BF16, stage=True) for i in range(2)]
        rot = [self.sb("lp_rot%d" % i, [128, TT], F32, stage=True) for i in range(2)]
        ob = [self.sb("lp_o%d" % i, [128, TT], BF16, stage=True) for i in range(2)]
        e24 = self.sb("lp_e24", [128, 24], F32, stage=True)
        l16 = self.sb("lp_l16", [128, 16], BF16, stage=True)
        lf = self.sb("lp_lf", [128, 8], F32, stage=True)
        glo = self.sb("lp_glo", [128, 24], F32, stage=True)
        pp = [self.ps("lp_p%d" % i, [128, 512], F32, stage=True) for i in range(3)]
        psw = self.ps("lp_psw", [128, 512], F32, stage=True)
        for it in range(self.NT):
            t0 = it * TT
            for k in range(8):
                self.dma("s", slabs[k], HN[128 * k:128 * (k + 1), t0:t0 + TT])
            self.dma("s", cs2, self.dram["c_rope1_cs2"].ap()[:, t0:t0 + TT])
            self.dma("s", ss2, self.dram["c_rope1_ss2"].ap()[:, t0:t0 + TT])
            for nm, w in (("q", wq), ("k", wk), ("v", wv)):
                for m in range(8):
                    p = pp[m % 2]
                    for k in range(8):
                        self.mm(p, self.wch(w, D, k, m * 128, (m + 1) * 128),
                                slabs[k], k == 0, k == 7)
                    if nm == "v":
                        self.cp(ob[m % 2], p)
                        self.dma("s", dsts[nm][128 * m:128 * (m + 1), t0:t0 + TT],
                                 ob[m % 2])
                    else:
                        pr = pre[m % 2]
                        self.cp(pr, p)
                        self.mm(psw, swap, pr, True, True)
                        r = rot[m % 2]
                        self.tt(r, pr, cs2, ALU.mult)
                        self.tt(rot[1 - m % 2], psw, ss2, ALU.mult)
                        self.tt(ob[m % 2], r, rot[1 - m % 2], ALU.add)
                        self.dma("s", dsts[nm][128 * m:128 * (m + 1), t0:t0 + TT],
                                 ob[m % 2])
            for j in range(4):
                p24 = pp[2][:, 0:24]
                for k in range(8):
                    self.mm(p24, slabs[k][:, 128 * j:128 * (j + 1)],
                            self.wch(wg, 24, k, 0, 24), k == 0, False)
                self.mm(p24, self.ones_lhs, gbrow, False, True)
                self.act(e24, p24, AF.Exp, scale=-1.0)
                self.ts(e24, e24, 1.0, ALU.add)
                self.recip(glo[:, :], e24)
                self.act(lf, e24[:, 0:8], AF.Ln)
                self.cp(l16[:, 0:8], lf)
                self.tt(lf, lf, l16[:, 0:8], ALU.subtract)
                self.cp(l16[:, 8:16], lf)
                pl = pp[2][:, 32:48]
                self.mm(pl, self.cum_bf[:, :], l16, True, True)
                self.cp(lf, pl[:, 0:8])
                self.tt(lf, lf, pl[:, 8:16], ALU.add)
                self.ts(lf, lf, -1.0, ALU.mult)
                r0 = t0 + 128 * j
                self.dma("s", GL[r0:r0 + 128, 0:8], lf)
                self.dma("s", GL[r0:r0 + 128, 8:24], glo[:, 8:24])
        self.end_stage()

    def s_gla(self):
        T = self.T
        QF, KF, VF = (self.dram[n].ap() for n in ("QF1", "KF1", "VF1"))
        GL = self.dram["GL1"].ap()
        Y1 = self.dram["Y1F"].ap()
        M_sb = [self.sb("g_M%d" % h, [128, 128], F32, stage=True) for h in range(8)]
        M_bf = [self.sb("g_Mb%d" % h, [128, 128], BF16, stage=True) for h in range(8)]
        for h in range(8):
            self.memset(M_sb[h][:, :], 0.0)
            self.memset(M_bf[h][:, :], 0.0)
        ones2 = self.sb("g_ones2", [2, 128], BF16, stage=True)
        self.dma("s", ones2, self.dram["c_ones2_bf"].ap())
        gm = self.sb("g_gm", [128, 128], F32, stage=True)
        gb = self.sb("g_gb", [128, 128], F32, stage=True)
        self.dma("s", gm, self.dram["c_gla_m"].ap())
        self.dma("s", gb, self.dram["c_gla_b"].ap())
        qsl = [self.sb("g_q%d" % h, [128, TT], BF16, stage=True) for h in range(8)]
        ksl = [self.sb("g_k%d" % h, [128, TT], BF16, stage=True) for h in range(8)]
        vsl = [self.sb("g_v%d" % h, [128, TT], BF16, stage=True) for h in range(8)]
        glt = [self.sb("g_gl%d" % j, [128, 24], F32, stage=True) for j in range(4)]
        lb2 = self.sb("g_lb2", [128, 2], BF16, stage=True)
        lrow = self.sb("g_lrow", [2, 128], BF16, stage=True)
        ogb = self.sb("g_ogb", [128, 1], BF16, stage=True)
        ogrow = self.sb("g_ogr", [1, 128], BF16, stage=True)
        ogsb = self.sb("g_ogsb", [128, 128], F32, stage=True)
        dm = self.sb("g_dm", [128, 128], F32, stage=True)
        wd = self.sb("g_wd", [128, 128], F32, stage=True)
        elbc = self.sb("g_el", [128, 128], F32, stage=True)
        qd = self.sb("g_qd", [128, 128], BF16, stage=True)
        att = self.sb("g_att", [128, 128], BF16, stage=True)
        vtm = self.sb("g_vtm", [128, 128], BF16, stage=True)
        kd = self.sb("g_kd", [128, 128], BF16, stage=True)
        ybf = self.sb("g_y", [128, 128], BF16, stage=True)
        gscol = self.sb("g_gs", [128, 1], F32, stage=True)
        dcol = self.sb("g_dc", [128, 1], F32, stage=True)
        elc = self.sb("g_elc", [128, 1], F32, stage=True)
        pS = self.ps("g_pS", [128, 512], F32, stage=True)
        pY = self.ps("g_pY", [128, 512], F32, stage=True)
        pM = self.ps("g_pM", [128, 512], F32, stage=True)
        pB = self.ps("g_pB", [128, 512], F32, stage=True)
        pT = self.ps("g_pT", [128, 1024], BF16, stage=True)
        SC = 1.0 / (128.0 ** 0.5)
        for it in range(self.NT):
            t0 = it * TT
            for h in range(8):
                self.dma("s", qsl[h], QF[128 * h:128 * (h + 1), t0:t0 + TT])
                self.dma("s", ksl[h], KF[128 * h:128 * (h + 1), t0:t0 + TT])
                self.dma("s", vsl[h], VF[128 * h:128 * (h + 1), t0:t0 + TT])
            for j in range(4):
                self.dma("s", glt[j], GL[t0 + 128 * j:t0 + 128 * (j + 1), :])
            for j in range(4):
                sl = slice(128 * j, 128 * (j + 1))
                for h in range(8):
                    L = glt[j][:, h:h + 1]
                    q = qsl[h][:, sl]
                    k = ksl[h][:, sl]
                    v = vsl[h][:, sl]
                    self.cp(lb2[:, 0:1], L)
                    self.tt(lb2[:, 1:2], L, lb2[:, 0:1], ALU.subtract)
                    self.mm(pB[0:2, 0:128], lb2, self.ident_bf[:, :], True, True)
                    self.cp(lrow, pB[0:2, 0:128])
                    self.mm(pB[:, 128:256], ones2, lrow, True, True)
                    self.cp(ogb, glt[j][:, 16 + h:17 + h])
                    self.mm(pB[0:1, 384:512], ogb, self.ident_bf[:, :], True, True)
                    self.cp(ogrow, pB[0:1, 384:512])
                    self.mm(pB[:, 256:384], self.ones_lhs, ogrow, True, True)
                    self.cp(ogsb, pB[:, 256:384])
                    self.tt(dm, pB[:, 128:256], L.broadcast_to([128, 128]), ALU.subtract)
                    self.tt(dm, dm, gm, ALU.mult)
                    self.tt(dm, dm, gb, ALU.add)
                    self.act(wd, dm, AF.Exp)
                    self.act(elbc, pB[:, 128:256], AF.Exp)
                    self.ts(gscol, glt[j][:, 8 + h:9 + h], SC, ALU.mult)
                    self.mm(pS[:, 0:128], k, q, True, True)
                    self.tt(wd, pS[:, 0:128], wd, ALU.mult)
                    self.tt(att, wd, gscol.broadcast_to([128, 128]), ALU.mult)
                    self.tt(qd, q, elbc, ALU.mult)
                    self.trans(pT[:, 0:128], v, self.ident_bf[:, :])
                    self.cp(vtm, pT[:, 0:128])
                    self.mm(pY[:, 0:128], M_bf[h], qd, True, False)
                    self.mm(pY[:, 0:128], vtm, att, False, True)
                    self.tt(ybf, pY[:, 0:128], ogsb, ALU.mult)
                    self.dma("s", Y1[128 * h:128 * (h + 1), t0 + 128 * j:
                                     t0 + 128 * (j + 1)], ybf)
                    self.mm(pB[0:128, 128:129], ones2, lrow[:, 127:128], True, True)
                    self.tt(dcol, pB[0:128, 128:129], L, ALU.subtract)
                    self.act(dcol, dcol, AF.Exp)
                    self.tt(dcol, dcol, gscol, ALU.mult)
                    self.act(elc, pB[0:128, 128:129], AF.Exp)
                    self.trans(pT[:, 128:256], k, self.ident_bf[:, :])
                    self.tt(kd, pT[:, 128:256], dcol.broadcast_to([128, 128]), ALU.mult)
                    self.mm(pM[:, 0:128], kd, vtm, True, True)
                    self.stt(M_sb[h], M_sb[h], elc, pM[:, 0:128], ALU.mult, ALU.add)
                    self.cp(M_bf[h], M_sb[h])
        self.end_stage()

    def s_wout(self):
        T = self.T
        Y1 = self.dram["Y1F"].ap()
        S1 = self.dram["S1"].ap()
        wo = self.load_w_bf("wo_w", self.dram["L1_attn_WoutT"].ap(), D, D)
        bcol = self.sb("wo_b", [128, 8], F32, stage=True)
        for m in range(8):
            self.dma("w", bcol[:, m:m + 1], self.dram["L1_attn_bout"].ap()
                     [128 * m:128 * (m + 1)].rearrange("(a b) -> a b", b=1))
        slabs = [self.sb("wo_s%d" % k, [128, TT], BF16, stage=True) for k in range(8)]
        sfm = [self.sb("wo_f%d" % m, [128, TT], BF16, stage=True) for m in range(8)]
        stm = [self.sb("wo_t%d" % i, [128, 1024], BF16, stage=True) for i in range(2)]
        pp = [self.ps("wo_p%d" % i, [128, 512], F32, stage=True) for i in range(2)]
        ptr = [self.ps("wo_pt%d" % i, [128, 1024], BF16, stage=True) for i in range(2)]
        for it in range(self.NT):
            t0 = it * TT
            for k in range(8):
                self.dma("s", slabs[k], Y1[128 * k:128 * (k + 1), t0:t0 + TT])
            for m in range(8):
                p = pp[m % 2]
                for k in range(8):
                    self.mm(p, self.wch(wo, D, k, m * 128, (m + 1) * 128), slabs[k],
                            k == 0, k == 7)
                self.act(sfm[m], p, AF.Identity, bias=bcol[:, m:m + 1])
            for j in range(4):
                for m in range(8):
                    self.trans(ptr[j % 2][:, 128 * m:128 * (m + 1)],
                               sfm[m][:, 128 * j:128 * (j + 1)], self.ident_bf[:, :])
                st = stm[j % 2]
                self.cp(st, ptr[j % 2][:, :])
                self.dma("s", S1[t0 + 128 * j:t0 + 128 * (j + 1), :], st)
        self.end_stage()


    def s_l1ffn_a(self, li):
        T = self.T
        src_x = self.dram["XFA"].ap()
        dst_x = self.dram["XFB"].ap()
        wqkv = self.load_w_bf("fa_wqkv", self.dram["L1_ffn_Wqkv"].ap()[li], D, 3 * D)
        wo = self.load_w_bf("fa_wo", self.dram["L1_ffn_Wo"].ap()[li], D, D)
        qsn = self.sb("fa_qsn", [128, 4], F32, stage=True)
        for (ci, nm, swp) in ((0, "L1_ffn_q_scale", 0), (1, "L1_ffn_q_scale", 1),
                              (2, "L1_ffn_k_scale", 0), (3, "L1_ffn_k_scale", 1)):
            wsrc = self.dram[nm].ap()[li]
            for half in range(2):
                for p32 in range(2):
                    s0 = 32 * ((p32 + swp) % 2)
                    self.dma("w", qsn[64 * half + 32 * p32:64 * half + 32 * (p32 + 1),
                                      ci:ci + 1],
                             wsrc[s0:s0 + 32].rearrange("(a b) -> a b", b=1))
        self.ts(qsn[:, 0:1], qsn[:, 0:1], 8.0, ALU.mult)
        self.ts(qsn[:, 1:2], qsn[:, 1:2], 8.0, ALU.mult)
        swap = self.sb("fa_swap", [128, 128], BF16, stage=True)
        self.dma("s", swap, self.dram["c_swap64_bf"].ap())
        ind2 = self.sb("fa_ind2", [128, 2], BF16, stage=True)
        self.dma("s", ind2, self.dram["c_ind2_bf"].ap())
        ind2T = self.sb("fa_ind2T", [2, 128], BF16, stage=True)
        self.dma("s", ind2T, self.dram["c_ind2T_bf"].ap())
        ones64 = self.sb("fa_o64", [1, 64], BF16, stage=True)
        self.dma("s", ones64, self.dram["c_ones64_bf"].ap())
        lamg = self.sb("fa_lg", [128, 384], BF16, stage=True)
        lamf0 = self.sb("fa_lf0", [128, 384], BF16, stage=True)
        lamf1 = self.sb("fa_lf1", [128, 384], BF16, stage=True)
        self.dma("s", lamg, self.dram["c_lam_gen_bf"].ap())
        self.dma("s", lamf0, self.dram["c_lam_f0_bf"].ap())
        self.dma("s", lamf1, self.dram["c_lam_f1_bf"].ap())
        tabs = [self.sb("fa_tab%d" % i, [128, TT], F32, stage=True) for i in range(4)]
        AB = [self.sb("fa_ab%d" % i, [128, TT], BF16, stage=True) for i in range(4)]
        xsb = [self.sb("fa_xs%d" % j, [128, D], BF16, stage=True) for j in range(4)]
        xsfm = [self.sb("fa_xf%d" % k, [128, TT], BF16, stage=True) for k in range(8)]
        qraw = [self.sb("fa_qr%d" % c, [128, TT], BF16, stage=True) for c in range(2)]
        sqb = self.sb("fa_sq", [128, TT], BF16, stage=True)
        nrm = self.sb("fa_nrm", [2, 8 * TT], F32, stage=True)
        nrmb = self.sb("fa_nrmb", [2, 8 * TT], BF16, stage=True)
        qn = self.sb("fa_qn", [128, TT], BF16, stage=True)
        r1 = self.sb("fa_r1", [128, TT], F32, stage=True)
        r2 = self.sb("fa_r2", [128, TT], F32, stage=True)
        qrot = [self.sb("fa_qq%d" % c, [128, TT], BF16, stage=True) for c in range(8)]
        kroll = [self.sb("fa_kr%d" % c, [128, 768], BF16, stage=True) for c in range(8)]
        vroll = [self.sb("fa_vr%d" % c, [128, 768], BF16, stage=True) for c in range(8)]
        vtm = [self.sb("fa_vt%d" % h, [128, 384], BF16, stage=True) for h in range(16)]
        pbf = self.sb("fa_pbf", [128, 384], BF16, stage=True)
        ptsb = self.sb("fa_pts", [128, 384], BF16, stage=True)
        rrow = self.sb("fa_rr", [1, 256], F32, stage=True)
        rrb = self.sb("fa_rrb", [1, 256], BF16, stage=True)
        bcs = self.sb("fa_bcs", [128, 128], F32, stage=True)
        afm = [self.sb("fa_af%d" % c, [128, TT], BF16, stage=True) for c in range(8)]
        wofm = [self.sb("fa_wf%d" % m, [128, TT], BF16, stage=True) for m in range(8)]
        xut = [self.sb("fa_xu0", [128, D], F32, stage=True)]
        xot = [self.sb("fa_xo0", [128, D], F32, stage=True)]
        cols = self.sb("fa_c", [128, 4], F32, stage=True)
        pQ = [self.ps("fa_pq%d" % i, [128, 512], F32, stage=True) for i in range(2)]
        pN = self.ps("fa_pn", [128, 512], F32, stage=True)
        pBC = self.ps("fa_pbc", [128, 512], F32, stage=True)
        pS = self.ps("fa_ps", [128, 512], F32, stage=True)
        pOR = self.ps("fa_por", [128, 512], F32, stage=True)
        pT1 = self.ps("fa_pt1", [128, 1024], BF16, stage=True)
        pT2 = self.ps("fa_pt2", [128, 1024], BF16, stage=True)
        for c in range(8):
            self.memset(kroll[c][:, :], 0.0)
            self.memset(vroll[c][:, :], 0.0)
        for h in range(16):
            self.memset(vtm[h][:, :], 0.0)
        for it in range(self.NT):
            t0 = it * TT
            if it > 0:
                for c in range(8):
                    self.cp(kroll[c][:, 0:256], kroll[c][:, 512:768])
                    self.cp(vroll[c][:, 0:256], vroll[c][:, 512:768])
                for h in range(16):
                    self.cp(vtm[h][:, 0:128], vtm[h][:, 256:384])
            self.dma("s", tabs[0][0:64, :], self.dram["c_xp_cs"].ap()[0:64, t0:t0 + TT])
            self.dma("s", tabs[0][64:128, :], self.dram["c_xp_cs"].ap()[0:64, t0:t0 + TT])
            self.dma("s", tabs[1][0:64, :], self.dram["c_xp_ssq"].ap()[0:64, t0:t0 + TT])
            self.dma("s", tabs[1][64:128, :], self.dram["c_xp_ssq"].ap()[0:64, t0:t0 + TT])
            self.dma("s", tabs[2][0:64, :], self.dram["c_xp_ci"].ap()[0:64, t0:t0 + TT])
            self.dma("s", tabs[2][64:128, :], self.dram["c_xp_ci"].ap()[0:64, t0:t0 + TT])
            self.dma("s", tabs[3][0:64, :], self.dram["c_xp_ssk"].ap()[0:64, t0:t0 + TT])
            self.dma("s", tabs[3][64:128, :], self.dram["c_xp_ssk"].ap()[0:64, t0:t0 + TT])
            self.tt(AB[0], tabs[0], qsn[:, 0:1].broadcast_to([128, TT]), ALU.mult)
            self.tt(AB[1], tabs[1], qsn[:, 1:2].broadcast_to([128, TT]), ALU.mult)
            self.tt(AB[2], tabs[2], qsn[:, 2:3].broadcast_to([128, TT]), ALU.mult)
            self.tt(AB[3], tabs[3], qsn[:, 3:4].broadcast_to([128, TT]), ALU.mult)
            for j in range(4):
                r0 = t0 + 128 * j
                self.dma("w", xsb[j][:, 0:512], src_x[8 + r0:8 + r0 + 128, 0:512])
                self.dma("w", xsb[j][:, 512:1024], src_x[7 + r0:7 + r0 + 128, 512:1024])
            for c in range(8):
                for j in range(4):
                    self.trans(pT1[:, 128 * j:128 * (j + 1)],
                               xsb[j][:, 128 * c:128 * (c + 1)], self.ident_bf[:, :])
                self.cp(xsfm[c], pT1[:, 0:512])
            for grp, base in (("q", 0), ("k", 8), ("v", 16)):
                for c in range(8):
                    m = base + c
                    p = pQ[c % 2]
                    for k in range(8):
                        self.mm(p, self.wch(wqkv, 3 * D, k, m * 128, (m + 1) * 128),
                                xsfm[k], k == 0, k == 7)
                    if grp == "v":
                        self.cp(vroll[c][:, 256:768], p)
                    else:
                        qr_ = qraw[c % 2]
                        self.cp(qr_, p)
                        self.act(sqb, p, AF.Square)
                        nd = nrm
                        ndb = nrmb
                        self.mm(pN[0:2, 0:512], ind2, sqb, True, True)
                        seg = nd[:, TT * c:TT * (c + 1)]
                        self.act(seg, pN[0:2, 0:512], AF.Ln,
                                 bias=self.e24col[0:2, 0:1])
                        self.act(ndb[:, TT * c:TT * (c + 1)], seg,
                                 AF.Exp, scale=-0.5)
                        self.mm(pBC, ind2T, ndb[:, TT * c:TT * (c + 1)], True, True)
                        self.tt(qn, qr_, pBC, ALU.mult)
                        self.mm(pS, swap, qn, True, True)
                        ta = AB[0] if grp == "q" else AB[2]
                        tb = AB[1] if grp == "q" else AB[3]
                        self.tt(r1, qn, ta, ALU.mult)
                        self.tt(r2, pS, tb, ALU.mult)
                        if grp == "q":
                            self.tt(qrot[c], r1, r2, ALU.add)
                        else:
                            self.tt(kroll[c][:, 256:768], r1, r2, ALU.add)
            for c in range(8):
                for half in range(2):
                    h = 2 * c + half
                    for j in range(4):
                        self.trans(pT2[:, 64 * j:64 * (j + 1)],
                                   vroll[c][64 * half:64 * (half + 1),
                                            256 + 128 * j:256 + 128 * (j + 1)],
                                   self.ident_bf[64 * half:64 * (half + 1),
                                                  64 * half:64 * (half + 1)])
                    self.cp(vtm[h][:, 128:384], pT2[:, 0:256])
            for j in range(4):
                if it == 0 and j == 0:
                    lam = lamf0
                elif it == 0 and j == 1:
                    lam = lamf1
                else:
                    lam = lamg
                for c in range(8):
                    for half in range(2):
                        h = 2 * c + half
                        hb = 64 * half
                        grpS = pS[:, 0:384]
                        self.mm(grpS, self.ident_bf[:, :], lam, True, False)
                        self.mm(grpS, qrot[c][hb:hb + 64, 128 * j:128 * (j + 1)],
                                kroll[c][hb:hb + 64, 128 * j:128 * j + 384],
                                False, True)
                        nmx = cols[:, 0:1]
                        self.red(nmx, grpS, ALU.max, negate=True)
                        self.act(pbf, grpS, AF.Exp, bias=nmx)
                        for b in range(3):
                            self.trans(pT1[:, 128 * b:128 * (b + 1)],
                                       pbf[:, 128 * b:128 * (b + 1)],
                                       self.ident_bf[:, :])
                        self.cp(ptsb, pT1[:, 0:384])
                        for b in range(3):
                            self.mm(pOR[0:1, 128 * half:128 * (half + 1)],
                                    self.ones_col, ptsb[:, 128 * b:128 * (b + 1)],
                                    b == 0, b == 2)
                        for b in range(3):
                            self.mm(pOR[hb:hb + 64, 256:384],
                                    vtm[h][:, 64 * (j + b):64 * (j + b + 1)],
                                    ptsb[:, 128 * b:128 * (b + 1)],
                                    b == 0, b == 2)
                    self.act(rrow, pOR[0:1, 0:256], AF.Ln)
                    self.act(rrb, rrow, AF.Exp, scale=-1.0)
                    self.mm(pBC[0:64, 0:128], ones64, rrb[:, 0:128], True, True)
                    self.mm(pBC[64:128, 0:128], ones64, rrb[:, 128:256], True, True)
                    self.cp(bcs, pBC[:, 0:128])
                    self.tt(afm[c][:, 128 * j:128 * (j + 1)], pOR[:, 256:384], bcs,
                            ALU.mult)
            for m in range(8):
                p = pQ[m % 2]
                for k in range(8):
                    self.mm(p, self.wch(wo, D, k, m * 128, (m + 1) * 128), afm[k],
                            k == 0, k == 7)
                self.cp(wofm[m], p)
            for j in range(4):
                r0 = t0 + 128 * j
                for m in range(8):
                    self.trans(pT2[:, 128 * m:128 * (m + 1)],
                               wofm[m][:, 128 * j:128 * (j + 1)], self.ident_bf[:, :])
                xu = xut[0]
                xo = xot[0]
                self.dma("s", xu, src_x[8 + r0:8 + r0 + 128, :])
                self.tt(xo, pT2[:, 0:1024], xu, ALU.add)
                self.dma("s", dst_x[8 + r0:8 + r0 + 128, :], xo)
        self.end_stage()

    def s_l1ffn_b(self, li):
        T = self.T
        src_x = self.dram["XFB"].ap()
        dst_x = self.dram["XFA"].ap()
        HP = 2816
        W1 = self.dram["L1_ffn_W1"].ap()[li]
        w1a = self.sb("fb_w1a", [128, 8 * HP], BF16, stage=True)
        w1g = self.sb("fb_w1g", [128, 8 * HP], BF16, stage=True)
        for k in range(8):
            self.dma("w", w1a[:, HP * k:HP * (k + 1)],
                     W1[128 * k:128 * (k + 1), 0:HP])
            self.dma("w", w1g[:, HP * k:HP * (k + 1)],
                     W1[128 * k:128 * (k + 1), HP:2 * HP])
        w2 = self.load_w_bf("fb_w2", self.dram["L1_ffn_W2"].ap()[li], HP, D)
        lngc = self.sb("fb_lng", [128, 8], F32, stage=True)
        lnbc = self.sb("fb_lnb", [128, 8], BF16, stage=True)
        for k in range(8):
            self.dma("w", lngc[:, k:k + 1], self.dram["L1_ffn_ln_g"].ap()
                     [li, 128 * k:128 * (k + 1)].rearrange("(a b) -> a b", b=1))
            self.dma("w", lnbc[:, k:k + 1], self.dram["L1_ffn_ln_b"].ap()
                     [li, 128 * k:128 * (k + 1)].rearrange("(a b) -> a b", b=1))
        b1pa = self.sb("fb_b1a", [128, 22], F32, stage=True)
        b1pg = self.sb("fb_b1g", [128, 22], F32, stage=True)
        b2c = self.sb("fb_b2", [128, 8], F32, stage=True)
        for m in range(22):
            self.dma("w", b1pa[:, m:m + 1], self.dram["L1_ffn_b1"].ap()
                     [li, 128 * m:128 * (m + 1)].rearrange("(a b) -> a b", b=1))
            self.dma("w", b1pg[:, m:m + 1], self.dram["L1_ffn_b1"].ap()
                     [li, HP + 128 * m:HP + 128 * (m + 1)].rearrange("(a b) -> a b", b=1))
        for m in range(8):
            self.dma("w", b2c[:, m:m + 1], self.dram["L1_ffn_b2"].ap()
                     [li, 128 * m:128 * (m + 1)].rearrange("(a b) -> a b", b=1))
        pE = self.ps("fb_pe", [128, 512], F32, stage=True)
        for (wx, bx) in ((w1a, b1pa), (w1g, b1pg)):
            for m in range(22):
                for k in range(8):
                    self.mm(pE[:, m:m + 1],
                            wx[:, HP * k + 128 * m:HP * k + 128 * (m + 1)],
                            lnbc[:, k:k + 1], k == 0, k == 7)
            self.tt(bx[:, :], bx[:, :], pE[:, 0:22], ALU.add)
        for k in range(8):
            self.tt(w1a[:, HP * k:HP * (k + 1)], w1a[:, HP * k:HP * (k + 1)],
                    lngc[:, k:k + 1].broadcast_to([128, HP]), ALU.mult)
            self.tt(w1g[:, HP * k:HP * (k + 1)], w1g[:, HP * k:HP * (k + 1)],
                    lngc[:, k:k + 1].broadcast_to([128, HP]), ALU.mult)
        xt = [self.sb("fb_x%d" % i, [128, D], F32, stage=True) for i in range(2)]
        xh = [self.sb("fb_xh%d" % j, [128, D], BF16, stage=True) for j in range(4)]
        xhfm = [self.sb("fb_xf%d" % k, [128, TT], BF16, stage=True) for k in range(8)]
        sqb = self.sb("fb_sq", [128, D], BF16, stage=True)
        gfb = self.sb("fb_gf", [128, TT], BF16, stage=True)
        h2 = [self.sb("fb_h2%d" % mi, [128, TT], BF16, stage=True) for mi in range(22)]
        ofm = [self.sb("fb_of%d" % m, [128, TT], BF16, stage=True) for m in range(8)]
        xut = [self.sb("fb_xu0", [128, D], F32, stage=True)]
        xot = [self.sb("fb_xo0", [128, D], F32, stage=True)]
        cols = self.sb("fb_c", [128, 8], F32, stage=True)
        pQ = [self.ps("fb_pq%d" % i, [128, 512], F32, stage=True) for i in range(2)]
        pT1 = self.ps("fb_pt1", [128, 1024], BF16, stage=True)
        pT2 = self.ps("fb_pt2", [128, 1024], BF16, stage=True)
        for it in range(self.NT):
            t0 = it * TT
            for j in range(4):
                r0 = t0 + 128 * j
                x = xt[j % 2]
                self.dma("s", x[:, 0:512], src_x[8 + r0:8 + r0 + 128, 0:512])
                self.dma("s", x[:, 512:1024], src_x[7 + r0:7 + r0 + 128, 512:1024])
                nm = cols[:, 0:1]
                v = cols[:, 1:2]
                rs = cols[:, 2:3]
                self.red(nm, x[:, :], ALU.add, negate=True)
                self.ts(nm, nm, 1.0 / D, ALU.mult)
                self.act(x[:, :], x[:, :], AF.Identity, bias=nm)
                self.act(sqb, x[:, :], AF.Square)
                self.red(v, sqb, ALU.add)
                self.ts(v, v, 1.0 / D, ALU.mult)
                self.ts(v, v, 1e-5, ALU.add)
                self.act(v, v, AF.Ln)
                self.act(rs, v, AF.Exp, scale=-0.5)
                self.tt(xh[j], x[:, :], rs.broadcast_to([128, D]), ALU.mult)
            for c in range(8):
                for j in range(4):
                    self.trans(pT1[:, 128 * j:128 * (j + 1)],
                               xh[j][:, 128 * c:128 * (c + 1)], self.ident_bf[:, :])
                self.cp(xhfm[c], pT1[:, 0:512])
            for mi in range(22):
                pa = pQ[0]
                pg = pQ[1]
                for k in range(8):
                    self.mm(pa, w1a[:, HP * k + 128 * mi:HP * k + 128 * (mi + 1)],
                            xhfm[k], k == 0, k == 7)
                for k in range(8):
                    self.mm(pg, w1g[:, HP * k + 128 * mi:HP * k + 128 * (mi + 1)],
                            xhfm[k], k == 0, k == 7)
                self.act(gfb, pg, AF.Gelu, bias=b1pg[:, mi:mi + 1])
                self.act(h2[mi], pa, AF.Identity, bias=b1pa[:, mi:mi + 1])
                self.tt(h2[mi], h2[mi], gfb, ALU.mult)
            for m in range(8):
                p = pQ[m % 2]
                for k in range(22):
                    self.mm(p, self.wch(w2, D, k, m * 128, (m + 1) * 128), h2[k],
                            k == 0, k == 21)
                self.act(ofm[m], p, AF.Identity, bias=b2c[:, m:m + 1])
            for j in range(4):
                r0 = t0 + 128 * j
                for m in range(8):
                    self.trans(pT2[:, 128 * m:128 * (m + 1)],
                               ofm[m][:, 128 * j:128 * (j + 1)], self.ident_bf[:, :])
                xu = xut[0]
                xo = xot[0]
                self.dma("s", xu, src_x[8 + r0:8 + r0 + 128, :])
                self.tt(xo, pT2[:, 0:1024], xu, ALU.add)
                self.dma("s", dst_x[8 + r0:8 + r0 + 128, :], xo)
        self.end_stage()

    def s_l1ffn(self):
        for li in range(6):
            self.s_l1ffn_a(li)
            self.s_l1ffn_b(li)

    def s_final(self):
        T = self.T
        X1 = self.dram["X1"].ap()
        S1 = self.dram["S1"].ap()
        XF = self.dram["XFA"].ap()
        OUT = self.dram["out"].ap()
        xt = [self.sb("fn_x%d" % i, [128, D], F32, stage=True) for i in range(2)]
        st = [self.sb("fn_s%d" % i, [128, D], BF16, stage=True) for i in range(2)]
        gt = [self.sb("fn_g%d" % i, [128, D], F32, stage=True) for i in range(2)]
        t1 = self.sb("fn_t", [128, D], F32, stage=True)
        ot = [self.sb("fn_o%d" % i, [128, D], F32, stage=True) for i in range(2)]
        for c in range(self.NC):
            r0 = c * 128
            x = xt[c % 2]
            s = st[c % 2]
            g = gt[c % 2]
            o = ot[c % 2]
            self.dma("s", x, X1[r0:r0 + 128, :])
            self.dma("s", s, S1[r0:r0 + 128, :])
            self.dma("s", g, XF[8 + r0:8 + r0 + 128, :])
            self.act(t1, g, AF.Exp, scale=-1.0)
            self.act(t1, t1, AF.Ln, bias=self.onef[:, 0:1])
            self.act(t1, t1, AF.Exp, scale=-1.0)
            self.tt(g, g, t1, ALU.mult)
            self.tt(t1, s, g, ALU.mult)
            self.tt(o, x, t1, ALU.add)
            self.dma("s", OUT[r0:r0 + 128, :], o)
        self.end_stage()


    def build(self):
        self.declare()
        for name, fn in self.STAGES:
            fn(self)
            if self.stop_after == name:
                break
        self.P.barrier()
        semnames = list(ENGINES) + ["d" + q for q in DMAQ]
        sems = {}
        for n in semnames:
            g = self.nc.semaphore("sem_" + n)
            sems[n] = g.__enter__()
            self._guards.append(g)
        nc = self.nc
        P = self.P
        with nc.Block() as block:
            block.tensor(lambda eo: P.replay(sems, eo, "t"))
            block.vector(lambda eo: P.replay(sems, eo, "v"))
            block.scalar(lambda eo: P.replay(sems, eo, "a"))
            block.gpsimd(lambda eo: P.replay(sems, eo, "p"))
            block.sync(lambda eo: P.replay(sems, eo, "s"))
        nc.finalize()
        return nc



Kern.STAGES = [("ln0", Kern.s_ln0), ("scores", Kern.s_scores), ("proj0", Kern.s_proj0), ("fir", Kern.s_fir), ("delta", Kern.s_delta), ("router", Kern.s_router), ("probs", Kern.s_probs), ("ctxwo", Kern.s_ctx_wo), ("shared", Kern.s_shared), ("moe", Kern.s_moe), ("combine0", Kern.s_combine0), ("l1proj", Kern.s_l1proj), ("gla", Kern.s_gla), ("wout", Kern.s_wout), ("l1f0a", lambda s: s.s_l1ffn_a(0)), ("l1f0b", lambda s: s.s_l1ffn_b(0)),
    ("l1f1a", lambda s: s.s_l1ffn_a(1)), ("l1f1b", lambda s: s.s_l1ffn_b(1)),
    ("l1f2a", lambda s: s.s_l1ffn_a(2)), ("l1f2b", lambda s: s.s_l1ffn_b(2)),
    ("l1f3a", lambda s: s.s_l1ffn_a(3)), ("l1f3b", lambda s: s.s_l1ffn_b(3)),
    ("l1f4a", lambda s: s.s_l1ffn_a(4)), ("l1f4b", lambda s: s.s_l1ffn_b(4)),
    ("l1f5a", lambda s: s.s_l1ffn_a(5)), ("l1f5b", lambda s: s.s_l1ffn_b(5)),
    ("final", Kern.s_final)]


def host_prep(inputs, T=8192):
    inp = {k: np.ascontiguousarray(np.asarray(v, np.float32))
           for k, v in inputs.items()}
    x = inp.pop("L0_attn_hidden_states")

    for nm in ("Wq", "Wk", "Wv", "Wout"):
        inp["L1_attn_" + nm + "T"] = np.ascontiguousarray(inp.pop("L1_attn_" + nm).T)
    wg = np.concatenate([inp.pop("L1_attn_Wbeta").T, inp.pop("L1_attn_Wig").T,
                         inp.pop("L1_attn_Wog").T], axis=1)
    inp["L1_attn_WgT"] = np.ascontiguousarray(wg)
    inp["L1_attn_gb"] = np.concatenate([inp.pop("L1_attn_bbeta"),
                                        inp.pop("L1_attn_big"),
                                        inp.pop("L1_attn_bog")])

    HI, HP = 2730, 2816
    W1 = inp.pop("L1_ffn_W1"); b1 = inp.pop("L1_ffn_b1"); W2 = inp.pop("L1_ffn_W2")
    W1p = np.zeros((6, 1024, 2 * HP), np.float32)
    W1p[:, :, :HI] = W1[:, :, :HI]
    W1p[:, :, HP:HP + HI] = W1[:, :, HI:]
    b1p = np.zeros((6, 2 * HP), np.float32)
    b1p[:, :HI] = b1[:, :HI]
    b1p[:, HP:HP + HI] = b1[:, HI:]
    W2p = np.zeros((6, HP, 1024), np.float32)
    W2p[:, :HI] = W2
    inp["L1_ffn_W1"], inp["L1_ffn_b1"], inp["L1_ffn_W2"] = W1p, b1p, W2p

    c = {}
    I = np.eye(128, dtype=np.float32)
    c["c_ident_bf"] = _bf(I)
    c["c_ident_f32"] = I.copy()
    c["c_ones_lhs_bf"] = _bf(np.ones((1, 128)))
    c["c_ones_col_bf"] = _bf(np.ones((128, 1)))
    c["c_cum_bf"] = _bf(np.triu(np.ones((128, 128))))
    c["c_mask_sl_neg"] = -np.tril(np.ones((128, 128), np.float32), -1)
    c["c_mask_su_neg"] = -np.triu(np.ones((128, 128), np.float32), 1)
    c["c_mask_ui"] = np.triu(np.ones((128, 128), np.float32))
    gm = np.triu(np.ones((128, 128), np.float32))
    c["c_gla_m"] = gm
    c["c_gla_b"] = ((1.0 - gm) * NEG).astype(np.float32)
    qi = np.arange(128)[:, None]
    kj = np.arange(384)[None, :] - 256
    m = (qi < kj) | (qi > kj + 256)
    c["c_lam_gen"] = np.where(m, NEG, 0.0).astype(np.float32)
    c["c_lam_gen_bf"] = _bf(c["c_lam_gen"])
    bj = np.arange(384)[None, :]
    c["c_lam_f0_bf"] = _bf(np.where(m | (bj < 256), NEG, 0.0))
    c["c_lam_f1_bf"] = _bf(np.where(m | (bj < 128), NEG, 0.0))
    ind2 = np.zeros((128, 2), np.float32)
    ind2[:64, 0] = 1.0
    ind2[64:, 1] = 1.0
    c["c_ind2_bf"] = _bf(ind2)
    c["c_ind2T_bf"] = _bf(np.ascontiguousarray(ind2.T))
    sw = np.zeros((128, 128), np.float32)
    for h in range(2):
        b0 = 64 * h
        for r in range(32):
            sw[b0 + 32 + r, b0 + r] = 1.0
            sw[b0 + r, b0 + 32 + r] = 1.0
    c["c_swap64_bf"] = _bf(sw)
    swh = np.zeros((128, 128), np.float32)
    for m in range(128):
        swh[(m + 64) % 128, m] = 1.0
    c["c_swap_h_bf"] = _bf(swh)
    c["c_ones2_bf"] = _bf(np.ones((2, 128)))
    c["c_ones64_bf"] = _bf(np.ones((1, 64)))
    sel = np.zeros((16, 8 * 128), np.float32)
    for cc in range(8):
        sel[2 * cc, cc * 128:cc * 128 + 64] = 1.0
        sel[2 * cc + 1, cc * 128 + 64:cc * 128 + 128] = 1.0
    c["c_sel16"] = _bf(sel)
    half = 64
    invf = 1.0 / 10000.0 ** (np.arange(half, dtype=np.float32) / half)
    fr = np.arange(T, dtype=np.float32)[:, None] * invf[None, :]
    cosr = np.cos(fr).T
    sinr = np.sin(fr).T
    c["c_rope1_cs2"] = np.ascontiguousarray(np.concatenate([cosr, cosr], 0))
    c["c_rope1_ss2"] = np.ascontiguousarray(np.concatenate([-sinr, sinr], 0))
    h2 = np.arange(0, 64, 2, dtype=np.float32)
    invf2 = 1.0 / 10000.0 ** (h2 / 64.0)
    t = np.arange(T, dtype=np.float32)
    fr2 = t[:, None] * invf2[None, :]
    fr2 = np.concatenate([fr2, fr2], -1)
    base = (h2 + 0.4 * 64) / (1.4 * 64)
    power = (t - T // 2) / 128.0
    sc = base[None, :] ** power[:, None]
    sc = np.concatenate([sc, sc], -1)
    sgn = np.ones(64, np.float32)
    sgn[:32] = -1.0
    c["c_xp_cs"] = np.ascontiguousarray((np.cos(fr2) * sc).T)
    c["c_xp_ssq"] = np.ascontiguousarray((np.sin(fr2) * sc).T * sgn[:, None])
    c["c_xp_ssk"] = np.ascontiguousarray((np.sin(fr2) / sc).T * sgn[:, None])
    c["c_xp_ci"] = np.ascontiguousarray((np.cos(fr2) / sc).T)
    for k in ("c_ident_bf", "c_ones_lhs_bf", "c_ones_col_bf", "c_cum_bf",
              "c_ind2_bf", "c_ind2T_bf", "c_swap64_bf"):
        pass

    maps = []
    for core in range(NCORES):
        mcore = {"x0": np.ascontiguousarray(x[core, :T])}
        mcore.update(inp)
        mcore.update(c)
        maps.append(mcore)
    return maps


def kernel(**inputs):
    T = 8192
    maps = host_prep(inputs, T)
    b = Kern(T=T)
    nc = b.build()
    names = {n for n, t in b.dram.items()}
    in_names = [n for n, t in b.dram.items()]
    core_maps = []
    for m in maps:
        core_maps.append({k: v for k, v in m.items() if k in names})
    res = run_bass_kernel_spmd(nc, core_maps, core_ids=list(range(NCORES)))
    out = np.stack([np.asarray(r["out"], np.float32) for r in res.results], axis=0)
    return out
